# version 03
# baseline (speedup 1.0000x reference)
import numpy as np
import concourse.bass as bass
import concourse.mybir as mybir
from concourse.bass_utils import run_bass_kernel_spmd

F32 = mybir.dt.float32
AF = mybir.ActivationFunctionType
ALU = mybir.AluOpType
AX = mybir.AxisListType

C = 1024
T = 2048
H = 16
KC = 8
NS = 4
IC = 11
NE = 10
NEG = -1.0e30
ARW = 27648


class Sched:

    def __init__(self):
        self.items = []
        self.cnt = {"syd": 0, "gpd": 0, "pe": 0, "dve": 0, "act": 0, "gpc": 0}
        self.lastw = {}

    def op(self, eng, fn, waits=(), inc=None, amt=1):
        w = []
        for tok in waits:
            if tok is None:
                continue
            s, v = tok
            if v <= 0:
                continue
            assert v <= self.cnt[s], (s, v, self.cnt[s])
            k = (eng, s)
            if self.lastw.get(k, -1) >= v:
                continue
            self.lastw[k] = v
            w.append((s, v))
        tok = None
        if inc is not None:
            self.cnt[inc] += amt
            tok = (inc, self.cnt[inc])
        self.items.append((eng, fn, tuple(w), inc, amt))
        return tok

    def now(self, sem):
        return (sem, self.cnt[sem])

    def barrier(self):
        toks = [self.now(s) for s in self.cnt]
        for eng in ("sync", "tensor", "vector", "scalar", "gpsimd"):
            self.op(eng, lambda e: None, waits=toks)

    def emit(self, eng_name, eng, sems):
        for e, fn, w, inc, amt in self.items:
            if e != eng_name:
                continue
            for s, v in w:
                eng.wait_ge(sems[s], v)
            ins = fn(eng)
            if inc is not None:
                ins.then_inc(sems[inc], amt)


def build_program(stage=99):
    nc = bass.Bass("TRN2", target_bir_lowering=False)
    S = Sched()
    now = S.now

    dr = {}
    def din(name, shape):
        dr[name] = nc.dram_tensor(name, list(shape), F32, kind="ExternalInput").ap()
        return dr[name]

    x_in = din("x", (C, T))
    wa = din("wa", (16 * 128, 1536))
    wb = din("wb", (16 * 128, 1536))
    wouta = din("wouta", (128, 8192))
    woutb = din("woutb", (128, 8192))
    bqkv_a = din("bqkv_a", (64, 48))
    bqkv_b = din("bqkv_b", (64, 48))
    bout_a = din("bout_a", (128, 8))
    bout_b = din("bout_b", (128, 8))
    wgate = din("wgate", (128, 64))
    wseg = din("wseg", (128, 8))
    wsur = din("wsur", (128, 8))
    wgu = din("wgu", (NE * IC * 128, 2048))
    wdn = din("wdn", (NE * 8 * 128, 1408))
    scal = din("scal", (1, 4))
    identD = din("ident", (128, 128))
    onesrD = din("onesr", (1, 128))
    onescD = din("onesc", (128, 1))
    wminkD = din("wmink", (64, 4))
    selTD = din("selT", (2, 128))
    selSD = din("selS", (2, 128))
    sel8D = din("sel8", (8, 1024))
    maskAD = din("maskA", (128, 2048))
    maskBD = din("maskB", (128, 128))
    epscD = din("epsc", (1, 1))
    negcD = din("negc", (128, 1))

    x_hbm = nc.dram_tensor("x_hbm", [C, T], F32).ap()
    y_hbm = nc.dram_tensor("y_hbm", [C, T], F32).ap()
    out = nc.dram_tensor("out", [C, T], F32, kind="ExternalOutput").ap()

    xn = nc.alloc_sbuf_tensor("xn", [128, KC * T], F32).ap()
    AR = nc.alloc_sbuf_tensor("arena", [128, ARW], F32).ap()
    CO = {}
    for name, shape in [("ident", (128, 128)), ("onesr", (1, 128)),
                        ("onesc", (128, 1)),
                        ("bqkv_a", (64, 48)), ("bqkv_b", (64, 48)),
                        ("bout_a", (128, 8)), ("bout_b", (128, 8)),
                        ("wgate", (128, 64)), ("wseg", (128, 8)),
                        ("wsur", (128, 8)), ("scal", (1, 4)),
                        ("rows", (1, 8)),
                        ("wfm", (8, T)), ("swrow", (1, T)), ("spike", (1, T)),
                        ("scalecol", (128, 1)), ("thr", (1, 1)),
                        ("sel8", (8, 1024)), ("epsc", (1, 1)),
                        ("negc", (128, 1)), ("ltm", (128, 128)),
                        ("wtm", (128, 128)), ("ebf", (128, 128)),
                        ("s2f", (128, 128)), ("mkf", (128, 128)),
                        ("b2f", (128, 128)), ("mx16", (128, 16)),
                        ("rc16", (128, 16)), ("mxs", (128, 16)),
                        ("rcs", (128, 16))]:
        CO[name] = nc.alloc_sbuf_tensor("c_" + name, [shape[0], shape[1]], F32).ap()
    PS = [nc.alloc_psum_tensor("ps%d" % i, [128, 512], F32).ap() for i in range(8)]
    BF = {i: None for i in range(8)}
    ident = CO["ident"]

    class Arena:
        def __init__(self):
            self.off = 0
        def take(self, parts, width):
            ap = AR[0:parts, self.off:self.off + width]
            self.off += width
            assert self.off <= ARW, self.off
            return ap

    def dma(q, dst, src, waits=(), accum=False):
        sem = "syd" if q == "sync" else "gpd"
        if accum:
            fn = lambda e, d=dst, s=src: e.dma_start(out=d, in_=s, accum_op=ALU.add)
        else:
            fn = lambda e, d=dst, s=src: e.dma_start(out=d, in_=s)
        return S.op(q, fn, waits, inc=sem, amt=16)

    def mm1(dst, lhsT, rhs, start, stop, waits=(), inc=False):
        fn = lambda e, d=dst, l=lhsT, r=rhs, st=start, sp=stop: \
            e.matmul(d, l, r, start=st, stop=sp)
        return S.op("tensor", fn, waits, inc="pe" if inc else None)

    def mmc(bank, pairs, waits=(), rows=128, n=512, col0=0):
        ap = PS[bank][0:rows, col0:col0 + n]
        t = None
        for i, (l, r) in enumerate(pairs):
            w = (list(waits) + [BF[bank]]) if i == 0 else []
            t = mm1(ap, l, r, i == 0, i == len(pairs) - 1, waits=w,
                    inc=(i == len(pairs) - 1))
        return t

    def trc(bank, lst, waits=()):
        t = None
        for i, (d, s_, idn) in enumerate(lst):
            w = (list(waits) + [BF[bank]]) if i == 0 else []
            fn = lambda e, dd=d, ss=s_, ii=idn, st=(i == 0), sp=(i == len(lst) - 1): \
                e.matmul(dd, ss, ii, start=st, stop=sp, is_transpose=True)
            t = S.op("tensor", fn, waits=w, inc="pe" if i == len(lst) - 1 else None)
        return t

    def rd(bank, tok):
        BF[bank] = tok
        return tok

    def dve(meth, *args, waits=(), **kw):
        fn = lambda e, a=args, k=kw, m=meth: getattr(e, m)(*a, **k)
        return S.op("vector", fn, waits, inc="dve")

    def act(out_ap, in_ap, func, waits=(), **kw):
        fn = lambda e, o=out_ap, i=in_ap, f=func, k=kw: e.activation(o, i, f, **k)
        return S.op("scalar", fn, waits, inc="act")

    def fence_dve():
        return S.op("vector", lambda e: e.drain(fusable=False), inc="dve")

    def prologue():
        toks = []
        for name, d in [("ident", identD), ("onesr", onesrD), ("onesc", onescD),
                        ("bqkv_a", bqkv_a), ("bqkv_b", bqkv_b),
                        ("bout_a", bout_a), ("bout_b", bout_b),
                        ("wgate", wgate), ("wseg", wseg), ("wsur", wsur),
                        ("scal", scal), ("sel8", sel8D),
                        ("epsc", epscD), ("negc", negcD)]:
            toks.append(dma("sync", CO[name], d))
        LD = now("syd")
        th = CO["scal"][0:1, 1:2]
        r = CO["rows"]
        act(r[:, 0:1], th, AF.Ln, waits=[LD])
        dve("tensor_scalar", r[:, 2:3], th, -1.0, 1.0, ALU.mult, ALU.add, waits=[LD])
        act(r[:, 4:5], r[:, 2:3], AF.Ln, waits=[now("dve")])
        dve("tensor_sub", CO["thr"], r[:, 0:1], r[:, 4:5], waits=[now("act")])
        t = mmc(7, [(CO["onesr"], CO["scal"][0:1, 2:3])], waits=[LD], n=1)
        rd(7, dve("tensor_copy", CO["scalecol"], PS[7][0:128, 0:1], waits=[t]))
        return LD

    def layer_norm(src_dram, ar, routing=False, spiking=False, dbg=None):
        S.barrier()
        xch = [ar.take(128, T), ar.take(128, T)]
        xsq = [ar.take(128, T), ar.take(128, T)]
        rstd_bc = ar.take(128, T)
        mr_bc = ar.take(128, T)
        rowA = ar.take(1, T)
        rowB = ar.take(1, T)
        rowC = ar.take(1, T)
        rowD = ar.take(1, T)
        rowE = ar.take(1, T)

        ch_tok = [None, None]
        sq_tok = [None, None]
        for kc in range(KC):
            b = kc % 2
            ld = dma("sync", xch[b], src_dram[kc * 128:(kc + 1) * 128, :],
                     waits=[ch_tok[b]])
            sq_tok[b] = dve("scalar_tensor_tensor", xsq[b], xch[b], 1.0, xch[b],
                            ALU.mult, ALU.mult, waits=[ld, sq_tok[b]])
            for n in range(NS):
                mm1(PS[n][0:1, :], CO["onesc"], xch[b][:, n * 512:(n + 1) * 512],
                    kc == 0, kc == KC - 1,
                    waits=([ld, BF[n]] if kc == 0 else [ld]), inc=True)
                mm1(PS[4 + n][0:1, :], CO["onesc"], xsq[b][:, n * 512:(n + 1) * 512],
                    kc == 0, kc == KC - 1,
                    waits=([sq_tok[b], BF[4 + n]] if kc == 0 else [sq_tok[b]]),
                    inc=True)
            ch_tok[b] = now("pe")
        STATS = now("pe")
        for n in range(NS):
            rd(n, dve("tensor_copy", rowA[:, n * 512:(n + 1) * 512], PS[n][0:1, :],
                      waits=[STATS]))
            rd(4 + n, dve("tensor_copy", rowB[:, n * 512:(n + 1) * 512],
                          PS[4 + n][0:1, :], waits=[STATS]))
        dve("tensor_scalar_mul", rowC, rowA, 1.0 / C)
        dve("scalar_tensor_tensor", rowA, rowC, 1.0, rowC, ALU.mult, ALU.mult)
        dve("scalar_tensor_tensor", rowB, rowB, 1.0 / C, rowA, ALU.mult, ALU.subtract)
        act(rowD, rowB, AF.Ln, bias=CO["epsc"], waits=[now("dve")])
        act(rowD, rowD, AF.Exp, scale=-0.5)
        RSTD = now("act")
        dve("tensor_mul", rowE, rowC, rowD, waits=[RSTD])
        ROWS = now("dve")
        for n in range(NS):
            sl = slice(n * 512, (n + 1) * 512)
            t = mmc(0, [(CO["onesr"], rowD[:, sl])], waits=[ROWS])
            rd(0, dve("tensor_copy", rstd_bc[:, sl], PS[0], waits=[t]))
            t = mmc(1, [(CO["onesr"], rowE[:, sl])], waits=[ROWS])
            rd(1, dve("tensor_copy", mr_bc[:, sl], PS[1], waits=[t]))
        BC = now("dve")
        for kc in range(KC):
            b = kc % 2
            ld = dma("sync", xch[b], src_dram[kc * 128:(kc + 1) * 128, :],
                     waits=[ch_tok[b]])
            dve("scalar_tensor_tensor", xsq[b], xch[b], 1.0, rstd_bc,
                ALU.mult, ALU.mult, waits=[ld, BC, sq_tok[b]])
            sq_tok[b] = None
            ch_tok[b] = dve("tensor_sub", xn[:, kc * T:(kc + 1) * T], xsq[b], mr_bc)
        XN = now("dve")

        if spiking:
            for n in range(NS):
                t = mmc(2, [(CO["wsur"][:, kc:kc + 1],
                             xn[:, kc * T + n * 512: kc * T + (n + 1) * 512])
                            for kc in range(KC)], waits=[XN], rows=1)
                dve("tensor_scalar_add", rowA[:, n * 512:(n + 1) * 512],
                    PS[2][0:1, :], CO["scal"][0:1, 0:1], waits=[t])
                rd(2, dve("tensor_single_scalar",
                          CO["spike"][:, n * 512:(n + 1) * 512],
                          rowA[:, n * 512:(n + 1) * 512], CO["thr"], ALU.is_gt))

        RT = None
        if routing:
            lgfm = ar.take(8, T)
            ltm, wtm, ebf, s2f = CO["ltm"], CO["wtm"], CO["ebf"], CO["s2f"]
            mkf, b2f, mx16, rc16 = CO["mkf"], CO["b2f"], CO["mx16"], CO["rc16"]
            for n in range(NS):
                sl2 = slice(n * 512, (n + 1) * 512)
                t = mmc(2, [(CO["wgate"][:, kc * 8:(kc + 1) * 8],
                             xn[:, kc * T + n * 512: kc * T + (n + 1) * 512])
                            for kc in range(KC)], waits=[XN], rows=8)
                rd(2, dve("tensor_copy", lgfm[:, sl2], PS[2][0:8, :], waits=[t]))
                t = mmc(3, [(CO["wseg"][:, kc:kc + 1],
                             xn[:, kc * T + n * 512: kc * T + (n + 1) * 512])
                            for kc in range(KC)], waits=[XN], rows=1)
                rd(3, act(CO["swrow"][:, sl2], PS[3][0:1, :], AF.Sigmoid, waits=[t]))
            LG = now("dve")
            for tc in range(16):
                t = trc(4, [(PS[4][:, 0:8], lgfm[:, tc * 128:(tc + 1) * 128],
                             ident[0:8, 0:8])], waits=[LG])
                rd(4, dve("tensor_copy", ltm[:, tc * 8:(tc + 1) * 8], PS[4][:, 0:8],
                          waits=[t]))
            LTM = fence_dve()
            seg = lambda ap, e: ap.rearrange("p (s e) -> p s e", e=e)
            b1 = lambda ap: seg(ap, 1).broadcast_to((128, 16, 8))
            a = act(ebf, ltm, AF.Exp, waits=[LTM])
            dve("tensor_reduce", seg(mx16, 1), seg(ebf, 8), AX.X, ALU.add,
                waits=[a])
            aL = act(CO["rcs"], mx16, AF.Ln, waits=[now("dve")])
            aE = act(rc16, CO["rcs"], AF.Exp, scale=-1.0)
            dve("tensor_tensor", seg(s2f, 8), seg(ebf, 8), b1(rc16), ALU.mult,
                waits=[aE])
            r2 = dve("tensor_reduce", seg(mx16, 1), seg(s2f, 8), AX.X, ALU.max)
            dve("tensor_tensor", seg(mkf, 8), seg(s2f, 8), b1(mx16),
                ALU.subtract, waits=[r2])
            dve("tensor_single_scalar", mkf, mkf, 0.0, ALU.is_ge)
            dve("scalar_tensor_tensor", b2f, mkf, NEG, s2f, ALU.mult, ALU.add)
            r3 = dve("tensor_reduce", seg(mx16, 1), seg(b2f, 8), AX.X, ALU.max)
            dve("tensor_tensor", seg(mkf, 8), seg(s2f, 8), b1(mx16),
                ALU.subtract, waits=[r3])
            dve("tensor_single_scalar", mkf, mkf, 0.0, ALU.is_ge)
            dve("tensor_mul", wtm, mkf, s2f)
            W = fence_dve()
            for tc in range(16):
                t = trc(5, [(PS[5][0:8, 0:128], wtm[:, tc * 8:(tc + 1) * 8],
                             ident)], waits=[W])
                rd(5, dve("tensor_copy", CO["wfm"][:, tc * 128:(tc + 1) * 128],
                          PS[5][0:8, 0:128], waits=[t]))
            RT = (now("dve"), now("act"))
            if dbg is not None:
                dbg.append(("lgfm", lgfm, [now("dve")]))
                dbg.append(("ltm", ltm, [now("dve")]))
                dbg.append(("wtm", wtm, [now("dve")]))
                dbg.append(("ebf", ebf, [now("dve")]))
                dbg.append(("s2f", s2f, [now("dve")]))
                dbg.append(("mkf", mkf, [now("dve")]))
                dbg.append(("b2f", b2f, [now("dve")]))
                dbg.append(("mx16", mx16, [now("dve")]))
                dbg.append(("rc16", rc16, [now("dve")]))
                dbg.append(("mxs", CO["mxs"], [now("dve")]))
                dbg.append(("rcs", CO["rcs"], [now("dve")]))
        return XN, RT

    def attn_a(ar, XNt):
        S.barrier()
        qh = ar.take(64, T)
        kh = ar.take(64, T)
        vh = ar.take(64, T)
        Qp = ar.take(64, T)
        Kp = ar.take(64, T)
        time_sb = ar.take(2, T)
        srow_sb = ar.take(2, T)
        vaug = ar.take(128, 16 * 65)
        wsl = [ar.take(128, 1536), ar.take(128, 1536)]
        CH = [{n: ar.take(128, 512) for n in ("B1", "B2", "B3", "pt")}
              for _ in range(2)]
        for cs in range(2):
            CH[cs]["ptm"] = CH[cs]["B3"]
        ysb = [ar.take(64, 512), ar.take(64, 512)]
        csr = ar.take(1, 1536)
        maskA = ar.take(128, 2048)
        selT = ar.take(2, 128)
        selS = ar.take(2, 128)
        wmink = ar.take(64, 4)
        erow = [CH[0]["B1"][0:2, 0:512], CH[0]["B2"][0:2, 0:512],
                CH[0]["B3"][0:2, 0:512], CH[1]["B1"][0:2, 0:512],
                CH[1]["B2"][0:2, 0:512]]

        CLD = now("syd")
        for d, s_ in [(maskA, maskAD), (selT, selTD), (selS, selSD),
                      (wmink, wminkD)]:
            dma("sync", d, s_)
        CLD = now("syd")
        VA = dve("memset", vaug, 1.0)

        wtok = [None, None]
        ytok = [None, None]
        set_tok = [None, None]
        pv_use = [None, None]
        yev = None
        for h in range(H):
            b = h % 2
            WL = dma("sync", wsl[b], wa[h * 128:(h + 1) * 128, :], waits=[wtok[b]])
            dsts = {0: qh, 1: kh, 2: vh}
            for part in range(3):
                for n in range(NS):
                    bank = part % 2
                    t = mmc(bank,
                            [(wsl[b][:, kc * 192 + part * 64: kc * 192 + (part + 1) * 64],
                              xn[:, kc * T + n * 512: kc * T + (n + 1) * 512])
                             for kc in range(KC)],
                            waits=[WL, XNt], rows=64)
                    rd(bank, dve("tensor_scalar_add",
                                 dsts[part][:, n * 512:(n + 1) * 512], PS[bank][0:64, :],
                                 CO["bqkv_a"][:, part * 16 + h: part * 16 + h + 1],
                                 waits=[t]))
            PROJ = now("dve")
            wtok[b] = PROJ
            dve("scalar_tensor_tensor", Qp, qh, 1.0, qh, ALU.mult, ALU.mult,
                waits=[set_tok[0], set_tok[1]])
            dve("scalar_tensor_tensor", Kp, kh, 1.0, kh, ALU.mult, ALU.mult)
            USQ = now("dve")
            dve("memset", qh[0:1, :], 1.0)
            dve("memset", kh[0:1, :], 1.0)
            for n in range(NS):
                sl = slice(n * 512, (n + 1) * 512)
                t = mmc(2, [(wmink[:, 0:2], Qp[:, sl]), (wmink[:, 2:4], Kp[:, sl])],
                        waits=[USQ, CLD], rows=2)
                eA, eB, eC, eD, eE = erow
                d1 = dve("tensor_scalar_max", eA, PS[2][0:2, :], 1e-8, waits=[t])
                rd(2, d1)
                act(eB, eA, AF.Ln, waits=[d1])
                act(eC, eB, AF.Exp, scale=0.5)
                act(eD, eC, AF.Exp)
                act(eE, eC, AF.Exp, scale=-1.0)
                a5 = act(eC, eB, AF.Exp, scale=-0.5)
                dve("tensor_add", eA, eD, eE, waits=[a5])
                dve("tensor_scalar_mul", time_sb[0:2, sl], eA, 0.5)
                dve("tensor_sub", eA, eD, eE)
                dve("scalar_tensor_tensor", srow_sb[0:2, sl], eA, 0.5, eC,
                    ALU.mult, ALU.mult)
            EXPR = now("dve")
            for n in range(NS):
                sl = slice(n * 512, (n + 1) * 512)
                t = mmc(3, [(selT[:, 0:64], time_sb[0:2, sl]),
                            (selS[:, 0:64], srow_sb[0:2, sl])],
                        waits=[EXPR, CLD], rows=64)
                rd(3, dve("tensor_mul", Qp[:, sl], PS[3][0:64, :], qh[:, sl],
                          waits=[t]))
                t = mmc(4, [(selT[:, 64:128], time_sb[0:2, sl]),
                            (selS[:, 64:128], srow_sb[0:2, sl])],
                        waits=[EXPR], rows=64)
                rd(4, dve("tensor_mul", Kp[:, sl], PS[4][0:64, :], kh[:, sl],
                          waits=[t]))
            QPKP = now("dve")
            for tc in range(16):
                t = trc(5, [(PS[5][:, 0:64], vh[:, tc * 128:(tc + 1) * 128],
                             ident[0:64, 0:64])], waits=[PROJ, VA])
                rd(5, dve("tensor_copy", vaug[:, tc * 65: tc * 65 + 64],
                          PS[5][:, 0:64], waits=[t]))
            VAUG = now("dve")
            for s in range(NS):
                qo = s * 512
                nkc = 4 * (s + 1)
                pv_tok = None
                for kc in range(nkc):
                    cs = kc % 2
                    bank = 5 + cs
                    B1, B2, B3 = CH[cs]["B1"], CH[cs]["B2"], CH[cs]["B3"]
                    pt, ptm = CH[cs]["pt"], CH[cs]["ptm"]
                    t = mmc(bank, [(Kp[:, kc * 128:(kc + 1) * 128], Qp[:, qo:qo + 512])],
                            waits=[QPKP, VAUG])
                    d1 = dve("tensor_scalar_max", B1, PS[bank], 1.0000001,
                             waits=[t, set_tok[cs]])
                    rd(bank, d1)
                    d2 = dve("scalar_tensor_tensor", B2, B1, 1.0, B1, ALU.mult, ALU.mult)
                    a1 = act(B3, B2, AF.Ln, bias=CO["negc"], waits=[d2])
                    a2 = act(B2, B3, AF.Exp, scale=0.5)
                    d3 = dve("tensor_add", B3, B1, B2, waits=[a2])
                    a3 = act(B1, B3, AF.Ln, waits=[d3])
                    d4 = dve("scalar_tensor_tensor", B2, B1, 1.0, B1,
                             ALU.mult, ALU.mult, waits=[a3])
                    a4 = act(pt, B2, AF.Exp, scale=-0.125,
                             waits=[d4, pv_use[cs]])
                    set_tok[cs] = a4
                    fin = a4
                    src = pt
                    if kc >= 4 * s:
                        v = kc - 4 * s
                        fin = dve("tensor_mul", ptm, pt,
                                  maskA[:, v * 512:(v + 1) * 512],
                                  waits=[a4, CLD, pv_use[cs]])
                        src = ptm
                    w = [fin]
                    if kc == 0:
                        w.append(BF[7])
                    pv_tok = mm1(PS[7][0:65, :], vaug[:, kc * 65:(kc + 1) * 65], src,
                                 kc == 0, kc == nkc - 1, waits=w,
                                 inc=(kc == nkc - 1))
                    pv_use[cs] = pv_tok if kc == nkc - 1 else None
                    if kc == nkc - 1:
                        for j in range(2):
                            pv_use[j] = pv_tok
                yb = ysb[s % 2]
                rr = csr[:, 0:512]
                cz = csr[:, 512:1024]
                t1r = csr[:, 1024:1536]
                aL = act(t1r, PS[7][64:65, :], AF.Ln, waits=[pv_tok])
                aE = act(rr, t1r, AF.Exp, scale=-1.0)
                CS = dve("tensor_mul", cz, rr, CO["spike"][:, qo:qo + 512],
                         waits=[aE])
                t = mmc(3, [(CO["onesr"][:, 0:64], cz)], waits=[CS], rows=64)
                cp = dve("tensor_copy", yb, PS[7][0:64, :], waits=[t, ytok[s % 2]])
                rd(7, cp)
                yev = dve("tensor_mul", yb, yb, PS[3][0:64, :])
                rd(3, yev)
                ytok[s % 2] = dma("gpsimd", y_hbm[h * 64:(h + 1) * 64, qo:qo + 512],
                                  yb, waits=[yev])
        return now("gpd")

    def attn_b(ar, XNt, dbg=None):
        S.barrier()
        qh = ar.take(64, T)
        kh = ar.take(64, T)
        vh = ar.take(64, T)
        vtm = ar.take(128, 1024)
        wsl = [ar.take(128, 1536), ar.take(128, 1536)]
        Srow = ar.take(128, T)
        pnA = [ar.take(128, T) for _ in range(4)]
        ptb = [ar.take(128, 512), ar.take(128, 512)]
        ysb = [ar.take(64, 512), ar.take(64, 512)]
        mcol = ar.take(128, 8)
        maskB = ar.take(128, 128)
        MB = dma("sync", maskB, maskBD)

        wtok = [None, None]
        ytok = [None, None]
        pt_free = [None, None]
        pn_free = [None] * 4
        sr_free = None
        mcol_free = None
        yev = None
        for h in range(H):
            b = h % 2
            WL = dma("sync", wsl[b], wb[h * 128:(h + 1) * 128, :], waits=[wtok[b]])
            dsts = {0: qh, 1: kh, 2: vh}
            for part in range(3):
                for n in range(NS):
                    bank = part % 2
                    t = mmc(bank,
                            [(wsl[b][:, kc * 192 + part * 64: kc * 192 + (part + 1) * 64],
                              xn[:, kc * T + n * 512: kc * T + (n + 1) * 512])
                             for kc in range(KC)],
                            waits=[WL, XNt], rows=64)
                    rd(bank, dve("tensor_scalar_add",
                                 dsts[part][:, n * 512:(n + 1) * 512], PS[bank][0:64, :],
                                 CO["bqkv_b"][:, part * 16 + h: part * 16 + h + 1],
                                 waits=[t]))
            PROJ = now("dve")
            wtok[b] = PROJ
            for tc in range(16):
                t = trc(5, [(PS[5][:, 0:64], vh[:, tc * 128:(tc + 1) * 128],
                             ident[0:64, 0:64])], waits=[PROJ])
                rd(5, dve("tensor_copy", vtm[:, tc * 64:(tc + 1) * 64],
                          PS[5][:, 0:64], waits=[t]))
            VTM = now("dve")
            for g in range(4):
                for qt in range(4):
                    Q = g * 4 + qt
                    qo = Q * 128
                    L = T - qo
                    lead = qo - g * 512
                    pnq = pnA[qt]
                    t = mmc(6, [(qh[:, qo:qo + 128], kh[:, qo:qo + 128])],
                            waits=[PROJ], n=128)
                    rd(6, dve("scalar_tensor_tensor", Srow[:, 0:128], PS[6][:, 0:128],
                              1.0, maskB, ALU.mult, ALU.add,
                              waits=[t, MB, sr_free]))
                    off = 128
                    pidx = 1
                    while qo + off < T:
                        ln_ = min(512, T - qo - off)
                        bank = 6 if pidx % 2 == 0 else 5
                        t = mmc(bank, [(qh[:, qo:qo + 128],
                                        kh[:, qo + off:qo + off + ln_])],
                                waits=[PROJ], n=ln_)
                        rd(bank, dve("tensor_copy", Srow[:, off:off + ln_],
                                     PS[bank][:, 0:ln_], waits=[t]))
                        off += ln_
                        pidx += 1
                    SR = now("dve")
                    rmx = dve("reduce_max", mcol[:, 0:1], Srow[:, 0:L], AX.X,
                              waits=[SR, mcol_free])
                    MX = dve("tensor_sub", Srow[:, 0:L], Srow[:, 0:L],
                             mcol[:, 0:1].broadcast_to((128, L)), waits=[rmx])
                    pw = [MX, pn_free[qt]]
                    if lead > 0:
                        pw = [dve("memset", pnq[:, 0:lead], 0.0, waits=pw)]
                    sr_free = act(pnq[:, lead:lead + L], Srow[:, 0:L], AF.Exp,
                                  scale=CO["scalecol"], waits=pw)
                    mcol_free = sr_free
                    if dbg is not None and h == 15 and g == 2 and qt == 1:
                        sn3 = ar.take(128, T)
                        sn4 = ar.take(128, T)
                        dve("tensor_copy", sn3, Srow, waits=[sr_free])
                        dve("tensor_copy", sn4, pnq)
                        dbg.append(("srow1", sn3, [now("dve")]))
                        dbg.append(("pnraw1", sn4, [now("dve")]))
                    dve("tensor_reduce", mcol[:, 4:5], pnq[:, lead:lead + L],
                        AX.X, ALU.add, waits=[sr_free])
                    aL = act(mcol[:, 2:3], mcol[:, 4:5], AF.Ln,
                             waits=[now("dve")])
                    aE = act(mcol[:, 6:7], mcol[:, 2:3], AF.Exp, scale=-1.0)
                    dve("tensor_mul", pnq[:, lead:lead + L],
                        pnq[:, lead:lead + L],
                        mcol[:, 6:7].broadcast_to((128, L)), waits=[aE])
                    if dbg is not None and h == 15 and g == 2 and qt == 1:
                        sn2 = ar.take(128, 8)
                        dve("tensor_copy", sn2, mcol)
                        dbg.append(("mc1", sn2, [now("dve")]))
                PN = now("dve")
                Lg = T - g * 512
                nkc = Lg // 128
                pv_tok = None
                for kc in range(nkc):
                    cs = kc % 2
                    bank = 3 + cs
                    t = trc(bank, [(PS[bank][:, qt * 128:(qt + 1) * 128],
                                    pnA[qt][:, kc * 128:(kc + 1) * 128], ident)
                                   for qt in range(4)],
                            waits=[PN, pt_free[cs]])
                    ptile = ptb[cs]
                    ptr = dve("tensor_copy", ptile, PS[bank], waits=[t, pt_free[cs]])
                    rd(bank, ptr)
                    pt_free[cs] = ptr
                    w = [ptr, VTM]
                    if kc == 0:
                        w.append(BF[7])
                    pv_tok = mm1(PS[7][0:64, :],
                                 vtm[:, (g * 4 + kc) * 64:(g * 4 + kc + 1) * 64],
                                 ptile, kc == 0, kc == nkc - 1, waits=w,
                                 inc=(kc == nkc - 1))
                    pt_free[cs] = pv_tok if kc >= nkc - 2 else pt_free[cs]
                for qt in range(4):
                    pn_free[qt] = pv_tok
                yb = ysb[g % 2]
                yev = dve("tensor_copy", yb, PS[7][0:64, :],
                          waits=[pv_tok, ytok[g % 2]])
                rd(7, yev)
                ytok[g % 2] = dma("gpsimd",
                                  y_hbm[h * 64:(h + 1) * 64, g * 512:(g + 1) * 512],
                                  yb, waits=[yev])
        return now("gpd")

    def out_proj(ar, wdram, bias_co, src_x, dst, y_done):
        S.barrier()
        wsb = ar.take(128, 8192)
        yc = ar.take(128, 4096)
        xp = [ar.take(128, 512), ar.take(128, 512)]
        ob = [ar.take(128, 512), ar.take(128, 512)]
        WL = dma("sync", wsb, wdram)
        xp_tok = [None, None]
        ob_tok = [None, None]
        yc_free = None
        for n in range(NS):
            yld = []
            for kc in range(KC):
                yld.append(dma("sync", yc[:, kc * 512:(kc + 1) * 512],
                               y_hbm[kc * 128:(kc + 1) * 128, n * 512:(n + 1) * 512],
                               waits=[y_done, yc_free]))
            for m in range(KC):
                bank = m % 2
                t = mmc(bank, [(wsb[:, kc * 1024 + m * 128: kc * 1024 + (m + 1) * 128],
                                yc[:, kc * 512:(kc + 1) * 512]) for kc in range(KC)],
                        waits=[WL] + yld)
                if m == KC - 1:
                    yc_free = t
                XL = dma("sync", xp[bank],
                         src_x[m * 128:(m + 1) * 128, n * 512:(n + 1) * 512],
                         waits=[xp_tok[bank]])
                d1 = dve("scalar_tensor_tensor", ob[bank], PS[bank], 1.0, xp[bank],
                         ALU.mult, ALU.add, waits=[t, XL, ob_tok[bank]])
                rd(bank, d1)
                d2 = dve("tensor_scalar_add", ob[bank], ob[bank], bias_co[:, m:m + 1])
                xp_tok[bank] = d2
                ob_tok[bank] = dma("gpsimd",
                                   dst[m * 128:(m + 1) * 128, n * 512:(n + 1) * 512],
                                   ob[bank], waits=[d2])
        return now("gpd")

    def ffn(ar, XNt, RT, dst_x, seed_tok):
        S.barrier()
        hw = ar.take(128, IC * 1024)
        gsl = [ar.take(128, 2048), ar.take(128, 2048)]
        dsl = [ar.take(128, 1408), ar.take(128, 1408)]
        wbc = ar.take(128, 1024)
        c1 = ar.take(128, 512)
        c2 = ar.take(128, 512)
        c3 = [ar.take(128, 512), ar.take(128, 512)]
        rt_dve, rt_act = RT
        gtok = [None, None]
        dtok = [None, None]
        ctok = [None, None]
        hw_rd = None
        wbc_rd = None
        prev_d = None
        for half in range(2):
            for e in range(NE):
                for n2 in range(2):
                    nb = half * 1024 + n2 * 512
                    if e < 8:
                        t = mmc(4, [(CO["sel8"][:, e * 128:(e + 1) * 128],
                                     CO["wfm"][:, nb:nb + 512])],
                                waits=[rt_dve])
                    else:
                        t = mmc(4, [(CO["onesr"], CO["swrow"][:, nb:nb + 512])],
                                waits=[rt_act])
                    rd(4, dve("tensor_copy", wbc[:, n2 * 512:(n2 + 1) * 512],
                              PS[4], waits=[t, wbc_rd]))
                WBC = now("dve")
                for i in range(IC):
                    b = i % 2
                    GL = dma("sync", gsl[b],
                             wgu[(e * IC + i) * 128:(e * IC + i + 1) * 128, :],
                             waits=[gtok[b]])
                    for n2 in range(2):
                        nb = half * 1024 + n2 * 512
                        TG = mmc(0, [(gsl[b][:, kc * 128:(kc + 1) * 128],
                                      xn[:, kc * T + nb: kc * T + nb + 512])
                                     for kc in range(KC)], waits=[GL, XNt])
                        TU = mmc(1, [(gsl[b][:, 1024 + kc * 128:1024 + (kc + 1) * 128],
                                      xn[:, kc * T + nb: kc * T + nb + 512])
                                     for kc in range(KC)], waits=[GL])
                        a = act(c1, PS[0], AF.Silu, waits=[TG, prev_d])
                        rd(0, a)
                        d = dve("tensor_mul", c2, c1, PS[1], waits=[a, TU])
                        rd(1, d)
                        prev_d = d
                        gtok[b] = dve("tensor_mul",
                                      hw[:, i * 1024 + n2 * 512: i * 1024 + (n2 + 1) * 512],
                                      c2, wbc[:, n2 * 512:(n2 + 1) * 512],
                                      waits=[WBC, hw_rd if (i == 0 and n2 == 0) else None])
                HW = now("dve")
                wbc_rd = HW
                for m in range(KC):
                    b = m % 2
                    DL = dma("sync", dsl[b],
                             wdn[(e * 8 + m) * 128:(e * 8 + m + 1) * 128, :],
                             waits=[dtok[b]])
                    for n2 in range(2):
                        nb = half * 1024 + n2 * 512
                        bank = 2 + n2
                        t = mmc(bank, [(dsl[b][:, i * 128:(i + 1) * 128],
                                        hw[:, i * 1024 + n2 * 512: i * 1024 + (n2 + 1) * 512])
                                       for i in range(IC)], waits=[DL, HW])
                        dtok[b] = t
                        if m == KC - 1 and n2 == 1:
                            hw_rd = t
                        cb = c3[n2]
                        cpy = dve("tensor_copy", cb, PS[bank], waits=[t, ctok[n2]])
                        rd(bank, cpy)
                        ctok[n2] = dma("gpsimd",
                                       dst_x[m * 128:(m + 1) * 128, nb:nb + 512],
                                       cb, waits=[cpy, seed_tok], accum=True)
        return now("gpd")

    def dump(name, ap, waits=()):
        d = nc.dram_tensor("dbg_" + name, list(ap.shape), F32,
                           kind="ExternalOutput").ap()
        dma("gpsimd", d, ap, waits=waits)

    PRO = prologue()

    ar = Arena()
    XN1, _ = layer_norm(x_in, ar, spiking=True)
    if stage <= 1:
        dump("xn", xn, waits=[XN1, now("dve")])
        dump("spike", CO["spike"], waits=[now("dve")])
    if stage > 1:
        ar = Arena()
        Y1 = attn_a(ar, XN1)
        if stage <= 2:
            dump("y", y_hbm, waits=[Y1])
        if stage > 2:
            ar = Arena()
            X1 = out_proj(ar, wouta, CO["bout_a"], x_in, x_hbm, Y1)
            if stage <= 3:
                dump("x1", x_hbm, waits=[X1])
            if stage > 3:
                ar = Arena()
                dbg4 = [] if stage <= 4 else None
                XN2, RT2 = layer_norm(x_hbm, ar, routing=True, dbg=dbg4)
                if stage <= 4:
                    for nm, ap, w in dbg4:
                        dump(nm, ap, waits=w)
                    dump("xn", xn, waits=[XN2, now("dve")])
                    dump("wfm", CO["wfm"], waits=[now("dve")])
                    dump("swrow", CO["swrow"], waits=[now("act")])
                if stage > 4:
                    X2 = ffn(Arena(), XN2, RT2, x_hbm, None)
                    if stage <= 5:
                        dump("x2", x_hbm, waits=[X2])
                    if stage > 5:
                        ar = Arena()
                        XN3, _ = layer_norm(x_hbm, ar)
                        ar = Arena()
                        dbg6 = [] if stage <= 6 else None
                        Y3 = attn_b(ar, XN3, dbg=dbg6)
                        if stage <= 6:
                            for nm, ap, w in dbg6:
                                dump(nm, ap, waits=w)
                            dump("y", y_hbm, waits=[Y3])
                        if stage > 6:
                            ar = Arena()
                            X3 = out_proj(ar, woutb, CO["bout_b"], x_hbm,
                                          x_hbm, Y3)
                            if stage <= 7:
                                dump("x3", x_hbm, waits=[X3])
                            if stage > 7:
                                ar = Arena()
                                XN4, RT4 = layer_norm(x_hbm, ar, routing=True)
                                seed = dma("gpsimd", out, x_hbm, waits=[X3])
                                X4 = ffn(Arena(), XN4, RT4, out, seed)

    S.op("gpsimd", lambda e: None, waits=[now("gpd")])

    import contextlib
    sems = {}
    with contextlib.ExitStack() as st:
        for sname in S.cnt:
            sems[sname] = st.enter_context(nc.semaphore("s_" + sname))
        with nc.Block() as block:
            @block.sync
            def _s(sync):
                S.emit("sync", sync, sems)

            @block.tensor
            def _t(tensor):
                S.emit("tensor", tensor, sems)

            @block.vector
            def _v(vector):
                S.emit("vector", vector, sems)

            @block.scalar
            def _c(scalar):
                S.emit("scalar", scalar, sems)

            @block.gpsimd
            def _g(gpsimd):
                S.emit("gpsimd", gpsimd, sems)
    return nc


def _pack_weights(inp):
    g = {}
    Wqkv = np.asarray(inp["L0_attn_a_W_qkv"], np.float32)
    wa = np.empty((16, 128, 1536), np.float32)
    for h in range(16):
        cols = np.concatenate([np.arange(h * 64, h * 64 + 64),
                               1024 + np.arange(h * 64, h * 64 + 64),
                               2048 + np.arange(h * 64, h * 64 + 64)])
        wa[h] = Wqkv[:, cols].reshape(8, 128, 192).transpose(1, 0, 2).reshape(128, 1536)
    g["wa"] = wa.reshape(16 * 128, 1536)

    wbh = np.empty((16, 128, 1536), np.float32)
    WqT = np.ascontiguousarray(np.asarray(inp["L0_attn_b_Wq"], np.float32).T)
    WkT = np.ascontiguousarray(np.asarray(inp["L0_attn_b_Wk"], np.float32).T)
    WvT = np.ascontiguousarray(np.asarray(inp["L0_attn_b_Wv"], np.float32).T)
    for h in range(16):
        Wh = np.concatenate([WqT[:, h * 64:(h + 1) * 64], WkT[:, h * 64:(h + 1) * 64],
                             WvT[:, h * 64:(h + 1) * 64]], axis=1)
        wbh[h] = Wh.reshape(8, 128, 192).transpose(1, 0, 2).reshape(128, 1536)
    g["wb"] = wbh.reshape(16 * 128, 1536)

    g["wouta"] = np.asarray(inp["L0_attn_a_W_out"], np.float32).reshape(8, 128, 1024) \
        .transpose(1, 0, 2).reshape(128, 8192).copy()
    WoT = np.ascontiguousarray(np.asarray(inp["L0_attn_b_Wo"], np.float32).T)
    g["woutb"] = WoT.reshape(8, 128, 1024).transpose(1, 0, 2).reshape(128, 8192).copy()

    def bias48(bq, bk, bv):
        b = np.empty((64, 48), np.float32)
        for h in range(16):
            b[:, h] = bq[h * 64:(h + 1) * 64]
            b[:, 16 + h] = bk[h * 64:(h + 1) * 64]
            b[:, 32 + h] = bv[h * 64:(h + 1) * 64]
        return b
    bq = np.asarray(inp["L0_attn_a_b_qkv"], np.float32)
    g["bqkv_a"] = bias48(bq[0:1024], bq[1024:2048], bq[2048:3072])
    g["bqkv_b"] = bias48(np.asarray(inp["L0_attn_b_bq"], np.float32),
                         np.asarray(inp["L0_attn_b_bk"], np.float32),
                         np.asarray(inp["L0_attn_b_bv"], np.float32))
    g["bout_a"] = np.asarray(inp["L0_attn_a_b_out"], np.float32).reshape(8, 128).T.copy()
    g["bout_b"] = np.asarray(inp["L0_attn_b_bo"], np.float32).reshape(8, 128).T.copy()

    g["wgate"] = np.asarray(inp["L0_ffn_gate_w"], np.float32).T \
        .reshape(8, 128, 8).transpose(1, 0, 2).reshape(128, 64).copy()
    g["wseg"] = np.asarray(inp["L0_ffn_shared_expert_gate_w"], np.float32).T \
        .reshape(8, 128, 1).transpose(1, 0, 2).reshape(128, 8).copy()
    g["wsur"] = np.asarray(inp["L0_attn_a_w_sur"], np.float32) \
        .reshape(8, 128, 1).transpose(1, 0, 2).reshape(128, 8).copy()

    gu = np.asarray(inp["L0_ffn_experts_gate_up"], np.float32)
    dn = np.asarray(inp["L0_ffn_experts_down"], np.float32)
    sg = np.asarray(inp["L0_ffn_shared_gate_w"], np.float32)
    su = np.asarray(inp["L0_ffn_shared_up_w"], np.float32)
    sd = np.asarray(inp["L0_ffn_shared_down_w"], np.float32)
    wguh = np.empty((NE, IC, 128, 2048), np.float32)
    wdnh = np.empty((NE, 8, 128, 1408), np.float32)
    sgT, suT, sdT = sg.T, su.T, sd.T
    for e in range(NE):
        if e < 8:
            WT = gu[e].T
            gcols, ucols = WT[:, 0:1408], WT[:, 1408:2816]
            DT = dn[e].T
        else:
            off = (e - 8) * 1408
            gcols, ucols = sgT[:, off:off + 1408], suT[:, off:off + 1408]
            DT = sdT[off:off + 1408, :]
        for i in range(IC):
            gstrip = gcols[:, i * 128:(i + 1) * 128].reshape(8, 128, 128)
            ustrip = ucols[:, i * 128:(i + 1) * 128].reshape(8, 128, 128)
            wguh[e, i] = np.stack([gstrip, ustrip]).transpose(2, 0, 1, 3) \
                .reshape(128, 2048)
        for m in range(8):
            wdnh[e, m] = DT[:, m * 128:(m + 1) * 128].reshape(IC, 128, 128) \
                .transpose(1, 0, 2).reshape(128, 1408)
    g["wgu"] = wguh.reshape(NE * IC * 128, 2048)
    g["wdn"] = wdnh.reshape(NE * 8 * 128, 1408)

    g["scal"] = np.array([[np.float32(inp["L0_attn_a_b_sur"]),
                           np.float32(inp["L0_attn_a_threshold"]),
                           np.asarray(inp["L0_attn_b_attn_scale"],
                                      np.float32).reshape(-1)[0], 0.0]], np.float32)
    g["ident"] = np.eye(128, dtype=np.float32)
    g["onesr"] = np.ones((1, 128), np.float32)
    g["onesc"] = np.ones((128, 1), np.float32)
    wm = np.zeros((64, 4), np.float32)
    wm[:, 0] = 1.0; wm[0, 0] = -1.0
    wm[:, 3] = 1.0; wm[0, 3] = -1.0
    g["wmink"] = wm
    sT = np.zeros((2, 128), np.float32); sT[0, 0] = 1.0; sT[1, 64] = 1.0
    g["selT"] = sT
    sS = np.zeros((2, 128), np.float32); sS[0, 1:64] = -1.0; sS[1, 65:128] = 1.0
    g["selS"] = sS
    s8 = np.zeros((8, 1024), np.float32)
    for e in range(8):
        s8[e, e * 128:(e + 1) * 128] = 1.0
    g["sel8"] = s8
    mA = np.zeros((128, 2048), np.float32)
    i_ = np.arange(128)[:, None]; j_ = np.arange(512)[None, :]
    for v in range(4):
        mA[:, v * 512:(v + 1) * 512] = (j_ - i_ >= 128 * v).astype(np.float32)
    g["maskA"] = mA
    mB = np.zeros((128, 128), np.float32)
    ii = np.arange(128)[:, None]; jj = np.arange(128)[None, :]
    vis = ((jj // 64 == ii // 64) & (jj <= ii)) | (jj // 64 > ii // 64)
    mB[~vis] = NEG
    g["maskB"] = mB
    g["epsc"] = np.full((1, 1), 1e-5, np.float32)
    g["negc"] = np.full((128, 1), -1.0, np.float32)
    return g


def kernel(**inputs):
    x = np.asarray(inputs["L0_attn_a_x"], np.float32)
    g = _pack_weights(inputs)
    nc = build_program()
    in_maps = []
    for c in range(8):
        m = dict(g)
        m["x"] = np.ascontiguousarray(x[c].T)
        in_maps.append(m)
    res = run_bass_kernel_spmd(nc, in_maps, core_ids=list(range(8)))
    outs = [np.asarray(res.results[c]["out"]).T for c in range(8)]
    return np.ascontiguousarray(np.stack(outs)).astype(np.float32)


# version 08
# speedup vs baseline: 1.1073x; 1.1073x over previous
import numpy as np
import concourse.bass as bass
import concourse.mybir as mybir
from concourse.bass_utils import run_bass_kernel_spmd

F32 = mybir.dt.float32
BF16 = mybir.dt.bfloat16
AF = mybir.ActivationFunctionType
ALU = mybir.AluOpType
AX = mybir.AxisListType

C = 1024
T = 2048
H = 16
KC = 8
NS = 4
IC = 11
NE = 10
NEG = -1.0e30
ARW = 26768


class Sched:

    def __init__(self):
        self.items = []
        self.cnt = {"syd": 0, "gpd": 0, "pe": 0, "dve": 0, "act": 0, "gpc": 0}
        self.lastw = {}

    def op(self, eng, fn, waits=(), inc=None, amt=1):
        w = []
        for tok in waits:
            if tok is None:
                continue
            s, v = tok
            if v <= 0:
                continue
            assert v <= self.cnt[s], (s, v, self.cnt[s])
            k = (eng, s)
            if self.lastw.get(k, -1) >= v:
                continue
            self.lastw[k] = v
            w.append((s, v))
        tok = None
        if inc is not None:
            self.cnt[inc] += amt
            tok = (inc, self.cnt[inc])
        self.items.append((eng, fn, tuple(w), inc, amt))
        return tok

    def now(self, sem):
        return (sem, self.cnt[sem])

    def barrier(self):
        toks = [self.now(s) for s in self.cnt]
        for eng in ("sync", "tensor", "vector", "scalar", "gpsimd"):
            self.op(eng, lambda e: None, waits=toks)

    def emit(self, eng_name, eng, sems):
        for e, fn, w, inc, amt in self.items:
            if e != eng_name:
                continue
            for s, v in w:
                eng.wait_ge(sems[s], v)
            ins = fn(eng)
            if inc is not None:
                ins.then_inc(sems[inc], amt)


def build_program(stage=99):
    nc = bass.Bass("TRN2", target_bir_lowering=False)
    S = Sched()
    now = S.now

    dr = {}
    def din(name, shape):
        dr[name] = nc.dram_tensor(name, list(shape), F32, kind="ExternalInput").ap()
        return dr[name]

    x_in = din("x", (C, T))
    wa = din("wa", (16 * 128, 1536))
    wb = din("wb", (16 * 128, 1536))
    wouta = din("wouta", (128, 8192))
    woutb = din("woutb", (128, 8192))
    bqkv_a = din("bqkv_a", (64, 48))
    bqkv_b = din("bqkv_b", (64, 48))
    bout_a = din("bout_a", (128, 8))
    bout_b = din("bout_b", (128, 8))
    wgate = din("wgate", (128, 64))
    wseg = din("wseg", (128, 8))
    wsur = din("wsur", (128, 8))
    wgu = din("wgu", (NE * IC * 128, 2048))
    wdn = din("wdn", (NE * 8 * 128, 1408))
    scal = din("scal", (1, 4))
    identD = din("ident", (128, 128))
    onesrD = din("onesr", (1, 128))
    onescD = din("onesc", (128, 1))
    wminkD = din("wmink", (64, 4))
    selTD = din("selT", (2, 128))
    selSD = din("selS", (2, 128))
    sel8D = din("sel8", (8, 1024))
    maskAD = din("maskA", (128, 896))
    maskBD = din("maskB", (128, 128))
    epscD = din("epsc", (1, 1))
    negcD = din("negc", (128, 1))

    x_hbm = nc.dram_tensor("x_hbm", [C, T], F32).ap()
    y_hbm = nc.dram_tensor("y_hbm", [C, T], F32).ap()
    out = nc.dram_tensor("out", [C, T], F32, kind="ExternalOutput").ap()

    xnh = nc.alloc_sbuf_tensor("xnh", [128, KC * T], BF16).ap()
    xnl = nc.alloc_sbuf_tensor("xnl", [128, KC * T], BF16).ap()
    AR = nc.alloc_sbuf_tensor("arena", [128, ARW], F32).ap()
    CO = {}
    for name, shape in [("ident", (128, 128)), ("onesr", (1, 128)),
                        ("onesc", (128, 1)),
                        ("bqkv_a", (64, 48)), ("bqkv_b", (64, 48)),
                        ("bout_a", (128, 8)), ("bout_b", (128, 8)),
                        ("wgate", (128, 64)), ("wseg", (128, 8)),
                        ("wsur", (128, 8)), ("scal", (1, 4)),
                        ("rows", (1, 8)),
                        ("wfm", (8, T)), ("swrow", (1, T)), ("spike", (1, T)),
                        ("scalecol", (128, 1)), ("thr", (1, 1)),
                        ("sel8", (8, 1024)), ("epsc", (1, 1)),
                        ("negc", (128, 1)), ("ltm", (128, 128)),
                        ("wtm", (128, 128)), ("ebf", (128, 128)),
                        ("s2f", (128, 128)), ("mkf", (128, 128)),
                        ("b2f", (128, 128)), ("mx16", (128, 16)),
                        ("rc16", (128, 16)), ("mxs", (128, 16)),
                        ("rcs", (128, 16)), ("selT", (2, 128)),
                        ("selS", (2, 128)), ("wmink", (64, 4))]:
        CO[name] = nc.alloc_sbuf_tensor("c_" + name, [shape[0], shape[1]], F32).ap()
    for name, shape in [("wgateh", (128, 64)), ("wsegh", (128, 8)),
                        ("wsurh", (128, 8)), ("wgatel", (128, 64)),
                        ("wsegl", (128, 8)), ("wsurl", (128, 8))]:
        CO[name] = nc.alloc_sbuf_tensor("c_" + name, [shape[0], shape[1]], BF16).ap()
    PS = [nc.alloc_psum_tensor("ps%d" % i, [128, 512], F32).ap() for i in range(8)]
    BF = {i: None for i in range(8)}
    ident = CO["ident"]

    class Arena:
        def __init__(self):
            self.off = 0
        def take(self, parts, width):
            ap = AR[0:parts, self.off:self.off + width]
            self.off += width
            assert self.off <= ARW, self.off
            return ap

    def dma(q, dst, src, waits=(), accum=False):
        sem = "syd" if q == "sync" else "gpd"
        if accum:
            fn = lambda e, d=dst, s=src: e.dma_start(out=d, in_=s, accum_op=ALU.add)
        else:
            fn = lambda e, d=dst, s=src: e.dma_start(out=d, in_=s)
        return S.op(q, fn, waits, inc=sem, amt=16)

    def mm1(dst, lhsT, rhs, start, stop, waits=(), inc=False):
        fn = lambda e, d=dst, l=lhsT, r=rhs, st=start, sp=stop: \
            e.matmul(d, l, r, start=st, stop=sp)
        return S.op("tensor", fn, waits, inc="pe" if inc else None)

    def mmc(bank, pairs, waits=(), rows=128, n=512, col0=0):
        ap = PS[bank][0:rows, col0:col0 + n]
        t = None
        for i, (l, r) in enumerate(pairs):
            w = (list(waits) + [BF[bank]]) if i == 0 else []
            t = mm1(ap, l, r, i == 0, i == len(pairs) - 1, waits=w,
                    inc=(i == len(pairs) - 1))
        return t

    def mmc3(bank, quads, waits=(), rows=128, n=512):
        ap = PS[bank][0:rows, 0:n]
        total = len(quads) * 3
        j = 0
        t = None
        for lh, ll, rh, rl in quads:
            for l, r_ in ((lh, rh), (lh, rl), (ll, rh)):
                w = (list(waits) + [BF[bank]]) if j == 0 else []
                t = mm1(ap, l, r_, j == 0, j == total - 1, waits=w,
                        inc=(j == total - 1))
                j += 1
        return t

    def trc(bank, lst, waits=()):
        t = None
        for i, (d, s_, idn) in enumerate(lst):
            w = (list(waits) + [BF[bank]]) if i == 0 else []
            fn = lambda e, dd=d, ss=s_, ii=idn, st=(i == 0), sp=(i == len(lst) - 1): \
                e.matmul(dd, ss, ii, start=st, stop=sp, is_transpose=True)
            t = S.op("tensor", fn, waits=w, inc="pe" if i == len(lst) - 1 else None)
        return t

    def rd(bank, tok):
        BF[bank] = tok
        return tok

    def dve(meth, *args, waits=(), **kw):
        fn = lambda e, a=args, k=kw, m=meth: getattr(e, m)(*a, **k)
        return S.op("vector", fn, waits, inc="dve")

    def act(out_ap, in_ap, func, waits=(), **kw):
        fn = lambda e, o=out_ap, i=in_ap, f=func, k=kw: e.activation(o, i, f, **k)
        return S.op("scalar", fn, waits, inc="act")

    def fence_dve():
        return S.op("vector", lambda e: e.drain(fusable=False), inc="dve")

    def prologue():
        toks = []
        for name, d in [("ident", identD), ("onesr", onesrD), ("onesc", onescD),
                        ("bqkv_a", bqkv_a), ("bqkv_b", bqkv_b),
                        ("bout_a", bout_a), ("bout_b", bout_b),
                        ("wgate", wgate), ("wseg", wseg), ("wsur", wsur),
                        ("scal", scal), ("sel8", sel8D),
                        ("epsc", epscD), ("negc", negcD),
                        ("selT", selTD), ("selS", selSD),
                        ("wmink", wminkD)]:
            toks.append(dma("sync", CO[name], d))
        LD = now("syd")
        th = CO["scal"][0:1, 1:2]
        r = CO["rows"]
        act(r[:, 0:1], th, AF.Ln, waits=[LD])
        dve("tensor_scalar", r[:, 2:3], th, -1.0, 1.0, ALU.mult, ALU.add, waits=[LD])
        act(r[:, 4:5], r[:, 2:3], AF.Ln, waits=[now("dve")])
        dve("tensor_sub", CO["thr"], r[:, 0:1], r[:, 4:5], waits=[now("act")])
        t = mmc(7, [(CO["onesr"], CO["scal"][0:1, 2:3])], waits=[LD], n=1)
        rd(7, dve("tensor_copy", CO["scalecol"], PS[7][0:128, 0:1], waits=[t]))
        for nm in ("wgate", "wseg", "wsur"):
            t = dve("tensor_copy", CO[nm + "h"], CO[nm], waits=[LD])
            dve("tensor_sub", CO[nm + "l"], CO[nm], CO[nm + "h"], waits=[t])
        return LD

    def layer_norm(src_dram, ar, routing=False, spiking=False, dbg=None):
        S.barrier()
        xch = [ar.take(128, T), ar.take(128, T)]
        xsq = [ar.take(128, T), ar.take(128, T)]
        rstd_bc = ar.take(128, T)
        mr_bc = ar.take(128, T)
        rowA = ar.take(1, T)
        rowB = ar.take(1, T)
        rowC = ar.take(1, T)
        rowD = ar.take(1, T)
        rowE = ar.take(1, T)

        ch_tok = [None, None]
        sq_tok = [None, None]
        for kc in range(KC):
            b = kc % 2
            ld = dma("sync", xch[b], src_dram[kc * 128:(kc + 1) * 128, :],
                     waits=[ch_tok[b]])
            sq_tok[b] = dve("scalar_tensor_tensor", xsq[b], xch[b], 1.0, xch[b],
                            ALU.mult, ALU.mult, waits=[ld, sq_tok[b]])
            for n in range(NS):
                mm1(PS[n][0:1, :], CO["onesc"], xch[b][:, n * 512:(n + 1) * 512],
                    kc == 0, kc == KC - 1,
                    waits=([ld, BF[n]] if kc == 0 else [ld]), inc=True)
                mm1(PS[4 + n][0:1, :], CO["onesc"], xsq[b][:, n * 512:(n + 1) * 512],
                    kc == 0, kc == KC - 1,
                    waits=([sq_tok[b], BF[4 + n]] if kc == 0 else [sq_tok[b]]),
                    inc=True)
            ch_tok[b] = now("pe")
        STATS = now("pe")
        for n in range(NS):
            rd(n, dve("tensor_copy", rowA[:, n * 512:(n + 1) * 512], PS[n][0:1, :],
                      waits=[STATS]))
            rd(4 + n, dve("tensor_copy", rowB[:, n * 512:(n + 1) * 512],
                          PS[4 + n][0:1, :], waits=[STATS]))
        dve("tensor_scalar_mul", rowC, rowA, 1.0 / C)
        dve("scalar_tensor_tensor", rowA, rowC, 1.0, rowC, ALU.mult, ALU.mult)
        dve("scalar_tensor_tensor", rowB, rowB, 1.0 / C, rowA, ALU.mult, ALU.subtract)
        act(rowD, rowB, AF.Ln, bias=CO["epsc"], waits=[now("dve")])
        act(rowD, rowD, AF.Exp, scale=-0.5)
        RSTD = now("act")
        dve("tensor_mul", rowE, rowC, rowD, waits=[RSTD])
        ROWS = now("dve")
        for n in range(NS):
            sl = slice(n * 512, (n + 1) * 512)
            t = mmc(0, [(CO["onesr"], rowD[:, sl])], waits=[ROWS])
            rd(0, dve("tensor_copy", rstd_bc[:, sl], PS[0], waits=[t]))
            t = mmc(1, [(CO["onesr"], rowE[:, sl])], waits=[ROWS])
            rd(1, dve("tensor_copy", mr_bc[:, sl], PS[1], waits=[t]))
        BC = now("dve")
        for kc in range(KC):
            b = kc % 2
            ld = dma("sync", xch[b], src_dram[kc * 128:(kc + 1) * 128, :],
                     waits=[ch_tok[b]])
            dve("scalar_tensor_tensor", xsq[b], xch[b], 1.0, rstd_bc,
                ALU.mult, ALU.mult, waits=[ld, BC, sq_tok[b]])
            sq_tok[b] = None
            dve("tensor_sub", xch[b], xsq[b], mr_bc)
            dve("tensor_copy", xnh[:, kc * T:(kc + 1) * T], xch[b])
            ch_tok[b] = dve("tensor_sub", xnl[:, kc * T:(kc + 1) * T], xch[b],
                            xnh[:, kc * T:(kc + 1) * T])
        XN = now("dve")

        if spiking:
            for n in range(NS):
                t = mmc3(2, [(CO["wsurh"][:, kc:kc + 1], CO["wsurl"][:, kc:kc + 1],
                              xnh[:, kc * T + n * 512: kc * T + (n + 1) * 512],
                              xnl[:, kc * T + n * 512: kc * T + (n + 1) * 512])
                             for kc in range(KC)], waits=[XN], rows=1)
                dve("tensor_scalar_add", rowA[:, n * 512:(n + 1) * 512],
                    PS[2][0:1, :], CO["scal"][0:1, 0:1], waits=[t])
                rd(2, dve("tensor_single_scalar",
                          CO["spike"][:, n * 512:(n + 1) * 512],
                          rowA[:, n * 512:(n + 1) * 512], CO["thr"], ALU.is_gt))
            if dbg is not None:
                dbg.append(("r1", rowA, [now("dve")]))
                dbg.append(("wsurh", CO["wsurh"], [now("dve")]))
                dbg.append(("wsurl", CO["wsurl"], [now("dve")]))
                dbg.append(("thr", CO["thr"], [now("dve")]))

        RT = None
        if routing:
            lgfm = ar.take(8, T)
            ltm, wtm, ebf, s2f = CO["ltm"], CO["wtm"], CO["ebf"], CO["s2f"]
            mkf, b2f, mx16, rc16 = CO["mkf"], CO["b2f"], CO["mx16"], CO["rc16"]
            for n in range(NS):
                sl2 = slice(n * 512, (n + 1) * 512)
                t = mmc3(2, [(CO["wgateh"][:, kc * 8:(kc + 1) * 8],
                              CO["wgatel"][:, kc * 8:(kc + 1) * 8],
                              xnh[:, kc * T + n * 512: kc * T + (n + 1) * 512],
                              xnl[:, kc * T + n * 512: kc * T + (n + 1) * 512])
                             for kc in range(KC)], waits=[XN], rows=8)
                rd(2, dve("tensor_copy", lgfm[:, sl2], PS[2][0:8, :], waits=[t]))
                t = mmc3(3, [(CO["wsegh"][:, kc:kc + 1], CO["wsegl"][:, kc:kc + 1],
                              xnh[:, kc * T + n * 512: kc * T + (n + 1) * 512],
                              xnl[:, kc * T + n * 512: kc * T + (n + 1) * 512])
                             for kc in range(KC)], waits=[XN], rows=1)
                rd(3, act(CO["swrow"][:, sl2], PS[3][0:1, :], AF.Sigmoid, waits=[t]))
            LG = now("dve")
            for tc in range(16):
                t = trc(4, [(PS[4][:, 0:8], lgfm[:, tc * 128:(tc + 1) * 128],
                             ident[0:8, 0:8])], waits=[LG])
                rd(4, dve("tensor_copy", ltm[:, tc * 8:(tc + 1) * 8], PS[4][:, 0:8],
                          waits=[t]))
            LTM = fence_dve()
            seg = lambda ap, e: ap.rearrange("p (s e) -> p s e", e=e)
            b1 = lambda ap: seg(ap, 1).broadcast_to((128, 16, 8))
            a = act(ebf, ltm, AF.Exp, waits=[LTM])
            dve("tensor_reduce", seg(mx16, 1), seg(ebf, 8), AX.X, ALU.add,
                waits=[a])
            aL = act(CO["rcs"], mx16, AF.Ln, waits=[now("dve")])
            aE = act(rc16, CO["rcs"], AF.Exp, scale=-1.0)
            dve("tensor_tensor", seg(s2f, 8), seg(ebf, 8), b1(rc16), ALU.mult,
                waits=[aE])
            r2 = dve("tensor_reduce", seg(mx16, 1), seg(s2f, 8), AX.X, ALU.max)
            dve("tensor_tensor", seg(mkf, 8), seg(s2f, 8), b1(mx16),
                ALU.subtract, waits=[r2])
            dve("tensor_single_scalar", mkf, mkf, 0.0, ALU.is_ge)
            dve("scalar_tensor_tensor", b2f, mkf, NEG, s2f, ALU.mult, ALU.add)
            r3 = dve("tensor_reduce", seg(mx16, 1), seg(b2f, 8), AX.X, ALU.max)
            dve("tensor_tensor", seg(mkf, 8), seg(s2f, 8), b1(mx16),
                ALU.subtract, waits=[r3])
            dve("tensor_single_scalar", mkf, mkf, 0.0, ALU.is_ge)
            dve("tensor_mul", wtm, mkf, s2f)
            W = fence_dve()
            for tc in range(16):
                t = trc(5, [(PS[5][0:8, 0:128], wtm[:, tc * 8:(tc + 1) * 8],
                             ident)], waits=[W])
                rd(5, dve("tensor_copy", CO["wfm"][:, tc * 128:(tc + 1) * 128],
                          PS[5][0:8, 0:128], waits=[t]))
            RT = (now("dve"), now("act"))
            if dbg is not None:
                dbg.append(("lgfm", lgfm, [now("dve")]))
                dbg.append(("ltm", ltm, [now("dve")]))
                dbg.append(("wtm", wtm, [now("dve")]))
                dbg.append(("ebf", ebf, [now("dve")]))
                dbg.append(("s2f", s2f, [now("dve")]))
                dbg.append(("mkf", mkf, [now("dve")]))
                dbg.append(("b2f", b2f, [now("dve")]))
                dbg.append(("mx16", mx16, [now("dve")]))
                dbg.append(("rc16", rc16, [now("dve")]))
                dbg.append(("mxs", CO["mxs"], [now("dve")]))
                dbg.append(("rcs", CO["rcs"], [now("dve")]))
        return XN, RT

    def attn_a(ar, XNt, dbg=None):
        S.barrier()
        qh = ar.take(64, T)
        kh = ar.take(64, T)
        vh = ar.take(64, T)
        Qp = ar.take(64, T)
        Kp = ar.take(64, T)
        time_sb = ar.take(2, 512)
        srow_sb = ar.take(2, 512)
        vaug = ar.take(128, 16 * 65)
        wslf = ar.take(128, 1536)
        wsh = [ar.take(128, 768).bitcast(BF16) for _ in range(2)]
        wsl_ = [ar.take(128, 768).bitcast(BF16) for _ in range(2)]
        NCS = 3
        CH = [{n: ar.take(128, 512) for n in ("B1", "B2", "B3", "pt")}
              for _ in range(NCS)]
        for cs in range(NCS):
            CH[cs]["ptm"] = CH[cs]["B3"]
        ysb = [AR[64:128, 0:512], AR[64:128, 512:1024]]
        czrow = ar.take(1, 512)
        rrow = ar.take(1, 512)
        t1row = ar.take(1, 512)
        maskAW = ar.take(128, 896)
        erow = [CH[0]["B1"][0:2, 0:512], CH[0]["B2"][0:2, 0:512],
                CH[0]["B3"][0:2, 0:512], CH[1]["B1"][0:2, 0:512],
                CH[1]["B2"][0:2, 0:512]]
        selT, selS, wmink = CO["selT"], CO["selS"], CO["wmink"]

        dma("sync", maskAW, maskAD)
        CLD = now("syd")
        VA = dve("memset", vaug, 1.0)
        SB = (5, 6, 0)

        wuse = [None, None]
        spl_tok = None
        ytok = [None, None]
        set_tok = [None] * NCS
        pv_use = [None] * NCS
        for h in range(H):
            b = h % 2
            WL = dma("sync", wslf, wa[h * 128:(h + 1) * 128, :], waits=[spl_tok])
            dve("tensor_copy", wsh[b], wslf, waits=[WL, wuse[b]])
            dve("tensor_sub", wsl_[b], wslf, wsh[b])
            dve("tensor_copy", CO["mxs"], CO["mx16"])
            spl_tok = dve("tensor_copy", CO["mxs"], CO["mx16"])
            dsts = {0: qh, 1: kh, 2: vh}
            t = None
            for part in range(3):
                for n in range(NS):
                    bank = 1 + part % 2
                    t = mmc3(bank,
                             [(wsh[b][:, kc * 192 + part * 64: kc * 192 + (part + 1) * 64],
                               wsl_[b][:, kc * 192 + part * 64: kc * 192 + (part + 1) * 64],
                               xnh[:, kc * T + n * 512: kc * T + (n + 1) * 512],
                               xnl[:, kc * T + n * 512: kc * T + (n + 1) * 512])
                              for kc in range(KC)],
                            waits=[spl_tok, XNt], rows=64)
                    rd(bank, dve("tensor_scalar_add",
                                 dsts[part][:, n * 512:(n + 1) * 512], PS[bank][0:64, :],
                                 CO["bqkv_a"][:, part * 16 + h: part * 16 + h + 1],
                                 waits=[t]))
            wuse[b] = t
            PROJ = now("dve")
            if dbg is not None and h == 15:
                dbg.append(("xnh2", xnh, [PROJ]))
                dbg.append(("xnl2", xnl, [PROJ]))
                dbg.append(("wslf", wslf, [PROJ]))
                dbg.append(("wsh", wsh[b], [PROJ]))
                dbg.append(("wsl_", wsl_[b], [PROJ]))
                dbg.append(("qh15", qh, [PROJ]))
            dve("scalar_tensor_tensor", Qp, qh, 1.0, qh, ALU.mult, ALU.mult,
                waits=[tok for tok in set_tok])
            dve("scalar_tensor_tensor", Kp, kh, 1.0, kh, ALU.mult, ALU.mult)
            USQ = now("dve")
            dve("memset", qh[0:1, :], 1.0)
            dve("memset", kh[0:1, :], 1.0)
            for n in range(NS):
                sl = slice(n * 512, (n + 1) * 512)
                t = mmc(2, [(wmink[:, 0:2], Qp[:, sl]), (wmink[:, 2:4], Kp[:, sl])],
                        waits=[USQ], rows=2)
                eA, eB, eC, eD, eE = erow
                d1 = dve("tensor_scalar_max", eA, PS[2][0:2, :], 1e-8, waits=[t])
                rd(2, d1)
                act(eB, eA, AF.Ln, waits=[d1])
                act(eC, eB, AF.Exp, scale=0.5)
                act(eD, eC, AF.Exp)
                act(eE, eC, AF.Exp, scale=-1.0)
                a5 = act(eC, eB, AF.Exp, scale=-0.5)
                dve("tensor_add", eA, eD, eE, waits=[a5])
                dve("tensor_scalar_mul", time_sb, eA, 0.5)
                dve("tensor_sub", eA, eD, eE)
                EXPR = dve("scalar_tensor_tensor", srow_sb, eA, 0.5, eC,
                           ALU.mult, ALU.mult)
                t = mmc(3, [(selT[:, 0:64], time_sb),
                            (selS[:, 0:64], srow_sb)],
                        waits=[EXPR], rows=64)
                rd(3, dve("tensor_mul", Qp[:, sl], PS[3][0:64, :], qh[:, sl],
                          waits=[t]))
                t = mmc(4, [(selT[:, 64:128], time_sb),
                            (selS[:, 64:128], srow_sb)],
                        waits=[EXPR], rows=64)
                rd(4, dve("tensor_mul", Kp[:, sl], PS[4][0:64, :], kh[:, sl],
                          waits=[t]))
            QPKP = now("dve")
            for g in range(4):
                bank = 3 + g % 2
                t = trc(bank,
                        [(PS[bank][:, j * 64:(j + 1) * 64],
                          vh[:, (g * 4 + j) * 128:(g * 4 + j + 1) * 128],
                          ident[0:64, 0:64]) for j in range(4)],
                        waits=[PROJ, VA])
                dstv = vaug[:, g * 260:(g + 1) * 260] \
                    .rearrange("p (s e) -> p s e", e=65)[:, :, 0:64]
                srcv = PS[bank][:, 0:256].rearrange("p (s e) -> p s e", e=64)
                rd(bank, dve("tensor_copy", dstv, srcv, waits=[t]))
            VAUG = now("dve")
            for s in range(NS):
                qo = s * 512
                nkc = 4 * (s + 1)
                pv_tok = None
                stok = {}
                def emit_score(kc, qo=qo):
                    bank = SB[kc % 3]
                    stok[kc] = (bank, mmc(bank,
                        [(Kp[:, kc * 128:(kc + 1) * 128], Qp[:, qo:qo + 512])],
                        waits=[QPKP, VAUG]))
                emit_score(0)
                if nkc > 1:
                    emit_score(1)
                for kc in range(nkc):
                    if kc + 2 < nkc:
                        emit_score(kc + 2)
                    cs = kc % NCS
                    bank, t = stok.pop(kc)
                    B1, B2, B3 = CH[cs]["B1"], CH[cs]["B2"], CH[cs]["B3"]
                    pt, ptm = CH[cs]["pt"], CH[cs]["ptm"]
                    d1 = dve("tensor_scalar_max", B1, PS[bank], 1.0000001,
                             waits=[t, set_tok[cs]])
                    rd(bank, d1)
                    d2 = dve("scalar_tensor_tensor", B2, B1, 1.0, B1, ALU.mult, ALU.mult)
                    a1 = act(B3, B2, AF.Ln, bias=CO["negc"], waits=[d2])
                    a2 = act(B2, B3, AF.Exp, scale=0.5)
                    d3 = dve("tensor_add", B3, B1, B2, waits=[a2])
                    a3 = act(B1, B3, AF.Ln, waits=[d3])
                    d4 = dve("scalar_tensor_tensor", B2, B1, 1.0, B1,
                             ALU.mult, ALU.mult, waits=[a3])
                    a4 = act(pt, B2, AF.Exp, scale=-0.125,
                             waits=[d4, pv_use[cs]])
                    set_tok[cs] = a4
                    fin = a4
                    src = pt
                    if kc >= 4 * s:
                        v = kc - 4 * s
                        fin = dve("tensor_mul", ptm, pt,
                                  maskAW[:, 384 - 128 * v: 896 - 128 * v],
                                  waits=[a4, CLD, pv_use[cs]])
                        src = ptm
                    w = [fin]
                    if kc == 0:
                        w.append(BF[7])
                    pv_tok = mm1(PS[7][0:65, :], vaug[:, kc * 65:(kc + 1) * 65], src,
                                 kc == 0, kc == nkc - 1, waits=w,
                                 inc=(kc == nkc - 1))
                    if kc == nkc - 1:
                        for j in range(NCS):
                            pv_use[j] = pv_tok
                yb = ysb[s % 2]
                aL = act(t1row, PS[7][64:65, :], AF.Ln, waits=[pv_tok])
                aE = act(rrow, t1row, AF.Exp, scale=-1.0)
                CS = dve("tensor_mul", czrow, rrow, CO["spike"][:, qo:qo + 512],
                         waits=[aE])
                t = mmc(3, [(CO["onesr"][:, 0:64], czrow)], waits=[CS], rows=64)
                cp = dve("tensor_copy", yb, PS[7][0:64, :], waits=[t, ytok[s % 2]])
                rd(7, cp)
                yev = dve("tensor_mul", yb, yb, PS[3][0:64, :])
                rd(3, yev)
                ytok[s % 2] = dma("gpsimd", y_hbm[h * 64:(h + 1) * 64, qo:qo + 512],
                                  yb, waits=[yev])
        return now("gpd")

    def attn_b(ar, XNt, dbg=None):
        S.barrier()
        qh = ar.take(64, T)
        kh = ar.take(64, T)
        vh = ar.take(64, T)
        vtm = ar.take(128, 1024)
        wslf = ar.take(128, 1536)
        wsh = [ar.take(128, 768).bitcast(BF16) for _ in range(2)]
        wsl_ = [ar.take(128, 768).bitcast(BF16) for _ in range(2)]
        Srow2 = [ar.take(128, T), ar.take(128, T)]
        pnA = [ar.take(128, T) for _ in range(4)]
        ptb = [ar.take(128, 512), ar.take(128, 512)]
        ysb = [ar.take(64, 512), ar.take(64, 512)]
        mcol = ar.take(128, 16)
        maskB = ar.take(128, 128)
        MB = dma("sync", maskB, maskBD)

        wuse = [None, None]
        spl_tok = None
        ytok = [None, None]
        pt_free = [None, None]
        pn_free = [None] * 4
        sr_free = [None, None]
        yev = None
        for h in range(H):
            b = h % 2
            WL = dma("sync", wslf, wb[h * 128:(h + 1) * 128, :], waits=[spl_tok])
            dve("tensor_copy", wsh[b], wslf, waits=[WL, wuse[b]])
            spl_tok = dve("tensor_sub", wsl_[b], wslf, wsh[b])
            dsts = {0: qh, 1: kh, 2: vh}
            t = None
            for part in range(3):
                for n in range(NS):
                    bank = part % 2
                    t = mmc3(bank,
                             [(wsh[b][:, kc * 192 + part * 64: kc * 192 + (part + 1) * 64],
                               wsl_[b][:, kc * 192 + part * 64: kc * 192 + (part + 1) * 64],
                               xnh[:, kc * T + n * 512: kc * T + (n + 1) * 512],
                               xnl[:, kc * T + n * 512: kc * T + (n + 1) * 512])
                              for kc in range(KC)],
                            waits=[spl_tok, XNt], rows=64)
                    rd(bank, dve("tensor_scalar_add",
                                 dsts[part][:, n * 512:(n + 1) * 512], PS[bank][0:64, :],
                                 CO["bqkv_b"][:, part * 16 + h: part * 16 + h + 1],
                                 waits=[t]))
            wuse[b] = t
            PROJ = now("dve")
            for tc in range(16):
                t = trc(5, [(PS[5][:, 0:64], vh[:, tc * 128:(tc + 1) * 128],
                             ident[0:64, 0:64])], waits=[PROJ])
                rd(5, dve("tensor_copy", vtm[:, tc * 64:(tc + 1) * 64],
                          PS[5][:, 0:64], waits=[t]))
            VTM = now("dve")
            for g in range(4):
                for qt in range(4):
                    Q = g * 4 + qt
                    qo = Q * 128
                    L = T - qo
                    lead = qo - g * 512
                    pnq = pnA[qt]
                    qb = qt % 2
                    Srow = Srow2[qb]
                    mc = mcol[:, 8 * qb: 8 * qb + 8]
                    t = mmc(6 if qb == 0 else 2,
                            [(qh[:, qo:qo + 128], kh[:, qo:qo + 128])],
                            waits=[PROJ], n=128)
                    bnk = 6 if qb == 0 else 2
                    rd(bnk, dve("scalar_tensor_tensor", Srow[:, 0:128],
                                PS[bnk][:, 0:128], 1.0, maskB, ALU.mult, ALU.add,
                                waits=[t, MB, sr_free[qb]]))
                    off = 128
                    pidx = 1
                    while qo + off < T:
                        ln_ = min(512, T - qo - off)
                        bank = (6, 5)[pidx % 2] if qb == 0 else (2, 1)[pidx % 2]
                        t = mmc(bank, [(qh[:, qo:qo + 128],
                                        kh[:, qo + off:qo + off + ln_])],
                                waits=[PROJ], n=ln_)
                        rd(bank, dve("tensor_copy", Srow[:, off:off + ln_],
                                     PS[bank][:, 0:ln_], waits=[t]))
                        off += ln_
                        pidx += 1
                    SR = now("dve")
                    rmx = dve("reduce_max", mc[:, 0:1], Srow[:, 0:L], AX.X,
                              waits=[SR])
                    MX = dve("tensor_sub", Srow[:, 0:L], Srow[:, 0:L],
                             mc[:, 0:1].broadcast_to((128, L)), waits=[rmx])
                    pw = [MX, pn_free[qt]]
                    if lead > 0:
                        pw = [dve("memset", pnq[:, 0:lead], 0.0, waits=pw)]
                    sr_free[qb] = act(pnq[:, lead:lead + L], Srow[:, 0:L], AF.Exp,
                                      scale=CO["scalecol"], waits=pw)
                    dve("tensor_reduce", mc[:, 4:5], pnq[:, lead:lead + L],
                        AX.X, ALU.add, waits=[sr_free[qb]])
                    aL = act(mc[:, 2:3], mc[:, 4:5], AF.Ln,
                             waits=[now("dve")])
                    aE = act(mc[:, 6:7], mc[:, 2:3], AF.Exp, scale=-1.0)
                    dve("tensor_mul", pnq[:, lead:lead + L],
                        pnq[:, lead:lead + L],
                        mc[:, 6:7].broadcast_to((128, L)), waits=[aE])
                PN = now("dve")
                Lg = T - g * 512
                nkc = Lg // 128
                pv_tok = None
                for kc in range(nkc):
                    cs = kc % 2
                    bank = 3 + cs
                    t = trc(bank, [(PS[bank][:, qt * 128:(qt + 1) * 128],
                                    pnA[qt][:, kc * 128:(kc + 1) * 128], ident)
                                   for qt in range(4)],
                            waits=[PN, pt_free[cs]])
                    ptile = ptb[cs]
                    ptr = dve("tensor_copy", ptile, PS[bank], waits=[t, pt_free[cs]])
                    rd(bank, ptr)
                    pt_free[cs] = ptr
                    w = [ptr, VTM]
                    if kc == 0:
                        w.append(BF[7])
                    pv_tok = mm1(PS[7][0:64, :],
                                 vtm[:, (g * 4 + kc) * 64:(g * 4 + kc + 1) * 64],
                                 ptile, kc == 0, kc == nkc - 1, waits=w,
                                 inc=(kc == nkc - 1))
                    pt_free[cs] = pv_tok if kc >= nkc - 2 else pt_free[cs]
                for qt in range(4):
                    pn_free[qt] = pv_tok
                yb = ysb[g % 2]
                yev = dve("tensor_copy", yb, PS[7][0:64, :],
                          waits=[pv_tok, ytok[g % 2]])
                rd(7, yev)
                ytok[g % 2] = dma("gpsimd",
                                  y_hbm[h * 64:(h + 1) * 64, g * 512:(g + 1) * 512],
                                  yb, waits=[yev])
        return now("gpd")

    def out_proj(ar, wdram, bias_co, src_x, dst, y_done):
        S.barrier()
        wsb = ar.take(128, 8192)
        yc = ar.take(128, 4096)
        xp = [ar.take(128, 512), ar.take(128, 512)]
        ob = [ar.take(128, 512), ar.take(128, 512)]
        WL = dma("sync", wsb, wdram)
        xp_tok = [None, None]
        ob_tok = [None, None]
        yc_free = None
        for n in range(NS):
            yld = []
            for kc in range(KC):
                yld.append(dma("sync", yc[:, kc * 512:(kc + 1) * 512],
                               y_hbm[kc * 128:(kc + 1) * 128, n * 512:(n + 1) * 512],
                               waits=[y_done, yc_free]))
            for m in range(KC):
                bank = m % 2
                t = mmc(bank, [(wsb[:, kc * 1024 + m * 128: kc * 1024 + (m + 1) * 128],
                                yc[:, kc * 512:(kc + 1) * 512]) for kc in range(KC)],
                        waits=[WL] + yld)
                if m == KC - 1:
                    yc_free = t
                XL = dma("sync", xp[bank],
                         src_x[m * 128:(m + 1) * 128, n * 512:(n + 1) * 512],
                         waits=[xp_tok[bank]])
                d1 = dve("scalar_tensor_tensor", ob[bank], PS[bank], 1.0, xp[bank],
                         ALU.mult, ALU.add, waits=[t, XL, ob_tok[bank]])
                rd(bank, d1)
                d2 = dve("tensor_scalar_add", ob[bank], ob[bank], bias_co[:, m:m + 1])
                xp_tok[bank] = d2
                ob_tok[bank] = dma("gpsimd",
                                   dst[m * 128:(m + 1) * 128, n * 512:(n + 1) * 512],
                                   ob[bank], waits=[d2])
        return now("gpd")

    def ffn(ar, XNt, RT, dst_x, seed_tok):
        S.barrier()
        hwh = ar.take(128, IC * 512).bitcast(BF16)
        hwl = ar.take(128, IC * 512).bitcast(BF16)
        gsl = [ar.take(128, 2048), ar.take(128, 2048)]
        gsh = [ar.take(128, 1024).bitcast(BF16) for _ in range(2)]
        gsll = [ar.take(128, 1024).bitcast(BF16) for _ in range(2)]
        dslf = ar.take(128, 1408)
        dsh = [ar.take(128, 704).bitcast(BF16) for _ in range(2)]
        dsll = [ar.take(128, 704).bitcast(BF16) for _ in range(2)]
        wbc = ar.take(128, 1024)
        c1 = ar.take(128, 512)
        c2 = ar.take(128, 512)
        c2b = c1
        c3 = [ar.take(128, 512), ar.take(128, 512)]
        rt_dve, rt_act = RT
        gtok = [None, None]
        guse = [None, None]
        dtok = [None, None]
        duse = [None, None]
        ctok = [None, None]
        hw_rd = None
        wbc_rd = None
        prev_d = None
        for half in range(2):
            for e in range(NE):
                for n2 in range(2):
                    nb = half * 1024 + n2 * 512
                    if e < 8:
                        t = mmc(4, [(CO["sel8"][:, e * 128:(e + 1) * 128],
                                     CO["wfm"][:, nb:nb + 512])],
                                waits=[rt_dve])
                    else:
                        t = mmc(4, [(CO["onesr"], CO["swrow"][:, nb:nb + 512])],
                                waits=[rt_act])
                    rd(4, dve("tensor_copy", wbc[:, n2 * 512:(n2 + 1) * 512],
                              PS[4], waits=[t, wbc_rd]))
                WBC = now("dve")
                for i in range(IC):
                    b = i % 2
                    GL = dma("sync", gsl[b],
                             wgu[(e * IC + i) * 128:(e * IC + i + 1) * 128, :],
                             waits=[gtok[b]])
                    dve("tensor_copy", gsh[b], gsl[b], waits=[GL, guse[b]])
                    gtok[b] = dve("tensor_sub", gsll[b], gsl[b], gsh[b])
                    SPL = gtok[b]
                    TG = TU = None
                    for n2 in range(2):
                        nb = half * 1024 + n2 * 512
                        TG = mmc3(0, [(gsh[b][:, kc * 128:(kc + 1) * 128],
                                       gsll[b][:, kc * 128:(kc + 1) * 128],
                                       xnh[:, kc * T + nb: kc * T + nb + 512],
                                       xnl[:, kc * T + nb: kc * T + nb + 512])
                                      for kc in range(KC)], waits=[SPL, XNt])
                        TU = mmc3(1, [(gsh[b][:, 1024 + kc * 128:1024 + (kc + 1) * 128],
                                       gsll[b][:, 1024 + kc * 128:1024 + (kc + 1) * 128],
                                       xnh[:, kc * T + nb: kc * T + nb + 512],
                                       xnl[:, kc * T + nb: kc * T + nb + 512])
                                      for kc in range(KC)], waits=[SPL])
                        a = act(c1, PS[0], AF.Silu, waits=[TG, prev_d])
                        rd(0, a)
                        d = dve("tensor_mul", c2, c1, PS[1], waits=[a, TU])
                        rd(1, d)
                        hwsl = slice(i * 1024 + n2 * 512, i * 1024 + (n2 + 1) * 512)
                        dve("tensor_mul", c2b, c2, wbc[:, n2 * 512:(n2 + 1) * 512],
                            waits=[WBC, hw_rd if (i == 0 and n2 == 0) else None])
                        dve("tensor_copy", hwh[:, hwsl], c2b)
                        prev_d = dve("tensor_sub", hwl[:, hwsl], c2b, hwh[:, hwsl])
                    guse[b] = now("pe")
                HW = now("dve")
                wbc_rd = HW
                dspl = None
                for m in range(KC):
                    b = m % 2
                    DL = dma("sync", dslf,
                             wdn[(e * 8 + m) * 128:(e * 8 + m + 1) * 128, :],
                             waits=[dspl])
                    dve("tensor_copy", dsh[b], dslf, waits=[DL, duse[b]])
                    dspl = dve("tensor_sub", dsll[b], dslf, dsh[b])
                    DSP = dspl
                    for n2 in range(2):
                        nb = half * 1024 + n2 * 512
                        bank = 2 + n2
                        t = mmc3(bank, [(dsh[b][:, i * 128:(i + 1) * 128],
                                         dsll[b][:, i * 128:(i + 1) * 128],
                                         hwh[:, i * 1024 + n2 * 512: i * 1024 + (n2 + 1) * 512],
                                         hwl[:, i * 1024 + n2 * 512: i * 1024 + (n2 + 1) * 512])
                                        for i in range(IC)], waits=[DSP, HW])
                        duse[b] = t
                        if m == KC - 1 and n2 == 1:
                            hw_rd = t
                        cb = c3[n2]
                        cpy = dve("tensor_copy", cb, PS[bank], waits=[t, ctok[n2]])
                        rd(bank, cpy)
                        ctok[n2] = dma("gpsimd",
                                       dst_x[m * 128:(m + 1) * 128, nb:nb + 512],
                                       cb, waits=[cpy, seed_tok], accum=True)
        return now("gpd")

    def dump(name, ap, waits=()):
        d = nc.dram_tensor("dbg_" + name, list(ap.shape), ap.dtype,
                           kind="ExternalOutput").ap()
        dma("gpsimd", d, ap, waits=waits)
        S.barrier()

    PRO = prologue()

    ar = Arena()
    dbg1 = [] if stage <= 1 else None
    XN1, _ = layer_norm(x_in, ar, spiking=True, dbg=dbg1)
    if stage <= 1:
        for nm, ap, w in (dbg1 or []):
            dump(nm, ap, waits=w)
        dump("xnh", xnh, waits=[XN1, now("dve")])
        dump("xnl", xnl, waits=[XN1, now("dve")])
        dump("spike", CO["spike"], waits=[now("dve")])
    if stage > 1:
        ar = Arena()
        dbg2 = [] if stage <= 2 else None
        Y1 = attn_a(ar, XN1, dbg=dbg2)
        if stage <= 2:
            for nm, ap, w in (dbg2 or []):
                dump(nm, ap, waits=w)
            dump("y", y_hbm, waits=[Y1])
        if stage > 2:
            ar = Arena()
            X1 = out_proj(ar, wouta, CO["bout_a"], x_in, x_hbm, Y1)
            if stage <= 3:
                dump("x1", x_hbm, waits=[X1])
            if stage > 3:
                ar = Arena()
                dbg4 = [] if stage <= 4 else None
                XN2, RT2 = layer_norm(x_hbm, ar, routing=True, dbg=dbg4)
                if stage <= 4:
                    for nm, ap, w in dbg4:
                        dump(nm, ap, waits=w)
                    dump("xnh", xnh, waits=[XN2, now("dve")])
                    dump("xnl", xnl, waits=[XN2, now("dve")])
                    dump("wfm", CO["wfm"], waits=[now("dve")])
                    dump("swrow", CO["swrow"], waits=[now("act")])
                if stage > 4:
                    X2 = ffn(Arena(), XN2, RT2, x_hbm, None)
                    if stage <= 5:
                        dump("x2", x_hbm, waits=[X2])
                    if stage > 5:
                        ar = Arena()
                        XN3, _ = layer_norm(x_hbm, ar)
                        ar = Arena()
                        dbg6 = [] if stage <= 6 else None
                        Y3 = attn_b(ar, XN3, dbg=dbg6)
                        if stage <= 6:
                            for nm, ap, w in dbg6:
                                dump(nm, ap, waits=w)
                            dump("y", y_hbm, waits=[Y3])
                        if stage > 6:
                            ar = Arena()
                            X3 = out_proj(ar, woutb, CO["bout_b"], x_hbm,
                                          x_hbm, Y3)
                            if stage <= 7:
                                dump("x3", x_hbm, waits=[X3])
                            if stage > 7:
                                ar = Arena()
                                XN4, RT4 = layer_norm(x_hbm, ar, routing=True)
                                seed = dma("gpsimd", out, x_hbm, waits=[X3])
                                X4 = ffn(Arena(), XN4, RT4, out, seed)

    S.op("gpsimd", lambda e: None, waits=[now("gpd")])

    import contextlib
    sems = {}
    with contextlib.ExitStack() as st:
        for sname in S.cnt:
            sems[sname] = st.enter_context(nc.semaphore("s_" + sname))
        with nc.Block() as block:
            @block.sync
            def _s(sync):
                S.emit("sync", sync, sems)

            @block.tensor
            def _t(tensor):
                S.emit("tensor", tensor, sems)

            @block.vector
            def _v(vector):
                S.emit("vector", vector, sems)

            @block.scalar
            def _c(scalar):
                S.emit("scalar", scalar, sems)

            @block.gpsimd
            def _g(gpsimd):
                S.emit("gpsimd", gpsimd, sems)
    return nc


def _pack_weights(inp):
    g = {}
    Wqkv = np.asarray(inp["L0_attn_a_W_qkv"], np.float32)
    wa = np.empty((16, 128, 1536), np.float32)
    for h in range(16):
        cols = np.concatenate([np.arange(h * 64, h * 64 + 64),
                               1024 + np.arange(h * 64, h * 64 + 64),
                               2048 + np.arange(h * 64, h * 64 + 64)])
        wa[h] = Wqkv[:, cols].reshape(8, 128, 192).transpose(1, 0, 2).reshape(128, 1536)
    g["wa"] = wa.reshape(16 * 128, 1536)

    wbh = np.empty((16, 128, 1536), np.float32)
    WqT = np.ascontiguousarray(np.asarray(inp["L0_attn_b_Wq"], np.float32).T)
    WkT = np.ascontiguousarray(np.asarray(inp["L0_attn_b_Wk"], np.float32).T)
    WvT = np.ascontiguousarray(np.asarray(inp["L0_attn_b_Wv"], np.float32).T)
    for h in range(16):
        Wh = np.concatenate([WqT[:, h * 64:(h + 1) * 64], WkT[:, h * 64:(h + 1) * 64],
                             WvT[:, h * 64:(h + 1) * 64]], axis=1)
        wbh[h] = Wh.reshape(8, 128, 192).transpose(1, 0, 2).reshape(128, 1536)
    g["wb"] = wbh.reshape(16 * 128, 1536)

    g["wouta"] = np.asarray(inp["L0_attn_a_W_out"], np.float32).reshape(8, 128, 1024) \
        .transpose(1, 0, 2).reshape(128, 8192).copy()
    WoT = np.ascontiguousarray(np.asarray(inp["L0_attn_b_Wo"], np.float32).T)
    g["woutb"] = WoT.reshape(8, 128, 1024).transpose(1, 0, 2).reshape(128, 8192).copy()

    def bias48(bq, bk, bv):
        b = np.empty((64, 48), np.float32)
        for h in range(16):
            b[:, h] = bq[h * 64:(h + 1) * 64]
            b[:, 16 + h] = bk[h * 64:(h + 1) * 64]
            b[:, 32 + h] = bv[h * 64:(h + 1) * 64]
        return b
    bq = np.asarray(inp["L0_attn_a_b_qkv"], np.float32)
    g["bqkv_a"] = bias48(bq[0:1024], bq[1024:2048], bq[2048:3072])
    g["bqkv_b"] = bias48(np.asarray(inp["L0_attn_b_bq"], np.float32),
                         np.asarray(inp["L0_attn_b_bk"], np.float32),
                         np.asarray(inp["L0_attn_b_bv"], np.float32))
    g["bout_a"] = np.asarray(inp["L0_attn_a_b_out"], np.float32).reshape(8, 128).T.copy()
    g["bout_b"] = np.asarray(inp["L0_attn_b_bo"], np.float32).reshape(8, 128).T.copy()

    g["wgate"] = np.asarray(inp["L0_ffn_gate_w"], np.float32).T \
        .reshape(8, 128, 8).transpose(1, 0, 2).reshape(128, 64).copy()
    g["wseg"] = np.asarray(inp["L0_ffn_shared_expert_gate_w"], np.float32).T \
        .reshape(8, 128, 1).transpose(1, 0, 2).reshape(128, 8).copy()
    g["wsur"] = np.asarray(inp["L0_attn_a_w_sur"], np.float32) \
        .reshape(8, 128, 1).transpose(1, 0, 2).reshape(128, 8).copy()

    gu = np.asarray(inp["L0_ffn_experts_gate_up"], np.float32)
    dn = np.asarray(inp["L0_ffn_experts_down"], np.float32)
    sg = np.asarray(inp["L0_ffn_shared_gate_w"], np.float32)
    su = np.asarray(inp["L0_ffn_shared_up_w"], np.float32)
    sd = np.asarray(inp["L0_ffn_shared_down_w"], np.float32)
    wguh = np.empty((NE, IC, 128, 2048), np.float32)
    wdnh = np.empty((NE, 8, 128, 1408), np.float32)
    sgT, suT, sdT = sg.T, su.T, sd.T
    for e in range(NE):
        if e < 8:
            WT = gu[e].T
            gcols, ucols = WT[:, 0:1408], WT[:, 1408:2816]
            DT = dn[e].T
        else:
            off = (e - 8) * 1408
            gcols, ucols = sgT[:, off:off + 1408], suT[:, off:off + 1408]
            DT = sdT[off:off + 1408, :]
        for i in range(IC):
            gstrip = gcols[:, i * 128:(i + 1) * 128].reshape(8, 128, 128)
            ustrip = ucols[:, i * 128:(i + 1) * 128].reshape(8, 128, 128)
            wguh[e, i] = np.stack([gstrip, ustrip]).transpose(2, 0, 1, 3) \
                .reshape(128, 2048)
        for m in range(8):
            wdnh[e, m] = DT[:, m * 128:(m + 1) * 128].reshape(IC, 128, 128) \
                .transpose(1, 0, 2).reshape(128, 1408)
    g["wgu"] = wguh.reshape(NE * IC * 128, 2048)
    g["wdn"] = wdnh.reshape(NE * 8 * 128, 1408)

    g["scal"] = np.array([[np.float32(inp["L0_attn_a_b_sur"]),
                           np.float32(inp["L0_attn_a_threshold"]),
                           np.asarray(inp["L0_attn_b_attn_scale"],
                                      np.float32).reshape(-1)[0], 0.0]], np.float32)
    g["ident"] = np.eye(128, dtype=np.float32)
    g["onesr"] = np.ones((1, 128), np.float32)
    g["onesc"] = np.ones((128, 1), np.float32)
    wm = np.zeros((64, 4), np.float32)
    wm[:, 0] = 1.0; wm[0, 0] = -1.0
    wm[:, 3] = 1.0; wm[0, 3] = -1.0
    g["wmink"] = wm
    sT = np.zeros((2, 128), np.float32); sT[0, 0] = 1.0; sT[1, 64] = 1.0
    g["selT"] = sT
    sS = np.zeros((2, 128), np.float32); sS[0, 1:64] = -1.0; sS[1, 65:128] = 1.0
    g["selS"] = sS
    s8 = np.zeros((8, 1024), np.float32)
    for e in range(8):
        s8[e, e * 128:(e + 1) * 128] = 1.0
    g["sel8"] = s8
    i_ = np.arange(128)[:, None]; u_ = np.arange(896)[None, :]
    g["maskA"] = (u_ - i_ >= 384).astype(np.float32)
    mB = np.zeros((128, 128), np.float32)
    ii = np.arange(128)[:, None]; jj = np.arange(128)[None, :]
    vis = ((jj // 64 == ii // 64) & (jj <= ii)) | (jj // 64 > ii // 64)
    mB[~vis] = NEG
    g["maskB"] = mB
    g["epsc"] = np.full((1, 1), 1e-5, np.float32)
    g["negc"] = np.full((128, 1), -1.0, np.float32)
    return g


def kernel(**inputs):
    x = np.asarray(inputs["L0_attn_a_x"], np.float32)
    g = _pack_weights(inputs)
    nc = build_program()
    in_maps = []
    for c in range(8):
        m = dict(g)
        m["x"] = np.ascontiguousarray(x[c].T)
        in_maps.append(m)
    res = run_bass_kernel_spmd(nc, in_maps, core_ids=list(range(8)))
    outs = [np.asarray(res.results[c]["out"]).T for c in range(8)]
    return np.ascontiguousarray(np.stack(outs)).astype(np.float32)


# version 09 (final)
# speedup vs baseline: 1.3334x; 1.2042x over previous
import numpy as np
import concourse.bass as bass
import concourse.mybir as mybir
from concourse.bass_utils import run_bass_kernel_spmd

F32 = mybir.dt.float32
BF16 = mybir.dt.bfloat16
AF = mybir.ActivationFunctionType
ALU = mybir.AluOpType
AX = mybir.AxisListType

C = 1024
T = 2048
H = 16
KC = 8
NS = 4
IC = 11
NE = 10
NEG = -1.0e30
ARW = 26768


class Sched:

    def __init__(self):
        self.items = []
        self.cnt = {"syd": 0, "gpd": 0, "pe": 0, "dve": 0, "act": 0, "gpc": 0}
        self.lastw = {}

    def op(self, eng, fn, waits=(), inc=None, amt=1):
        w = []
        for tok in waits:
            if tok is None:
                continue
            s, v = tok
            if v <= 0:
                continue
            assert v <= self.cnt[s], (s, v, self.cnt[s])
            k = (eng, s)
            if self.lastw.get(k, -1) >= v:
                continue
            self.lastw[k] = v
            w.append((s, v))
        tok = None
        if inc is not None:
            self.cnt[inc] += amt
            tok = (inc, self.cnt[inc])
        self.items.append((eng, fn, tuple(w), inc, amt))
        return tok

    def now(self, sem):
        return (sem, self.cnt[sem])

    def barrier(self):
        toks = [self.now(s) for s in self.cnt]
        for eng in ("sync", "tensor", "vector", "scalar", "gpsimd"):
            self.op(eng, lambda e: None, waits=toks)

    def emit(self, eng_name, eng, sems):
        for e, fn, w, inc, amt in self.items:
            if e != eng_name:
                continue
            for s, v in w:
                eng.wait_ge(sems[s], v)
            ins = fn(eng)
            if inc is not None:
                ins.then_inc(sems[inc], amt)


def build_program(stage=99):
    nc = bass.Bass("TRN2", target_bir_lowering=False)
    S = Sched()
    now = S.now

    dr = {}
    def din(name, shape):
        dr[name] = nc.dram_tensor(name, list(shape), F32, kind="ExternalInput").ap()
        return dr[name]

    x_in = din("x", (C, T))
    wa = din("wa", (16 * 128, 1536))
    wb = din("wb", (16 * 128, 1536))
    wouta = din("wouta", (128, 8192))
    woutb = din("woutb", (128, 8192))
    bqkv_a = din("bqkv_a", (64, 48))
    bqkv_b = din("bqkv_b", (64, 48))
    bout_a = din("bout_a", (128, 8))
    bout_b = din("bout_b", (128, 8))
    wgate = din("wgate", (128, 64))
    wseg = din("wseg", (128, 8))
    wsur = din("wsur", (128, 8))
    wgu = din("wgu", (NE * IC * 128, 2048))
    wdn = din("wdn", (NE * 8 * 128, 1408))
    scal = din("scal", (1, 4))
    identD = din("ident", (128, 128))
    onesrD = din("onesr", (1, 128))
    onescD = din("onesc", (128, 1))
    wminkD = din("wmink", (64, 4))
    selTD = din("selT", (2, 128))
    selSD = din("selS", (2, 128))
    sel8D = din("sel8", (8, 1024))
    maskAD = din("maskA", (128, 896))
    maskBD = din("maskB", (128, 128))
    epscD = din("epsc", (1, 1))
    negcD = din("negc", (128, 1))

    x_hbm = nc.dram_tensor("x_hbm", [C, T], F32).ap()
    y_hbm = nc.dram_tensor("y_hbm", [C, T], F32).ap()
    out = nc.dram_tensor("out", [C, T], F32, kind="ExternalOutput").ap()

    xnh = nc.alloc_sbuf_tensor("xnh", [128, KC * T], BF16).ap()
    xnl = nc.alloc_sbuf_tensor("xnl", [128, KC * T], BF16).ap()
    AR = nc.alloc_sbuf_tensor("arena", [128, ARW], F32).ap()
    CO = {}
    for name, shape in [("ident", (128, 128)), ("onesr", (1, 128)),
                        ("onesc", (128, 1)),
                        ("bqkv_a", (64, 48)), ("bqkv_b", (64, 48)),
                        ("bout_a", (128, 8)), ("bout_b", (128, 8)),
                        ("wgate", (128, 64)), ("wseg", (128, 8)),
                        ("wsur", (128, 8)), ("scal", (1, 4)),
                        ("rows", (1, 8)),
                        ("wfm", (8, T)), ("swrow", (1, T)), ("spike", (1, T)),
                        ("scalecol", (128, 1)), ("thr", (1, 1)),
                        ("sel8", (8, 1024)), ("epsc", (1, 1)),
                        ("negc", (128, 1)), ("ltm", (128, 128)),
                        ("wtm", (128, 128)), ("ebf", (128, 128)),
                        ("s2f", (128, 128)), ("mkf", (128, 128)),
                        ("b2f", (128, 128)), ("mx16", (128, 16)),
                        ("rc16", (128, 16)), ("mxs", (128, 16)),
                        ("rcs", (128, 16)), ("selT", (2, 128)),
                        ("selS", (2, 128)), ("wmink", (64, 4))]:
        CO[name] = nc.alloc_sbuf_tensor("c_" + name, [shape[0], shape[1]], F32).ap()
    for name, shape in [("wgateh", (128, 64)), ("wsegh", (128, 8)),
                        ("wsurh", (128, 8)), ("wgatel", (128, 64)),
                        ("wsegl", (128, 8)), ("wsurl", (128, 8))]:
        CO[name] = nc.alloc_sbuf_tensor("c_" + name, [shape[0], shape[1]], BF16).ap()
    PS = [nc.alloc_psum_tensor("ps%d" % i, [128, 512], F32).ap() for i in range(8)]
    BF = {i: None for i in range(8)}
    ident = CO["ident"]

    class Arena:
        def __init__(self):
            self.off = 0
        def take(self, parts, width):
            ap = AR[0:parts, self.off:self.off + width]
            self.off += width
            assert self.off <= ARW, self.off
            return ap

    def dma(q, dst, src, waits=(), accum=False):
        sem = "syd" if q == "sync" else "gpd"
        if accum:
            fn = lambda e, d=dst, s=src: e.dma_start(out=d, in_=s, accum_op=ALU.add)
        else:
            fn = lambda e, d=dst, s=src: e.dma_start(out=d, in_=s)
        return S.op(q, fn, waits, inc=sem, amt=16)

    def mm1(dst, lhsT, rhs, start, stop, waits=(), inc=False):
        fn = lambda e, d=dst, l=lhsT, r=rhs, st=start, sp=stop: \
            e.matmul(d, l, r, start=st, stop=sp)
        return S.op("tensor", fn, waits, inc="pe" if inc else None)

    def mmc(bank, pairs, waits=(), rows=128, n=512, col0=0):
        ap = PS[bank][0:rows, col0:col0 + n]
        t = None
        for i, (l, r) in enumerate(pairs):
            w = (list(waits) + [BF[bank]]) if i == 0 else []
            t = mm1(ap, l, r, i == 0, i == len(pairs) - 1, waits=w,
                    inc=(i == len(pairs) - 1))
        return t

    def mmc3(bank, quads, waits=(), rows=128, n=512):
        ap = PS[bank][0:rows, 0:n]
        total = len(quads) * 3
        j = 0
        t = None
        for lh, ll, rh, rl in quads:
            for l, r_ in ((lh, rh), (lh, rl), (ll, rh)):
                w = (list(waits) + [BF[bank]]) if j == 0 else []
                t = mm1(ap, l, r_, j == 0, j == total - 1, waits=w,
                        inc=(j == total - 1))
                j += 1
        return t

    def trc(bank, lst, waits=()):
        t = None
        for i, (d, s_, idn) in enumerate(lst):
            w = (list(waits) + [BF[bank]]) if i == 0 else []
            fn = lambda e, dd=d, ss=s_, ii=idn, st=(i == 0), sp=(i == len(lst) - 1): \
                e.matmul(dd, ss, ii, start=st, stop=sp, is_transpose=True)
            t = S.op("tensor", fn, waits=w, inc="pe" if i == len(lst) - 1 else None)
        return t

    def rd(bank, tok):
        BF[bank] = tok
        return tok

    def dve(meth, *args, waits=(), **kw):
        fn = lambda e, a=args, k=kw, m=meth: getattr(e, m)(*a, **k)
        return S.op("vector", fn, waits, inc="dve")

    def act(out_ap, in_ap, func, waits=(), **kw):
        fn = lambda e, o=out_ap, i=in_ap, f=func, k=kw: e.activation(o, i, f, **k)
        return S.op("scalar", fn, waits, inc="act")

    def fence_dve():
        return S.op("vector", lambda e: e.drain(fusable=False), inc="dve")

    def prologue():
        toks = []
        for name, d in [("ident", identD), ("onesr", onesrD), ("onesc", onescD),
                        ("bqkv_a", bqkv_a), ("bqkv_b", bqkv_b),
                        ("bout_a", bout_a), ("bout_b", bout_b),
                        ("wgate", wgate), ("wseg", wseg), ("wsur", wsur),
                        ("scal", scal), ("sel8", sel8D),
                        ("epsc", epscD), ("negc", negcD),
                        ("selT", selTD), ("selS", selSD),
                        ("wmink", wminkD)]:
            toks.append(dma("sync", CO[name], d))
        LD = now("syd")
        th = CO["scal"][0:1, 1:2]
        r = CO["rows"]
        act(r[:, 0:1], th, AF.Ln, waits=[LD])
        dve("tensor_scalar", r[:, 2:3], th, -1.0, 1.0, ALU.mult, ALU.add, waits=[LD])
        act(r[:, 4:5], r[:, 2:3], AF.Ln, waits=[now("dve")])
        dve("tensor_sub", CO["thr"], r[:, 0:1], r[:, 4:5], waits=[now("act")])
        t = mmc(7, [(CO["onesr"], CO["scal"][0:1, 2:3])], waits=[LD], n=1)
        rd(7, dve("tensor_copy", CO["scalecol"], PS[7][0:128, 0:1], waits=[t]))
        for nm in ("wgate", "wseg", "wsur"):
            t = dve("tensor_copy", CO[nm + "h"], CO[nm], waits=[LD])
            dve("tensor_sub", CO[nm + "l"], CO[nm], CO[nm + "h"], waits=[t])
        return LD

    def layer_norm(src_dram, ar, routing=False, spiking=False, dbg=None):
        S.barrier()
        xch = [ar.take(128, T), ar.take(128, T)]
        xsq = [ar.take(128, T), ar.take(128, T)]
        rstd_bc = ar.take(128, T)
        mr_bc = ar.take(128, T)
        rowA = ar.take(1, T)
        rowB = ar.take(1, T)
        rowC = ar.take(1, T)
        rowD = ar.take(1, T)
        rowE = ar.take(1, T)

        ch_tok = [None, None]
        sq_tok = [None, None]
        for kc in range(KC):
            b = kc % 2
            ld = dma("sync", xch[b], src_dram[kc * 128:(kc + 1) * 128, :],
                     waits=[ch_tok[b]])
            sq_tok[b] = dve("scalar_tensor_tensor", xsq[b], xch[b], 1.0, xch[b],
                            ALU.mult, ALU.mult, waits=[ld, sq_tok[b]])
            for n in range(NS):
                mm1(PS[n][0:1, :], CO["onesc"], xch[b][:, n * 512:(n + 1) * 512],
                    kc == 0, kc == KC - 1,
                    waits=([ld, BF[n]] if kc == 0 else [ld]), inc=True)
                mm1(PS[4 + n][0:1, :], CO["onesc"], xsq[b][:, n * 512:(n + 1) * 512],
                    kc == 0, kc == KC - 1,
                    waits=([sq_tok[b], BF[4 + n]] if kc == 0 else [sq_tok[b]]),
                    inc=True)
            ch_tok[b] = now("pe")
        STATS = now("pe")
        for n in range(NS):
            rd(n, dve("tensor_copy", rowA[:, n * 512:(n + 1) * 512], PS[n][0:1, :],
                      waits=[STATS]))
            rd(4 + n, dve("tensor_copy", rowB[:, n * 512:(n + 1) * 512],
                          PS[4 + n][0:1, :], waits=[STATS]))
        dve("tensor_scalar_mul", rowC, rowA, 1.0 / C)
        dve("scalar_tensor_tensor", rowA, rowC, 1.0, rowC, ALU.mult, ALU.mult)
        dve("scalar_tensor_tensor", rowB, rowB, 1.0 / C, rowA, ALU.mult, ALU.subtract)
        act(rowD, rowB, AF.Ln, bias=CO["epsc"], waits=[now("dve")])
        act(rowD, rowD, AF.Exp, scale=-0.5)
        RSTD = now("act")
        dve("tensor_mul", rowE, rowC, rowD, waits=[RSTD])
        ROWS = now("dve")
        for n in range(NS):
            sl = slice(n * 512, (n + 1) * 512)
            t = mmc(0, [(CO["onesr"], rowD[:, sl])], waits=[ROWS])
            rd(0, dve("tensor_copy", rstd_bc[:, sl], PS[0], waits=[t]))
            t = mmc(1, [(CO["onesr"], rowE[:, sl])], waits=[ROWS])
            rd(1, dve("tensor_copy", mr_bc[:, sl], PS[1], waits=[t]))
        BC = now("dve")
        for kc in range(KC):
            b = kc % 2
            ld = dma("sync", xch[b], src_dram[kc * 128:(kc + 1) * 128, :],
                     waits=[ch_tok[b]])
            dve("scalar_tensor_tensor", xsq[b], xch[b], 1.0, rstd_bc,
                ALU.mult, ALU.mult, waits=[ld, BC, sq_tok[b]])
            sq_tok[b] = None
            dve("tensor_sub", xch[b], xsq[b], mr_bc)
            dve("tensor_copy", xnh[:, kc * T:(kc + 1) * T], xch[b])
            ch_tok[b] = dve("tensor_sub", xnl[:, kc * T:(kc + 1) * T], xch[b],
                            xnh[:, kc * T:(kc + 1) * T])
        XN = now("dve")

        if spiking:
            for n in range(NS):
                t = mmc3(2, [(CO["wsurh"][:, kc:kc + 1], CO["wsurl"][:, kc:kc + 1],
                              xnh[:, kc * T + n * 512: kc * T + (n + 1) * 512],
                              xnl[:, kc * T + n * 512: kc * T + (n + 1) * 512])
                             for kc in range(KC)], waits=[XN], rows=1)
                dve("tensor_scalar_add", rowA[:, n * 512:(n + 1) * 512],
                    PS[2][0:1, :], CO["scal"][0:1, 0:1], waits=[t])
                rd(2, dve("tensor_single_scalar",
                          CO["spike"][:, n * 512:(n + 1) * 512],
                          rowA[:, n * 512:(n + 1) * 512], CO["thr"], ALU.is_gt))
            if dbg is not None:
                dbg.append(("r1", rowA, [now("dve")]))
                dbg.append(("wsurh", CO["wsurh"], [now("dve")]))
                dbg.append(("wsurl", CO["wsurl"], [now("dve")]))
                dbg.append(("thr", CO["thr"], [now("dve")]))

        RT = None
        if routing:
            lgfm = ar.take(8, T)
            ltm, wtm, ebf, s2f = CO["ltm"], CO["wtm"], CO["ebf"], CO["s2f"]
            mkf, b2f, mx16, rc16 = CO["mkf"], CO["b2f"], CO["mx16"], CO["rc16"]
            for n in range(NS):
                sl2 = slice(n * 512, (n + 1) * 512)
                t = mmc3(2, [(CO["wgateh"][:, kc * 8:(kc + 1) * 8],
                              CO["wgatel"][:, kc * 8:(kc + 1) * 8],
                              xnh[:, kc * T + n * 512: kc * T + (n + 1) * 512],
                              xnl[:, kc * T + n * 512: kc * T + (n + 1) * 512])
                             for kc in range(KC)], waits=[XN], rows=8)
                rd(2, dve("tensor_copy", lgfm[:, sl2], PS[2][0:8, :], waits=[t]))
                t = mmc3(3, [(CO["wsegh"][:, kc:kc + 1], CO["wsegl"][:, kc:kc + 1],
                              xnh[:, kc * T + n * 512: kc * T + (n + 1) * 512],
                              xnl[:, kc * T + n * 512: kc * T + (n + 1) * 512])
                             for kc in range(KC)], waits=[XN], rows=1)
                rd(3, act(CO["swrow"][:, sl2], PS[3][0:1, :], AF.Sigmoid, waits=[t]))
            LG = now("dve")
            for tc in range(16):
                t = trc(4, [(PS[4][:, 0:8], lgfm[:, tc * 128:(tc + 1) * 128],
                             ident[0:8, 0:8])], waits=[LG])
                rd(4, dve("tensor_copy", ltm[:, tc * 8:(tc + 1) * 8], PS[4][:, 0:8],
                          waits=[t]))
            LTM = fence_dve()
            seg = lambda ap, e: ap.rearrange("p (s e) -> p s e", e=e)
            b1 = lambda ap: seg(ap, 1).broadcast_to((128, 16, 8))
            a = act(ebf, ltm, AF.Exp, waits=[LTM])
            dve("tensor_reduce", seg(mx16, 1), seg(ebf, 8), AX.X, ALU.add,
                waits=[a])
            aL = act(CO["rcs"], mx16, AF.Ln, waits=[now("dve")])
            aE = act(rc16, CO["rcs"], AF.Exp, scale=-1.0)
            dve("tensor_tensor", seg(s2f, 8), seg(ebf, 8), b1(rc16), ALU.mult,
                waits=[aE])
            r2 = dve("tensor_reduce", seg(mx16, 1), seg(s2f, 8), AX.X, ALU.max)
            dve("tensor_tensor", seg(mkf, 8), seg(s2f, 8), b1(mx16),
                ALU.subtract, waits=[r2])
            dve("tensor_single_scalar", mkf, mkf, 0.0, ALU.is_ge)
            dve("scalar_tensor_tensor", b2f, mkf, NEG, s2f, ALU.mult, ALU.add)
            r3 = dve("tensor_reduce", seg(mx16, 1), seg(b2f, 8), AX.X, ALU.max)
            dve("tensor_tensor", seg(mkf, 8), seg(s2f, 8), b1(mx16),
                ALU.subtract, waits=[r3])
            dve("tensor_single_scalar", mkf, mkf, 0.0, ALU.is_ge)
            dve("tensor_mul", wtm, mkf, s2f)
            W = fence_dve()
            for tc in range(16):
                t = trc(5, [(PS[5][0:8, 0:128], wtm[:, tc * 8:(tc + 1) * 8],
                             ident)], waits=[W])
                rd(5, dve("tensor_copy", CO["wfm"][:, tc * 128:(tc + 1) * 128],
                          PS[5][0:8, 0:128], waits=[t]))
            RT = (now("dve"), now("act"))
            if dbg is not None:
                dbg.append(("lgfm", lgfm, [now("dve")]))
                dbg.append(("ltm", ltm, [now("dve")]))
                dbg.append(("wtm", wtm, [now("dve")]))
                dbg.append(("ebf", ebf, [now("dve")]))
                dbg.append(("s2f", s2f, [now("dve")]))
                dbg.append(("mkf", mkf, [now("dve")]))
                dbg.append(("b2f", b2f, [now("dve")]))
                dbg.append(("mx16", mx16, [now("dve")]))
                dbg.append(("rc16", rc16, [now("dve")]))
                dbg.append(("mxs", CO["mxs"], [now("dve")]))
                dbg.append(("rcs", CO["rcs"], [now("dve")]))
        return XN, RT

    def attn_a(ar, XNt, dbg=None):
        S.barrier()
        qh = ar.take(64, T)
        kh = ar.take(64, T)
        vh = ar.take(64, T)
        Qp = ar.take(64, T)
        Kp = ar.take(64, T)
        time_sb = ar.take(2, 512)
        srow_sb = ar.take(2, 512)
        vaug = ar.take(128, 16 * 65)
        wslf = ar.take(128, 1536)
        wsh = [ar.take(128, 768).bitcast(BF16) for _ in range(2)]
        wsl_ = [ar.take(128, 768).bitcast(BF16) for _ in range(2)]
        NCS = 3
        CH = [{n: ar.take(128, 512) for n in ("B1", "B2", "B3", "pt")}
              for _ in range(NCS)]
        for cs in range(NCS):
            CH[cs]["ptm"] = CH[cs]["B3"]
        ysb = [AR[64:128, 0:512], AR[64:128, 512:1024]]
        czrow = ar.take(1, 512)
        rrow = ar.take(1, 512)
        t1row = ar.take(1, 512)
        maskAW = ar.take(128, 896)
        erow = [CH[0]["B1"][0:2, 0:512], CH[0]["B2"][0:2, 0:512],
                CH[0]["B3"][0:2, 0:512], CH[1]["B1"][0:2, 0:512],
                CH[1]["B2"][0:2, 0:512]]
        selT, selS, wmink = CO["selT"], CO["selS"], CO["wmink"]

        dma("sync", maskAW, maskAD)
        CLD = now("syd")
        VA = dve("memset", vaug, 1.0)
        SB = (5, 6, 0)

        wuse = [None, None]
        spl_tok = None
        ytok = [None, None]
        set_tok = [None] * NCS
        pv_use = [None] * NCS
        for h in range(H):
            b = h % 2
            WL = dma("sync", wslf, wa[h * 128:(h + 1) * 128, :], waits=[spl_tok])
            dve("tensor_copy", wsh[b], wslf, waits=[WL, wuse[b]])
            dve("tensor_sub", wsl_[b], wslf, wsh[b])
            dve("tensor_copy", CO["mxs"], CO["mx16"])
            spl_tok = dve("tensor_copy", CO["mxs"], CO["mx16"])
            dsts = {0: qh, 1: kh, 2: vh}
            t = None
            for part in range(3):
                for n in range(NS):
                    bank = 1 + part % 2
                    t = mmc3(bank,
                             [(wsh[b][:, kc * 192 + part * 64: kc * 192 + (part + 1) * 64],
                               wsl_[b][:, kc * 192 + part * 64: kc * 192 + (part + 1) * 64],
                               xnh[:, kc * T + n * 512: kc * T + (n + 1) * 512],
                               xnl[:, kc * T + n * 512: kc * T + (n + 1) * 512])
                              for kc in range(KC)],
                            waits=[spl_tok, XNt], rows=64)
                    rd(bank, dve("tensor_scalar_add",
                                 dsts[part][:, n * 512:(n + 1) * 512], PS[bank][0:64, :],
                                 CO["bqkv_a"][:, part * 16 + h: part * 16 + h + 1],
                                 waits=[t]))
            wuse[b] = t
            PROJ = now("dve")
            if dbg is not None and h == 15:
                dbg.append(("xnh2", xnh, [PROJ]))
                dbg.append(("xnl2", xnl, [PROJ]))
                dbg.append(("wslf", wslf, [PROJ]))
                dbg.append(("wsh", wsh[b], [PROJ]))
                dbg.append(("wsl_", wsl_[b], [PROJ]))
                dbg.append(("qh15", qh, [PROJ]))
            dve("scalar_tensor_tensor", Qp, qh, 1.0, qh, ALU.mult, ALU.mult,
                waits=[tok for tok in set_tok])
            dve("scalar_tensor_tensor", Kp, kh, 1.0, kh, ALU.mult, ALU.mult)
            USQ = now("dve")
            dve("memset", qh[0:1, :], 1.0)
            dve("memset", kh[0:1, :], 1.0)
            for n in range(NS):
                sl = slice(n * 512, (n + 1) * 512)
                t = mmc(2, [(wmink[:, 0:2], Qp[:, sl]), (wmink[:, 2:4], Kp[:, sl])],
                        waits=[USQ], rows=2)
                eA, eB, eC, eD, eE = erow
                d1 = dve("tensor_scalar_max", eA, PS[2][0:2, :], 1e-8, waits=[t])
                rd(2, d1)
                act(eB, eA, AF.Ln, waits=[d1])
                act(eC, eB, AF.Exp, scale=0.5)
                act(eD, eC, AF.Exp)
                act(eE, eC, AF.Exp, scale=-1.0)
                a5 = act(eC, eB, AF.Exp, scale=-0.5)
                dve("tensor_add", eA, eD, eE, waits=[a5])
                dve("tensor_scalar_mul", time_sb, eA, 0.5)
                dve("tensor_sub", eA, eD, eE)
                EXPR = dve("scalar_tensor_tensor", srow_sb, eA, 0.5, eC,
                           ALU.mult, ALU.mult)
                t = mmc(3, [(selT[:, 0:64], time_sb),
                            (selS[:, 0:64], srow_sb)],
                        waits=[EXPR], rows=64)
                rd(3, dve("tensor_mul", Qp[:, sl], PS[3][0:64, :], qh[:, sl],
                          waits=[t]))
                t = mmc(4, [(selT[:, 64:128], time_sb),
                            (selS[:, 64:128], srow_sb)],
                        waits=[EXPR], rows=64)
                rd(4, dve("tensor_mul", Kp[:, sl], PS[4][0:64, :], kh[:, sl],
                          waits=[t]))
            QPKP = now("dve")
            for g in range(4):
                bank = 3 + g % 2
                t = trc(bank,
                        [(PS[bank][:, j * 64:(j + 1) * 64],
                          vh[:, (g * 4 + j) * 128:(g * 4 + j + 1) * 128],
                          ident[0:64, 0:64]) for j in range(4)],
                        waits=[PROJ, VA])
                dstv = vaug[:, g * 260:(g + 1) * 260] \
                    .rearrange("p (s e) -> p s e", e=65)[:, :, 0:64]
                srcv = PS[bank][:, 0:256].rearrange("p (s e) -> p s e", e=64)
                rd(bank, dve("tensor_copy", dstv, srcv, waits=[t]))
            VAUG = now("dve")
            for s in range(NS):
                qo = s * 512
                nkc = 4 * (s + 1)
                pv_tok = None
                stok = {}
                def emit_score(kc, qo=qo):
                    bank = SB[kc % 3]
                    stok[kc] = (bank, mmc(bank,
                        [(Kp[:, kc * 128:(kc + 1) * 128], Qp[:, qo:qo + 512])],
                        waits=[QPKP, VAUG]))
                emit_score(0)
                if nkc > 1:
                    emit_score(1)
                for kc in range(nkc):
                    if kc + 2 < nkc:
                        emit_score(kc + 2)
                    cs = kc % NCS
                    bank, t = stok.pop(kc)
                    B1, B2, B3 = CH[cs]["B1"], CH[cs]["B2"], CH[cs]["B3"]
                    pt, ptm = CH[cs]["pt"], CH[cs]["ptm"]
                    d1 = dve("tensor_scalar_max", B1, PS[bank], 1.0000001,
                             waits=[t, set_tok[cs]])
                    rd(bank, d1)
                    d2 = dve("scalar_tensor_tensor", B2, B1, 1.0, B1, ALU.mult, ALU.mult)
                    a1 = act(B3, B2, AF.Ln, bias=CO["negc"], waits=[d2])
                    a2 = act(B2, B3, AF.Exp, scale=0.5)
                    d3 = dve("tensor_add", B3, B1, B2, waits=[a2])
                    a3 = act(B1, B3, AF.Ln, waits=[d3])
                    d4 = dve("scalar_tensor_tensor", B2, B1, 1.0, B1,
                             ALU.mult, ALU.mult, waits=[a3])
                    a4 = act(pt, B2, AF.Exp, scale=-0.125,
                             waits=[d4, pv_use[cs]])
                    set_tok[cs] = a4
                    fin = a4
                    src = pt
                    if kc >= 4 * s:
                        v = kc - 4 * s
                        fin = dve("tensor_mul", ptm, pt,
                                  maskAW[:, 384 - 128 * v: 896 - 128 * v],
                                  waits=[a4, CLD, pv_use[cs]])
                        src = ptm
                    w = [fin]
                    if kc == 0:
                        w.append(BF[7])
                    pv_tok = mm1(PS[7][0:65, :], vaug[:, kc * 65:(kc + 1) * 65], src,
                                 kc == 0, kc == nkc - 1, waits=w,
                                 inc=(kc == nkc - 1))
                    if kc == nkc - 1:
                        for j in range(NCS):
                            pv_use[j] = pv_tok
                yb = ysb[s % 2]
                aL = act(t1row, PS[7][64:65, :], AF.Ln, waits=[pv_tok])
                aE = act(rrow, t1row, AF.Exp, scale=-1.0)
                CS = dve("tensor_mul", czrow, rrow, CO["spike"][:, qo:qo + 512],
                         waits=[aE])
                t = mmc(3, [(CO["onesr"][:, 0:64], czrow)], waits=[CS], rows=64)
                cp = dve("tensor_copy", yb, PS[7][0:64, :], waits=[t, ytok[s % 2]])
                rd(7, cp)
                yev = dve("tensor_mul", yb, yb, PS[3][0:64, :])
                rd(3, yev)
                ytok[s % 2] = dma("gpsimd", y_hbm[h * 64:(h + 1) * 64, qo:qo + 512],
                                  yb, waits=[yev])
        return now("gpd")

    def attn_b(ar, XNt, dbg=None):
        S.barrier()
        qh = ar.take(64, T)
        kh = ar.take(64, T)
        vh = ar.take(64, T)
        vtm = ar.take(128, 1024)
        wslf = ar.take(128, 1536)
        wsh = [ar.take(128, 768).bitcast(BF16) for _ in range(2)]
        wsl_ = [ar.take(128, 768).bitcast(BF16) for _ in range(2)]
        Srow2 = [ar.take(128, T), ar.take(128, T)]
        pnA = [ar.take(128, T) for _ in range(4)]
        ptb = [ar.take(128, 512), ar.take(128, 512)]
        ysb = [ar.take(64, 512), ar.take(64, 512)]
        mcol = ar.take(128, 16)
        maskB = ar.take(128, 128)
        MB = dma("sync", maskB, maskBD)

        wuse = [None, None]
        spl_tok = None
        ytok = [None, None]
        pt_free = [None, None]
        pn_free = [None] * 4
        sr_free = [None, None]
        yev = None
        for h in range(H):
            b = h % 2
            WL = dma("sync", wslf, wb[h * 128:(h + 1) * 128, :], waits=[spl_tok])
            dve("tensor_copy", wsh[b], wslf, waits=[WL, wuse[b]])
            spl_tok = dve("tensor_sub", wsl_[b], wslf, wsh[b])
            dsts = {0: qh, 1: kh, 2: vh}
            t = None
            for part in range(3):
                for n in range(NS):
                    bank = part % 2
                    t = mmc3(bank,
                             [(wsh[b][:, kc * 192 + part * 64: kc * 192 + (part + 1) * 64],
                               wsl_[b][:, kc * 192 + part * 64: kc * 192 + (part + 1) * 64],
                               xnh[:, kc * T + n * 512: kc * T + (n + 1) * 512],
                               xnl[:, kc * T + n * 512: kc * T + (n + 1) * 512])
                              for kc in range(KC)],
                            waits=[spl_tok, XNt], rows=64)
                    rd(bank, dve("tensor_scalar_add",
                                 dsts[part][:, n * 512:(n + 1) * 512], PS[bank][0:64, :],
                                 CO["bqkv_b"][:, part * 16 + h: part * 16 + h + 1],
                                 waits=[t]))
            wuse[b] = t
            PROJ = now("dve")
            for tc in range(16):
                t = trc(5, [(PS[5][:, 0:64], vh[:, tc * 128:(tc + 1) * 128],
                             ident[0:64, 0:64])], waits=[PROJ])
                rd(5, dve("tensor_copy", vtm[:, tc * 64:(tc + 1) * 64],
                          PS[5][:, 0:64], waits=[t]))
            VTM = now("dve")
            for g in range(4):
                for qt in range(4):
                    Q = g * 4 + qt
                    qo = Q * 128
                    L = T - qo
                    lead = qo - g * 512
                    pnq = pnA[qt]
                    qb = qt % 2
                    Srow = Srow2[qb]
                    mc = mcol[:, 8 * qb: 8 * qb + 8]
                    t = mmc(6 if qb == 0 else 2,
                            [(qh[:, qo:qo + 128], kh[:, qo:qo + 128])],
                            waits=[PROJ], n=128)
                    bnk = 6 if qb == 0 else 2
                    rd(bnk, dve("scalar_tensor_tensor", Srow[:, 0:128],
                                PS[bnk][:, 0:128], 1.0, maskB, ALU.mult, ALU.add,
                                waits=[t, MB, sr_free[qb]]))
                    off = 128
                    pidx = 1
                    while qo + off < T:
                        ln_ = min(512, T - qo - off)
                        bank = (6, 5)[pidx % 2] if qb == 0 else (2, 1)[pidx % 2]
                        t = mmc(bank, [(qh[:, qo:qo + 128],
                                        kh[:, qo + off:qo + off + ln_])],
                                waits=[PROJ], n=ln_)
                        rd(bank, dve("tensor_copy", Srow[:, off:off + ln_],
                                     PS[bank][:, 0:ln_], waits=[t]))
                        off += ln_
                        pidx += 1
                    SR = now("dve")
                    rmx = dve("reduce_max", mc[:, 0:1], Srow[:, 0:L], AX.X,
                              waits=[SR])
                    MX = dve("tensor_sub", Srow[:, 0:L], Srow[:, 0:L],
                             mc[:, 0:1].broadcast_to((128, L)), waits=[rmx])
                    pw = [MX, pn_free[qt]]
                    if lead > 0:
                        pw = [dve("memset", pnq[:, 0:lead], 0.0, waits=pw)]
                    sr_free[qb] = act(pnq[:, lead:lead + L], Srow[:, 0:L], AF.Exp,
                                      scale=CO["scalecol"], waits=pw)
                    dve("tensor_reduce", mc[:, 4:5], pnq[:, lead:lead + L],
                        AX.X, ALU.add, waits=[sr_free[qb]])
                    aL = act(mc[:, 2:3], mc[:, 4:5], AF.Ln,
                             waits=[now("dve")])
                    aE = act(mc[:, 6:7], mc[:, 2:3], AF.Exp, scale=-1.0)
                    dve("tensor_mul", pnq[:, lead:lead + L],
                        pnq[:, lead:lead + L],
                        mc[:, 6:7].broadcast_to((128, L)), waits=[aE])
                PN = now("dve")
                Lg = T - g * 512
                nkc = Lg // 128
                pv_tok = None
                for kc in range(nkc):
                    cs = kc % 2
                    bank = 3 + cs
                    t = trc(bank, [(PS[bank][:, qt * 128:(qt + 1) * 128],
                                    pnA[qt][:, kc * 128:(kc + 1) * 128], ident)
                                   for qt in range(4)],
                            waits=[PN, pt_free[cs]])
                    ptile = ptb[cs]
                    ptr = dve("tensor_copy", ptile, PS[bank], waits=[t, pt_free[cs]])
                    rd(bank, ptr)
                    pt_free[cs] = ptr
                    w = [ptr, VTM]
                    if kc == 0:
                        w.append(BF[7])
                    pv_tok = mm1(PS[7][0:64, :],
                                 vtm[:, (g * 4 + kc) * 64:(g * 4 + kc + 1) * 64],
                                 ptile, kc == 0, kc == nkc - 1, waits=w,
                                 inc=(kc == nkc - 1))
                    pt_free[cs] = pv_tok if kc >= nkc - 2 else pt_free[cs]
                for qt in range(4):
                    pn_free[qt] = pv_tok
                yb = ysb[g % 2]
                yev = dve("tensor_copy", yb, PS[7][0:64, :],
                          waits=[pv_tok, ytok[g % 2]])
                rd(7, yev)
                ytok[g % 2] = dma("gpsimd",
                                  y_hbm[h * 64:(h + 1) * 64, g * 512:(g + 1) * 512],
                                  yb, waits=[yev])
        return now("gpd")

    def out_proj(ar, wdram, bias_co, src_x, dst, y_done):
        S.barrier()
        wsb = ar.take(128, 8192)
        yc = ar.take(128, 4096)
        xp = [ar.take(128, 512), ar.take(128, 512)]
        ob = [ar.take(128, 512), ar.take(128, 512)]
        WL = dma("sync", wsb, wdram)
        xp_tok = [None, None]
        ob_tok = [None, None]
        yc_free = None
        for n in range(NS):
            yld = []
            for kc in range(KC):
                yld.append(dma("sync", yc[:, kc * 512:(kc + 1) * 512],
                               y_hbm[kc * 128:(kc + 1) * 128, n * 512:(n + 1) * 512],
                               waits=[y_done, yc_free]))
            for m in range(KC):
                bank = m % 2
                t = mmc(bank, [(wsb[:, kc * 1024 + m * 128: kc * 1024 + (m + 1) * 128],
                                yc[:, kc * 512:(kc + 1) * 512]) for kc in range(KC)],
                        waits=[WL] + yld)
                if m == KC - 1:
                    yc_free = t
                XL = dma("sync", xp[bank],
                         src_x[m * 128:(m + 1) * 128, n * 512:(n + 1) * 512],
                         waits=[xp_tok[bank]])
                d1 = dve("scalar_tensor_tensor", ob[bank], PS[bank], 1.0, xp[bank],
                         ALU.mult, ALU.add, waits=[t, XL, ob_tok[bank]])
                rd(bank, d1)
                d2 = dve("tensor_scalar_add", ob[bank], ob[bank], bias_co[:, m:m + 1])
                xp_tok[bank] = d2
                ob_tok[bank] = dma("gpsimd",
                                   dst[m * 128:(m + 1) * 128, n * 512:(n + 1) * 512],
                                   ob[bank], waits=[d2])
        return now("gpd")

    def ffn(ar, XNt, RT, dst_x, seed_tok):
        S.barrier()
        hwh = ar.take(128, IC * 512).bitcast(BF16)
        hwl = ar.take(128, IC * 512).bitcast(BF16)
        gsl = [ar.take(128, 2048), ar.take(128, 2048)]
        gsh = [ar.take(128, 1024).bitcast(BF16) for _ in range(2)]
        gsll = [ar.take(128, 1024).bitcast(BF16) for _ in range(2)]
        dslf = ar.take(128, 1408)
        dsh = [ar.take(128, 704).bitcast(BF16) for _ in range(2)]
        dsll = [ar.take(128, 704).bitcast(BF16) for _ in range(2)]
        wbc = ar.take(128, 1024)
        c1 = ar.take(128, 512)
        c2 = ar.take(128, 512)
        c2b = c1
        c3 = [ar.take(128, 512), ar.take(128, 512)]
        rt_dve, rt_act = RT
        gtok = [None, None]
        guse = [None, None]
        dtok = [None, None]
        duse = [None, None]
        ctok = [None, None]
        hw_rd = None
        wbc_rd = None
        prev_d = None
        for half in range(2):
            for e in range(NE):
                for n2 in range(2):
                    nb = half * 1024 + n2 * 512
                    if e < 8:
                        t = mmc(4, [(CO["sel8"][:, e * 128:(e + 1) * 128],
                                     CO["wfm"][:, nb:nb + 512])],
                                waits=[rt_dve])
                    else:
                        t = mmc(4, [(CO["onesr"], CO["swrow"][:, nb:nb + 512])],
                                waits=[rt_act])
                    rd(4, dve("tensor_copy", wbc[:, n2 * 512:(n2 + 1) * 512],
                              PS[4], waits=[t, wbc_rd]))
                WBC = now("dve")
                spl = {}
                def gu_load(i):
                    b = i % 2
                    GL = dma("sync", gsl[b],
                             wgu[(e * IC + i) * 128:(e * IC + i + 1) * 128, :],
                             waits=[gtok[b]])
                    dve("tensor_copy", gsh[b], gsl[b], waits=[GL, guse[b]])
                    gtok[b] = dve("tensor_sub", gsll[b], gsl[b], gsh[b])
                    spl[i] = gtok[b]
                gu_load(0)
                for i in range(IC):
                    b = i % 2
                    SPL = spl[i]
                    for n2 in range(2):
                        nb = half * 1024 + n2 * 512
                        TG = mmc3(0, [(gsh[b][:, kc * 128:(kc + 1) * 128],
                                       gsll[b][:, kc * 128:(kc + 1) * 128],
                                       xnh[:, kc * T + nb: kc * T + nb + 512],
                                       xnl[:, kc * T + nb: kc * T + nb + 512])
                                      for kc in range(KC)], waits=[SPL, XNt])
                        TU = mmc3(1, [(gsh[b][:, 1024 + kc * 128:1024 + (kc + 1) * 128],
                                       gsll[b][:, 1024 + kc * 128:1024 + (kc + 1) * 128],
                                       xnh[:, kc * T + nb: kc * T + nb + 512],
                                       xnl[:, kc * T + nb: kc * T + nb + 512])
                                      for kc in range(KC)], waits=[SPL])
                        a = act(c1, PS[0], AF.Silu, waits=[TG, prev_d])
                        rd(0, a)
                        d = dve("tensor_mul", c2, c1, PS[1], waits=[a, TU])
                        rd(1, d)
                        hwsl = slice(i * 1024 + n2 * 512, i * 1024 + (n2 + 1) * 512)
                        dve("tensor_mul", c2b, c2, wbc[:, n2 * 512:(n2 + 1) * 512],
                            waits=[WBC, hw_rd if (i == 0 and n2 == 0) else None])
                        dve("tensor_copy", hwh[:, hwsl], c2b)
                        prev_d = dve("tensor_sub", hwl[:, hwsl], c2b, hwh[:, hwsl])
                        if n2 == 0 and i + 1 < IC:
                            gu_load(i + 1)
                    guse[b] = now("pe")
                HW = now("dve")
                wbc_rd = HW
                dsp = {}
                def dn_load(m):
                    b = m % 2
                    DL = dma("sync", dslf,
                             wdn[(e * 8 + m) * 128:(e * 8 + m + 1) * 128, :],
                             waits=[dsp.get(m - 1)])
                    dve("tensor_copy", dsh[b], dslf, waits=[DL, duse[b]])
                    dsp[m] = dve("tensor_sub", dsll[b], dslf, dsh[b])
                dn_load(0)
                for m in range(KC):
                    b = m % 2
                    DSP = dsp[m]
                    for n2 in range(2):
                        nb = half * 1024 + n2 * 512
                        bank = 2 + n2
                        t = mmc3(bank, [(dsh[b][:, i * 128:(i + 1) * 128],
                                         dsll[b][:, i * 128:(i + 1) * 128],
                                         hwh[:, i * 1024 + n2 * 512: i * 1024 + (n2 + 1) * 512],
                                         hwl[:, i * 1024 + n2 * 512: i * 1024 + (n2 + 1) * 512])
                                        for i in range(IC)], waits=[DSP, HW])
                        duse[b] = t
                        if m == KC - 1 and n2 == 1:
                            hw_rd = t
                        cb = c3[n2]
                        cpy = dve("tensor_copy", cb, PS[bank], waits=[t, ctok[n2]])
                        rd(bank, cpy)
                        ctok[n2] = dma("gpsimd",
                                       dst_x[m * 128:(m + 1) * 128, nb:nb + 512],
                                       cb, waits=[cpy, seed_tok], accum=True)
                        if n2 == 0 and m + 1 < KC:
                            dn_load(m + 1)
        return now("gpd")

    def dump(name, ap, waits=()):
        d = nc.dram_tensor("dbg_" + name, list(ap.shape), ap.dtype,
                           kind="ExternalOutput").ap()
        dma("gpsimd", d, ap, waits=waits)
        S.barrier()

    PRO = prologue()

    ar = Arena()
    dbg1 = [] if stage <= 1 else None
    XN1, _ = layer_norm(x_in, ar, spiking=True, dbg=dbg1)
    if stage <= 1:
        for nm, ap, w in (dbg1 or []):
            dump(nm, ap, waits=w)
        dump("xnh", xnh, waits=[XN1, now("dve")])
        dump("xnl", xnl, waits=[XN1, now("dve")])
        dump("spike", CO["spike"], waits=[now("dve")])
    if stage > 1:
        ar = Arena()
        dbg2 = [] if stage <= 2 else None
        Y1 = attn_a(ar, XN1, dbg=dbg2)
        if stage <= 2:
            for nm, ap, w in (dbg2 or []):
                dump(nm, ap, waits=w)
            dump("y", y_hbm, waits=[Y1])
        if stage > 2:
            ar = Arena()
            X1 = out_proj(ar, wouta, CO["bout_a"], x_in, x_hbm, Y1)
            if stage <= 3:
                dump("x1", x_hbm, waits=[X1])
            if stage > 3:
                ar = Arena()
                dbg4 = [] if stage <= 4 else None
                XN2, RT2 = layer_norm(x_hbm, ar, routing=True, dbg=dbg4)
                if stage <= 4:
                    for nm, ap, w in dbg4:
                        dump(nm, ap, waits=w)
                    dump("xnh", xnh, waits=[XN2, now("dve")])
                    dump("xnl", xnl, waits=[XN2, now("dve")])
                    dump("wfm", CO["wfm"], waits=[now("dve")])
                    dump("swrow", CO["swrow"], waits=[now("act")])
                if stage > 4:
                    X2 = ffn(Arena(), XN2, RT2, x_hbm, None)
                    if stage <= 5:
                        dump("x2", x_hbm, waits=[X2])
                    if stage > 5:
                        ar = Arena()
                        XN3, _ = layer_norm(x_hbm, ar)
                        ar = Arena()
                        dbg6 = [] if stage <= 6 else None
                        Y3 = attn_b(ar, XN3, dbg=dbg6)
                        if stage <= 6:
                            for nm, ap, w in dbg6:
                                dump(nm, ap, waits=w)
                            dump("y", y_hbm, waits=[Y3])
                        if stage > 6:
                            ar = Arena()
                            X3 = out_proj(ar, woutb, CO["bout_b"], x_hbm,
                                          x_hbm, Y3)
                            if stage <= 7:
                                dump("x3", x_hbm, waits=[X3])
                            if stage > 7:
                                ar = Arena()
                                XN4, RT4 = layer_norm(x_hbm, ar, routing=True)
                                seed = dma("gpsimd", out, x_hbm, waits=[X3])
                                X4 = ffn(Arena(), XN4, RT4, out, seed)

    S.op("gpsimd", lambda e: None, waits=[now("gpd")])

    import contextlib
    sems = {}
    with contextlib.ExitStack() as st:
        for sname in S.cnt:
            sems[sname] = st.enter_context(nc.semaphore("s_" + sname))
        with nc.Block() as block:
            @block.sync
            def _s(sync):
                S.emit("sync", sync, sems)

            @block.tensor
            def _t(tensor):
                S.emit("tensor", tensor, sems)

            @block.vector
            def _v(vector):
                S.emit("vector", vector, sems)

            @block.scalar
            def _c(scalar):
                S.emit("scalar", scalar, sems)

            @block.gpsimd
            def _g(gpsimd):
                S.emit("gpsimd", gpsimd, sems)
    return nc


def _pack_weights(inp):
    g = {}
    Wqkv = np.asarray(inp["L0_attn_a_W_qkv"], np.float32)
    wa = np.empty((16, 128, 1536), np.float32)
    for h in range(16):
        cols = np.concatenate([np.arange(h * 64, h * 64 + 64),
                               1024 + np.arange(h * 64, h * 64 + 64),
                               2048 + np.arange(h * 64, h * 64 + 64)])
        wa[h] = Wqkv[:, cols].reshape(8, 128, 192).transpose(1, 0, 2).reshape(128, 1536)
    g["wa"] = wa.reshape(16 * 128, 1536)

    wbh = np.empty((16, 128, 1536), np.float32)
    WqT = np.ascontiguousarray(np.asarray(inp["L0_attn_b_Wq"], np.float32).T)
    WkT = np.ascontiguousarray(np.asarray(inp["L0_attn_b_Wk"], np.float32).T)
    WvT = np.ascontiguousarray(np.asarray(inp["L0_attn_b_Wv"], np.float32).T)
    for h in range(16):
        Wh = np.concatenate([WqT[:, h * 64:(h + 1) * 64], WkT[:, h * 64:(h + 1) * 64],
                             WvT[:, h * 64:(h + 1) * 64]], axis=1)
        wbh[h] = Wh.reshape(8, 128, 192).transpose(1, 0, 2).reshape(128, 1536)
    g["wb"] = wbh.reshape(16 * 128, 1536)

    g["wouta"] = np.asarray(inp["L0_attn_a_W_out"], np.float32).reshape(8, 128, 1024) \
        .transpose(1, 0, 2).reshape(128, 8192).copy()
    WoT = np.ascontiguousarray(np.asarray(inp["L0_attn_b_Wo"], np.float32).T)
    g["woutb"] = WoT.reshape(8, 128, 1024).transpose(1, 0, 2).reshape(128, 8192).copy()

    def bias48(bq, bk, bv):
        b = np.empty((64, 48), np.float32)
        for h in range(16):
            b[:, h] = bq[h * 64:(h + 1) * 64]
            b[:, 16 + h] = bk[h * 64:(h + 1) * 64]
            b[:, 32 + h] = bv[h * 64:(h + 1) * 64]
        return b
    bq = np.asarray(inp["L0_attn_a_b_qkv"], np.float32)
    g["bqkv_a"] = bias48(bq[0:1024], bq[1024:2048], bq[2048:3072])
    g["bqkv_b"] = bias48(np.asarray(inp["L0_attn_b_bq"], np.float32),
                         np.asarray(inp["L0_attn_b_bk"], np.float32),
                         np.asarray(inp["L0_attn_b_bv"], np.float32))
    g["bout_a"] = np.asarray(inp["L0_attn_a_b_out"], np.float32).reshape(8, 128).T.copy()
    g["bout_b"] = np.asarray(inp["L0_attn_b_bo"], np.float32).reshape(8, 128).T.copy()

    g["wgate"] = np.asarray(inp["L0_ffn_gate_w"], np.float32).T \
        .reshape(8, 128, 8).transpose(1, 0, 2).reshape(128, 64).copy()
    g["wseg"] = np.asarray(inp["L0_ffn_shared_expert_gate_w"], np.float32).T \
        .reshape(8, 128, 1).transpose(1, 0, 2).reshape(128, 8).copy()
    g["wsur"] = np.asarray(inp["L0_attn_a_w_sur"], np.float32) \
        .reshape(8, 128, 1).transpose(1, 0, 2).reshape(128, 8).copy()

    gu = np.asarray(inp["L0_ffn_experts_gate_up"], np.float32)
    dn = np.asarray(inp["L0_ffn_experts_down"], np.float32)
    sg = np.asarray(inp["L0_ffn_shared_gate_w"], np.float32)
    su = np.asarray(inp["L0_ffn_shared_up_w"], np.float32)
    sd = np.asarray(inp["L0_ffn_shared_down_w"], np.float32)
    wguh = np.empty((NE, IC, 128, 2048), np.float32)
    wdnh = np.empty((NE, 8, 128, 1408), np.float32)
    sgT, suT, sdT = sg.T, su.T, sd.T
    for e in range(NE):
        if e < 8:
            WT = gu[e].T
            gcols, ucols = WT[:, 0:1408], WT[:, 1408:2816]
            DT = dn[e].T
        else:
            off = (e - 8) * 1408
            gcols, ucols = sgT[:, off:off + 1408], suT[:, off:off + 1408]
            DT = sdT[off:off + 1408, :]
        for i in range(IC):
            gstrip = gcols[:, i * 128:(i + 1) * 128].reshape(8, 128, 128)
            ustrip = ucols[:, i * 128:(i + 1) * 128].reshape(8, 128, 128)
            wguh[e, i] = np.stack([gstrip, ustrip]).transpose(2, 0, 1, 3) \
                .reshape(128, 2048)
        for m in range(8):
            wdnh[e, m] = DT[:, m * 128:(m + 1) * 128].reshape(IC, 128, 128) \
                .transpose(1, 0, 2).reshape(128, 1408)
    g["wgu"] = wguh.reshape(NE * IC * 128, 2048)
    g["wdn"] = wdnh.reshape(NE * 8 * 128, 1408)

    g["scal"] = np.array([[np.float32(inp["L0_attn_a_b_sur"]),
                           np.float32(inp["L0_attn_a_threshold"]),
                           np.asarray(inp["L0_attn_b_attn_scale"],
                                      np.float32).reshape(-1)[0], 0.0]], np.float32)
    g["ident"] = np.eye(128, dtype=np.float32)
    g["onesr"] = np.ones((1, 128), np.float32)
    g["onesc"] = np.ones((128, 1), np.float32)
    wm = np.zeros((64, 4), np.float32)
    wm[:, 0] = 1.0; wm[0, 0] = -1.0
    wm[:, 3] = 1.0; wm[0, 3] = -1.0
    g["wmink"] = wm
    sT = np.zeros((2, 128), np.float32); sT[0, 0] = 1.0; sT[1, 64] = 1.0
    g["selT"] = sT
    sS = np.zeros((2, 128), np.float32); sS[0, 1:64] = -1.0; sS[1, 65:128] = 1.0
    g["selS"] = sS
    s8 = np.zeros((8, 1024), np.float32)
    for e in range(8):
        s8[e, e * 128:(e + 1) * 128] = 1.0
    g["sel8"] = s8
    i_ = np.arange(128)[:, None]; u_ = np.arange(896)[None, :]
    g["maskA"] = (u_ - i_ >= 384).astype(np.float32)
    mB = np.zeros((128, 128), np.float32)
    ii = np.arange(128)[:, None]; jj = np.arange(128)[None, :]
    vis = ((jj // 64 == ii // 64) & (jj <= ii)) | (jj // 64 > ii // 64)
    mB[~vis] = NEG
    g["maskB"] = mB
    g["epsc"] = np.full((1, 1), 1e-5, np.float32)
    g["negc"] = np.full((128, 1), -1.0, np.float32)
    return g


def kernel(**inputs):
    x = np.asarray(inputs["L0_attn_a_x"], np.float32)
    g = _pack_weights(inputs)
    nc = build_program()
    in_maps = []
    for c in range(8):
        m = dict(g)
        m["x"] = np.ascontiguousarray(x[c].T)
        in_maps.append(m)
    res = run_bass_kernel_spmd(nc, in_maps, core_ids=list(range(8)))
    outs = [np.asarray(res.results[c]["out"]).T for c in range(8)]
    return np.ascontiguousarray(np.stack(outs)).astype(np.float32)


# version 10
# speedup vs baseline: 1.3368x; 1.0026x over previous
import numpy as np
import concourse.bass as bass
import concourse.mybir as mybir
from concourse.bass_utils import run_bass_kernel_spmd

F32 = mybir.dt.float32
BF16 = mybir.dt.bfloat16
AF = mybir.ActivationFunctionType
ALU = mybir.AluOpType
AX = mybir.AxisListType

C = 1024
T = 2048
H = 16
KC = 8
NS = 4
IC = 11
NE = 10
NEG = -1.0e30
ARW = 26768


class Sched:

    def __init__(self):
        self.items = []
        self.cnt = {"syd": 0, "gpd": 0, "pe": 0, "dve": 0, "act": 0, "gpc": 0}
        self.lastw = {}

    def op(self, eng, fn, waits=(), inc=None, amt=1):
        w = []
        for tok in waits:
            if tok is None:
                continue
            s, v = tok
            if v <= 0:
                continue
            assert v <= self.cnt[s], (s, v, self.cnt[s])
            k = (eng, s)
            if self.lastw.get(k, -1) >= v:
                continue
            self.lastw[k] = v
            w.append((s, v))
        tok = None
        if inc is not None:
            self.cnt[inc] += amt
            tok = (inc, self.cnt[inc])
        self.items.append((eng, fn, tuple(w), inc, amt))
        return tok

    def now(self, sem):
        return (sem, self.cnt[sem])

    def barrier(self):
        toks = [self.now(s) for s in self.cnt]
        for eng in ("sync", "tensor", "vector", "scalar", "gpsimd"):
            self.op(eng, lambda e: None, waits=toks)

    def emit(self, eng_name, eng, sems):
        for e, fn, w, inc, amt in self.items:
            if e != eng_name:
                continue
            for s, v in w:
                eng.wait_ge(sems[s], v)
            ins = fn(eng)
            if inc is not None:
                ins.then_inc(sems[inc], amt)


def build_program(stage=99):
    nc = bass.Bass("TRN2", target_bir_lowering=False)
    S = Sched()
    now = S.now

    dr = {}
    def din(name, shape):
        dr[name] = nc.dram_tensor(name, list(shape), F32, kind="ExternalInput").ap()
        return dr[name]

    x_in = din("x", (C, T))
    wa = din("wa", (16 * 128, 1536))
    wb = din("wb", (16 * 128, 1536))
    wouta = din("wouta", (128, 8192))
    woutb = din("woutb", (128, 8192))
    bqkv_a = din("bqkv_a", (64, 48))
    bqkv_b = din("bqkv_b", (64, 48))
    bout_a = din("bout_a", (128, 8))
    bout_b = din("bout_b", (128, 8))
    wgate = din("wgate", (128, 64))
    wseg = din("wseg", (128, 8))
    wsur = din("wsur", (128, 8))
    wgu = din("wgu", (NE * IC * 128, 2048))
    wdn = din("wdn", (NE * 8 * 128, 1408))
    scal = din("scal", (1, 4))
    identD = din("ident", (128, 128))
    onesrD = din("onesr", (1, 128))
    onescD = din("onesc", (128, 1))
    wminkD = din("wmink", (64, 4))
    selTD = din("selT", (2, 128))
    selSD = din("selS", (2, 128))
    sel8D = din("sel8", (8, 1024))
    maskAD = din("maskA", (128, 896))
    maskBD = din("maskB", (128, 128))
    epscD = din("epsc", (1, 1))
    negcD = din("negc", (128, 1))

    x_hbm = nc.dram_tensor("x_hbm", [C, T], F32).ap()
    y_hbm = nc.dram_tensor("y_hbm", [C, T], F32).ap()
    out = nc.dram_tensor("out", [C, T], F32, kind="ExternalOutput").ap()

    xnh = nc.alloc_sbuf_tensor("xnh", [128, KC * T], BF16).ap()
    xnl = nc.alloc_sbuf_tensor("xnl", [128, KC * T], BF16).ap()
    AR = nc.alloc_sbuf_tensor("arena", [128, ARW], F32).ap()
    CO = {}
    for name, shape in [("ident", (128, 128)), ("onesr", (1, 128)),
                        ("onesc", (128, 1)),
                        ("bqkv_a", (64, 48)), ("bqkv_b", (64, 48)),
                        ("bout_a", (128, 8)), ("bout_b", (128, 8)),
                        ("wgate", (128, 64)), ("wseg", (128, 8)),
                        ("wsur", (128, 8)), ("scal", (1, 4)),
                        ("rows", (1, 8)),
                        ("wfm", (8, T)), ("swrow", (1, T)), ("spike", (1, T)),
                        ("scalecol", (128, 1)), ("thr", (1, 1)),
                        ("sel8", (8, 1024)), ("epsc", (1, 1)),
                        ("negc", (128, 1)), ("ltm", (128, 128)),
                        ("wtm", (128, 128)), ("ebf", (128, 128)),
                        ("s2f", (128, 128)), ("mkf", (128, 128)),
                        ("b2f", (128, 128)), ("mx16", (128, 16)),
                        ("rc16", (128, 16)), ("mxs", (128, 16)),
                        ("rcs", (128, 16)), ("selT", (2, 128)),
                        ("selS", (2, 128)), ("wmink", (64, 4))]:
        CO[name] = nc.alloc_sbuf_tensor("c_" + name, [shape[0], shape[1]], F32).ap()
    for name, shape in [("wgateh", (128, 64)), ("wsegh", (128, 8)),
                        ("wsurh", (128, 8)), ("wgatel", (128, 64)),
                        ("wsegl", (128, 8)), ("wsurl", (128, 8))]:
        CO[name] = nc.alloc_sbuf_tensor("c_" + name, [shape[0], shape[1]], BF16).ap()
    PS = [nc.alloc_psum_tensor("ps%d" % i, [128, 512], F32).ap() for i in range(8)]
    BF = {i: None for i in range(8)}
    ident = CO["ident"]

    class Arena:
        def __init__(self):
            self.off = 0
        def take(self, parts, width):
            ap = AR[0:parts, self.off:self.off + width]
            self.off += width
            assert self.off <= ARW, self.off
            return ap

    def dma(q, dst, src, waits=(), accum=False):
        sem = "syd" if q == "sync" else "gpd"
        if accum:
            fn = lambda e, d=dst, s=src: e.dma_start(out=d, in_=s, accum_op=ALU.add)
        else:
            fn = lambda e, d=dst, s=src: e.dma_start(out=d, in_=s)
        return S.op(q, fn, waits, inc=sem, amt=16)

    def mm1(dst, lhsT, rhs, start, stop, waits=(), inc=False):
        fn = lambda e, d=dst, l=lhsT, r=rhs, st=start, sp=stop: \
            e.matmul(d, l, r, start=st, stop=sp)
        return S.op("tensor", fn, waits, inc="pe" if inc else None)

    def mmc(bank, pairs, waits=(), rows=128, n=512, col0=0):
        ap = PS[bank][0:rows, col0:col0 + n]
        t = None
        for i, (l, r) in enumerate(pairs):
            w = (list(waits) + [BF[bank]]) if i == 0 else []
            t = mm1(ap, l, r, i == 0, i == len(pairs) - 1, waits=w,
                    inc=(i == len(pairs) - 1))
        return t

    def mmc3(bank, quads, waits=(), rows=128, n=512):
        ap = PS[bank][0:rows, 0:n]
        total = len(quads) * 3
        j = 0
        t = None
        for lh, ll, rh, rl in quads:
            for l, r_ in ((lh, rh), (lh, rl), (ll, rh)):
                w = (list(waits) + [BF[bank]]) if j == 0 else []
                t = mm1(ap, l, r_, j == 0, j == total - 1, waits=w,
                        inc=(j == total - 1))
                j += 1
        return t

    def trc(bank, lst, waits=()):
        t = None
        for i, (d, s_, idn) in enumerate(lst):
            w = (list(waits) + [BF[bank]]) if i == 0 else []
            fn = lambda e, dd=d, ss=s_, ii=idn, st=(i == 0), sp=(i == len(lst) - 1): \
                e.matmul(dd, ss, ii, start=st, stop=sp, is_transpose=True)
            t = S.op("tensor", fn, waits=w, inc="pe" if i == len(lst) - 1 else None)
        return t

    def rd(bank, tok):
        BF[bank] = tok
        return tok

    def dve(meth, *args, waits=(), **kw):
        fn = lambda e, a=args, k=kw, m=meth: getattr(e, m)(*a, **k)
        return S.op("vector", fn, waits, inc="dve")

    def act(out_ap, in_ap, func, waits=(), **kw):
        fn = lambda e, o=out_ap, i=in_ap, f=func, k=kw: e.activation(o, i, f, **k)
        return S.op("scalar", fn, waits, inc="act")

    def fence_dve():
        return S.op("vector", lambda e: e.drain(fusable=False), inc="dve")

    def prologue():
        toks = []
        for name, d in [("ident", identD), ("onesr", onesrD), ("onesc", onescD),
                        ("bqkv_a", bqkv_a), ("bqkv_b", bqkv_b),
                        ("bout_a", bout_a), ("bout_b", bout_b),
                        ("wgate", wgate), ("wseg", wseg), ("wsur", wsur),
                        ("scal", scal), ("sel8", sel8D),
                        ("epsc", epscD), ("negc", negcD),
                        ("selT", selTD), ("selS", selSD),
                        ("wmink", wminkD)]:
            toks.append(dma("sync", CO[name], d))
        LD = now("syd")
        th = CO["scal"][0:1, 1:2]
        r = CO["rows"]
        act(r[:, 0:1], th, AF.Ln, waits=[LD])
        dve("tensor_scalar", r[:, 2:3], th, -1.0, 1.0, ALU.mult, ALU.add, waits=[LD])
        act(r[:, 4:5], r[:, 2:3], AF.Ln, waits=[now("dve")])
        dve("tensor_sub", CO["thr"], r[:, 0:1], r[:, 4:5], waits=[now("act")])
        t = mmc(7, [(CO["onesr"], CO["scal"][0:1, 2:3])], waits=[LD], n=1)
        rd(7, dve("tensor_copy", CO["scalecol"], PS[7][0:128, 0:1], waits=[t]))
        for nm in ("wgate", "wseg", "wsur"):
            t = dve("tensor_copy", CO[nm + "h"], CO[nm], waits=[LD])
            dve("tensor_sub", CO[nm + "l"], CO[nm], CO[nm + "h"], waits=[t])
        return LD

    def layer_norm(src_dram, ar, routing=False, spiking=False, dbg=None):
        S.barrier()
        xch = [ar.take(128, T), ar.take(128, T)]
        xsq = [ar.take(128, T), ar.take(128, T)]
        rstd_bc = ar.take(128, T)
        mr_bc = ar.take(128, T)
        rowA = ar.take(1, T)
        rowB = ar.take(1, T)
        rowC = ar.take(1, T)
        rowD = ar.take(1, T)
        rowE = ar.take(1, T)

        ch_tok = [None, None]
        sq_tok = [None, None]
        for kc in range(KC):
            b = kc % 2
            ld = dma("sync", xch[b], src_dram[kc * 128:(kc + 1) * 128, :],
                     waits=[ch_tok[b]])
            sq_tok[b] = dve("scalar_tensor_tensor", xsq[b], xch[b], 1.0, xch[b],
                            ALU.mult, ALU.mult, waits=[ld, sq_tok[b]])
            for n in range(NS):
                mm1(PS[n][0:1, :], CO["onesc"], xch[b][:, n * 512:(n + 1) * 512],
                    kc == 0, kc == KC - 1,
                    waits=([ld, BF[n]] if kc == 0 else [ld]), inc=True)
                mm1(PS[4 + n][0:1, :], CO["onesc"], xsq[b][:, n * 512:(n + 1) * 512],
                    kc == 0, kc == KC - 1,
                    waits=([sq_tok[b], BF[4 + n]] if kc == 0 else [sq_tok[b]]),
                    inc=True)
            ch_tok[b] = now("pe")
        STATS = now("pe")
        for n in range(NS):
            rd(n, dve("tensor_copy", rowA[:, n * 512:(n + 1) * 512], PS[n][0:1, :],
                      waits=[STATS]))
            rd(4 + n, dve("tensor_copy", rowB[:, n * 512:(n + 1) * 512],
                          PS[4 + n][0:1, :], waits=[STATS]))
        dve("tensor_scalar_mul", rowC, rowA, 1.0 / C)
        dve("scalar_tensor_tensor", rowA, rowC, 1.0, rowC, ALU.mult, ALU.mult)
        dve("scalar_tensor_tensor", rowB, rowB, 1.0 / C, rowA, ALU.mult, ALU.subtract)
        act(rowD, rowB, AF.Ln, bias=CO["epsc"], waits=[now("dve")])
        act(rowD, rowD, AF.Exp, scale=-0.5)
        RSTD = now("act")
        dve("tensor_mul", rowE, rowC, rowD, waits=[RSTD])
        ROWS = now("dve")
        for n in range(NS):
            sl = slice(n * 512, (n + 1) * 512)
            t = mmc(0, [(CO["onesr"], rowD[:, sl])], waits=[ROWS])
            rd(0, dve("tensor_copy", rstd_bc[:, sl], PS[0], waits=[t]))
            t = mmc(1, [(CO["onesr"], rowE[:, sl])], waits=[ROWS])
            rd(1, dve("tensor_copy", mr_bc[:, sl], PS[1], waits=[t]))
        BC = now("dve")
        for kc in range(KC):
            b = kc % 2
            ld = dma("sync", xch[b], src_dram[kc * 128:(kc + 1) * 128, :],
                     waits=[ch_tok[b]])
            dve("scalar_tensor_tensor", xsq[b], xch[b], 1.0, rstd_bc,
                ALU.mult, ALU.mult, waits=[ld, BC, sq_tok[b]])
            sq_tok[b] = None
            dve("tensor_sub", xch[b], xsq[b], mr_bc)
            dve("tensor_copy", xnh[:, kc * T:(kc + 1) * T], xch[b])
            ch_tok[b] = dve("tensor_sub", xnl[:, kc * T:(kc + 1) * T], xch[b],
                            xnh[:, kc * T:(kc + 1) * T])
        XN = now("dve")

        if spiking:
            for n in range(NS):
                t = mmc3(2, [(CO["wsurh"][:, kc:kc + 1], CO["wsurl"][:, kc:kc + 1],
                              xnh[:, kc * T + n * 512: kc * T + (n + 1) * 512],
                              xnl[:, kc * T + n * 512: kc * T + (n + 1) * 512])
                             for kc in range(KC)], waits=[XN], rows=1)
                dve("tensor_scalar_add", rowA[:, n * 512:(n + 1) * 512],
                    PS[2][0:1, :], CO["scal"][0:1, 0:1], waits=[t])
                rd(2, dve("tensor_single_scalar",
                          CO["spike"][:, n * 512:(n + 1) * 512],
                          rowA[:, n * 512:(n + 1) * 512], CO["thr"], ALU.is_gt))
            if dbg is not None:
                dbg.append(("r1", rowA, [now("dve")]))
                dbg.append(("wsurh", CO["wsurh"], [now("dve")]))
                dbg.append(("wsurl", CO["wsurl"], [now("dve")]))
                dbg.append(("thr", CO["thr"], [now("dve")]))

        RT = None
        if routing:
            lgfm = ar.take(8, T)
            ltm, wtm, ebf, s2f = CO["ltm"], CO["wtm"], CO["ebf"], CO["s2f"]
            mkf, b2f, mx16, rc16 = CO["mkf"], CO["b2f"], CO["mx16"], CO["rc16"]
            for n in range(NS):
                sl2 = slice(n * 512, (n + 1) * 512)
                t = mmc3(2, [(CO["wgateh"][:, kc * 8:(kc + 1) * 8],
                              CO["wgatel"][:, kc * 8:(kc + 1) * 8],
                              xnh[:, kc * T + n * 512: kc * T + (n + 1) * 512],
                              xnl[:, kc * T + n * 512: kc * T + (n + 1) * 512])
                             for kc in range(KC)], waits=[XN], rows=8)
                rd(2, dve("tensor_copy", lgfm[:, sl2], PS[2][0:8, :], waits=[t]))
                t = mmc3(3, [(CO["wsegh"][:, kc:kc + 1], CO["wsegl"][:, kc:kc + 1],
                              xnh[:, kc * T + n * 512: kc * T + (n + 1) * 512],
                              xnl[:, kc * T + n * 512: kc * T + (n + 1) * 512])
                             for kc in range(KC)], waits=[XN], rows=1)
                rd(3, act(CO["swrow"][:, sl2], PS[3][0:1, :], AF.Sigmoid, waits=[t]))
            LG = now("dve")
            for tc in range(16):
                t = trc(4, [(PS[4][:, 0:8], lgfm[:, tc * 128:(tc + 1) * 128],
                             ident[0:8, 0:8])], waits=[LG])
                rd(4, dve("tensor_copy", ltm[:, tc * 8:(tc + 1) * 8], PS[4][:, 0:8],
                          waits=[t]))
            LTM = fence_dve()
            seg = lambda ap, e: ap.rearrange("p (s e) -> p s e", e=e)
            b1 = lambda ap: seg(ap, 1).broadcast_to((128, 16, 8))
            a = act(ebf, ltm, AF.Exp, waits=[LTM])
            dve("tensor_reduce", seg(mx16, 1), seg(ebf, 8), AX.X, ALU.add,
                waits=[a])
            aL = act(CO["rcs"], mx16, AF.Ln, waits=[now("dve")])
            aE = act(rc16, CO["rcs"], AF.Exp, scale=-1.0)
            dve("tensor_tensor", seg(s2f, 8), seg(ebf, 8), b1(rc16), ALU.mult,
                waits=[aE])
            r2 = dve("tensor_reduce", seg(mx16, 1), seg(s2f, 8), AX.X, ALU.max)
            dve("tensor_tensor", seg(mkf, 8), seg(s2f, 8), b1(mx16),
                ALU.subtract, waits=[r2])
            dve("tensor_single_scalar", mkf, mkf, 0.0, ALU.is_ge)
            dve("scalar_tensor_tensor", b2f, mkf, NEG, s2f, ALU.mult, ALU.add)
            r3 = dve("tensor_reduce", seg(mx16, 1), seg(b2f, 8), AX.X, ALU.max)
            dve("tensor_tensor", seg(mkf, 8), seg(s2f, 8), b1(mx16),
                ALU.subtract, waits=[r3])
            dve("tensor_single_scalar", mkf, mkf, 0.0, ALU.is_ge)
            dve("tensor_mul", wtm, mkf, s2f)
            W = fence_dve()
            for tc in range(16):
                t = trc(5, [(PS[5][0:8, 0:128], wtm[:, tc * 8:(tc + 1) * 8],
                             ident)], waits=[W])
                rd(5, dve("tensor_copy", CO["wfm"][:, tc * 128:(tc + 1) * 128],
                          PS[5][0:8, 0:128], waits=[t]))
            RT = (now("dve"), now("act"))
            if dbg is not None:
                dbg.append(("lgfm", lgfm, [now("dve")]))
                dbg.append(("ltm", ltm, [now("dve")]))
                dbg.append(("wtm", wtm, [now("dve")]))
                dbg.append(("ebf", ebf, [now("dve")]))
                dbg.append(("s2f", s2f, [now("dve")]))
                dbg.append(("mkf", mkf, [now("dve")]))
                dbg.append(("b2f", b2f, [now("dve")]))
                dbg.append(("mx16", mx16, [now("dve")]))
                dbg.append(("rc16", rc16, [now("dve")]))
                dbg.append(("mxs", CO["mxs"], [now("dve")]))
                dbg.append(("rcs", CO["rcs"], [now("dve")]))
        return XN, RT

    def attn_a(ar, XNt, dbg=None):
        S.barrier()
        qh = ar.take(64, T)
        kh = ar.take(64, T)
        vh = ar.take(64, T)
        Qp = ar.take(64, T)
        Kp = ar.take(64, T)
        time_sb = ar.take(2, 512)
        srow_sb = ar.take(2, 512)
        vaug = ar.take(128, 16 * 65)
        wslf = ar.take(128, 1536)
        wsh = [ar.take(128, 768).bitcast(BF16) for _ in range(2)]
        wsl_ = [ar.take(128, 768).bitcast(BF16) for _ in range(2)]
        NCS = 3
        CH = [{n: ar.take(128, 512) for n in ("B1", "B2", "B3", "pt")}
              for _ in range(NCS)]
        for cs in range(NCS):
            CH[cs]["ptm"] = CH[cs]["B3"]
        ysb = [AR[64:128, 0:512], AR[64:128, 512:1024]]
        czrow = ar.take(1, 512)
        rrow = ar.take(1, 512)
        t1row = ar.take(1, 512)
        maskAW = ar.take(128, 896)
        erow = [CH[0]["B1"][0:2, 0:512], CH[0]["B2"][0:2, 0:512],
                CH[0]["B3"][0:2, 0:512], CH[1]["B1"][0:2, 0:512],
                CH[1]["B2"][0:2, 0:512]]
        selT, selS, wmink = CO["selT"], CO["selS"], CO["wmink"]

        dma("sync", maskAW, maskAD)
        CLD = now("syd")
        VA = dve("memset", vaug, 1.0)
        SB = (5, 6, 0)

        wuse = [None, None]
        spl_tok = None
        ytok = [None, None]
        set_tok = [None] * NCS
        pv_use = [None] * NCS
        for h in range(H):
            b = h % 2
            WL = dma("sync", wslf, wa[h * 128:(h + 1) * 128, :], waits=[spl_tok])
            dve("tensor_copy", wsh[b], wslf, waits=[WL, wuse[b]])
            dve("tensor_sub", wsl_[b], wslf, wsh[b])
            dve("tensor_copy", CO["mxs"], CO["mx16"])
            spl_tok = dve("tensor_copy", CO["mxs"], CO["mx16"])
            dsts = {0: qh, 1: kh, 2: vh}
            t = None
            for part in range(3):
                for n in range(NS):
                    bank = 1 + part % 2
                    t = mmc3(bank,
                             [(wsh[b][:, kc * 192 + part * 64: kc * 192 + (part + 1) * 64],
                               wsl_[b][:, kc * 192 + part * 64: kc * 192 + (part + 1) * 64],
                               xnh[:, kc * T + n * 512: kc * T + (n + 1) * 512],
                               xnl[:, kc * T + n * 512: kc * T + (n + 1) * 512])
                              for kc in range(KC)],
                            waits=[spl_tok, XNt], rows=64)
                    rd(bank, dve("tensor_scalar_add",
                                 dsts[part][:, n * 512:(n + 1) * 512], PS[bank][0:64, :],
                                 CO["bqkv_a"][:, part * 16 + h: part * 16 + h + 1],
                                 waits=[t]))
            wuse[b] = t
            PROJ = now("dve")
            if dbg is not None and h == 15:
                dbg.append(("xnh2", xnh, [PROJ]))
                dbg.append(("xnl2", xnl, [PROJ]))
                dbg.append(("wslf", wslf, [PROJ]))
                dbg.append(("wsh", wsh[b], [PROJ]))
                dbg.append(("wsl_", wsl_[b], [PROJ]))
                dbg.append(("qh15", qh, [PROJ]))
            dve("scalar_tensor_tensor", Qp, qh, 1.0, qh, ALU.mult, ALU.mult,
                waits=[tok for tok in set_tok])
            dve("scalar_tensor_tensor", Kp, kh, 1.0, kh, ALU.mult, ALU.mult)
            USQ = now("dve")
            dve("memset", qh[0:1, :], 1.0)
            dve("memset", kh[0:1, :], 1.0)
            for n in range(NS):
                sl = slice(n * 512, (n + 1) * 512)
                t = mmc(2, [(wmink[:, 0:2], Qp[:, sl]), (wmink[:, 2:4], Kp[:, sl])],
                        waits=[USQ], rows=2)
                eA, eB, eC, eD, eE = erow
                d1 = dve("tensor_scalar_max", eA, PS[2][0:2, :], 1e-8, waits=[t])
                rd(2, d1)
                act(eB, eA, AF.Ln, waits=[d1])
                act(eC, eB, AF.Exp, scale=0.5)
                act(eD, eC, AF.Exp)
                act(eE, eC, AF.Exp, scale=-1.0)
                a5 = act(eC, eB, AF.Exp, scale=-0.5)
                dve("tensor_add", eA, eD, eE, waits=[a5])
                dve("tensor_scalar_mul", time_sb, eA, 0.5)
                dve("tensor_sub", eA, eD, eE)
                EXPR = dve("scalar_tensor_tensor", srow_sb, eA, 0.5, eC,
                           ALU.mult, ALU.mult)
                t = mmc(3, [(selT[:, 0:64], time_sb),
                            (selS[:, 0:64], srow_sb)],
                        waits=[EXPR], rows=64)
                rd(3, dve("tensor_mul", Qp[:, sl], PS[3][0:64, :], qh[:, sl],
                          waits=[t]))
                t = mmc(4, [(selT[:, 64:128], time_sb),
                            (selS[:, 64:128], srow_sb)],
                        waits=[EXPR], rows=64)
                rd(4, dve("tensor_mul", Kp[:, sl], PS[4][0:64, :], kh[:, sl],
                          waits=[t]))
            QPKP = now("dve")
            for g in range(4):
                bank = 3 + g % 2
                t = trc(bank,
                        [(PS[bank][:, j * 64:(j + 1) * 64],
                          vh[:, (g * 4 + j) * 128:(g * 4 + j + 1) * 128],
                          ident[0:64, 0:64]) for j in range(4)],
                        waits=[PROJ, VA])
                dstv = vaug[:, g * 260:(g + 1) * 260] \
                    .rearrange("p (s e) -> p s e", e=65)[:, :, 0:64]
                srcv = PS[bank][:, 0:256].rearrange("p (s e) -> p s e", e=64)
                rd(bank, dve("tensor_copy", dstv, srcv, waits=[t]))
            VAUG = now("dve")
            for s in range(NS):
                qo = s * 512
                nkc = 4 * (s + 1)
                pv_tok = None
                stok = {}
                def emit_score(kc, qo=qo):
                    bank = SB[kc % 3]
                    stok[kc] = (bank, mmc(bank,
                        [(Kp[:, kc * 128:(kc + 1) * 128], Qp[:, qo:qo + 512])],
                        waits=[QPKP, VAUG]))
                emit_score(0)
                if nkc > 1:
                    emit_score(1)
                for kc in range(nkc):
                    if kc + 2 < nkc:
                        emit_score(kc + 2)
                    cs = kc % NCS
                    bank, t = stok.pop(kc)
                    B1, B2, B3 = CH[cs]["B1"], CH[cs]["B2"], CH[cs]["B3"]
                    pt, ptm = CH[cs]["pt"], CH[cs]["ptm"]
                    d1 = dve("tensor_scalar_max", B1, PS[bank], 1.0000001,
                             waits=[t, set_tok[cs]])
                    rd(bank, d1)
                    d2 = dve("scalar_tensor_tensor", B2, B1, 1.0, B1, ALU.mult, ALU.mult)
                    a1 = act(B3, B2, AF.Ln, bias=CO["negc"], waits=[d2])
                    a2 = act(B2, B3, AF.Exp, scale=0.5)
                    d3 = dve("tensor_add", B3, B1, B2, waits=[a2])
                    a3 = act(B1, B3, AF.Ln, waits=[d3])
                    d4 = dve("scalar_tensor_tensor", B2, B1, 1.0, B1,
                             ALU.mult, ALU.mult, waits=[a3])
                    a4 = act(pt, B2, AF.Exp, scale=-0.125,
                             waits=[d4, pv_use[cs]])
                    set_tok[cs] = a4
                    fin = a4
                    src = pt
                    if kc >= 4 * s:
                        v = kc - 4 * s
                        fin = dve("tensor_mul", ptm, pt,
                                  maskAW[:, 384 - 128 * v: 896 - 128 * v],
                                  waits=[a4, CLD, pv_use[cs]])
                        src = ptm
                    w = [fin]
                    if kc == 0:
                        w.append(BF[7])
                    pv_tok = mm1(PS[7][0:65, :], vaug[:, kc * 65:(kc + 1) * 65], src,
                                 kc == 0, kc == nkc - 1, waits=w,
                                 inc=(kc == nkc - 1))
                    if kc == nkc - 1:
                        for j in range(NCS):
                            pv_use[j] = pv_tok
                yb = ysb[s % 2]
                aL = act(t1row, PS[7][64:65, :], AF.Ln, waits=[pv_tok])
                aE = act(rrow, t1row, AF.Exp, scale=-1.0)
                CS = dve("tensor_mul", czrow, rrow, CO["spike"][:, qo:qo + 512],
                         waits=[aE])
                t = mmc(3, [(CO["onesr"][:, 0:64], czrow)], waits=[CS], rows=64)
                cp = dve("tensor_copy", yb, PS[7][0:64, :], waits=[t, ytok[s % 2]])
                rd(7, cp)
                yev = dve("tensor_mul", yb, yb, PS[3][0:64, :])
                rd(3, yev)
                ytok[s % 2] = dma("gpsimd", y_hbm[h * 64:(h + 1) * 64, qo:qo + 512],
                                  yb, waits=[yev])
        return now("gpd")

    def attn_b(ar, XNt, dbg=None):
        S.barrier()
        qh = ar.take(64, T)
        kh = ar.take(64, T)
        vh = ar.take(64, T)
        vtm = ar.take(128, 1024)
        wslf = ar.take(128, 1536)
        wsh = [ar.take(128, 768).bitcast(BF16) for _ in range(2)]
        wsl_ = [ar.take(128, 768).bitcast(BF16) for _ in range(2)]
        Srow2 = [ar.take(128, T), ar.take(128, T)]
        pnA = [ar.take(128, T) for _ in range(4)]
        ptb = [ar.take(128, 512), ar.take(128, 512)]
        ysb = [ar.take(64, 512), ar.take(64, 512)]
        mcol = ar.take(128, 16)
        maskB = ar.take(128, 128)
        MB = dma("sync", maskB, maskBD)

        wuse = [None, None]
        spl_tok = None
        ytok = [None, None]
        pt_free = [None, None]
        pn_free = [None] * 4
        sr_free = [None, None]
        yev = None
        for h in range(H):
            b = h % 2
            WL = dma("sync", wslf, wb[h * 128:(h + 1) * 128, :], waits=[spl_tok])
            dve("tensor_copy", wsh[b], wslf, waits=[WL, wuse[b]])
            spl_tok = dve("tensor_sub", wsl_[b], wslf, wsh[b])
            dsts = {0: qh, 1: kh, 2: vh}
            t = None
            for part in range(3):
                for n in range(NS):
                    bank = part % 2
                    t = mmc3(bank,
                             [(wsh[b][:, kc * 192 + part * 64: kc * 192 + (part + 1) * 64],
                               wsl_[b][:, kc * 192 + part * 64: kc * 192 + (part + 1) * 64],
                               xnh[:, kc * T + n * 512: kc * T + (n + 1) * 512],
                               xnl[:, kc * T + n * 512: kc * T + (n + 1) * 512])
                              for kc in range(KC)],
                            waits=[spl_tok, XNt], rows=64)
                    rd(bank, dve("tensor_scalar_add",
                                 dsts[part][:, n * 512:(n + 1) * 512], PS[bank][0:64, :],
                                 CO["bqkv_b"][:, part * 16 + h: part * 16 + h + 1],
                                 waits=[t]))
            wuse[b] = t
            PROJ = now("dve")
            for tc in range(16):
                t = trc(5, [(PS[5][:, 0:64], vh[:, tc * 128:(tc + 1) * 128],
                             ident[0:64, 0:64])], waits=[PROJ])
                rd(5, dve("tensor_copy", vtm[:, tc * 64:(tc + 1) * 64],
                          PS[5][:, 0:64], waits=[t]))
            VTM = now("dve")
            for g in range(4):
                for qt in range(4):
                    Q = g * 4 + qt
                    qo = Q * 128
                    L = T - qo
                    lead = qo - g * 512
                    pnq = pnA[qt]
                    qb = qt % 2
                    Srow = Srow2[qb]
                    mc = mcol[:, 8 * qb: 8 * qb + 8]
                    t = mmc(6 if qb == 0 else 2,
                            [(qh[:, qo:qo + 128], kh[:, qo:qo + 128])],
                            waits=[PROJ], n=128)
                    bnk = 6 if qb == 0 else 2
                    rd(bnk, dve("scalar_tensor_tensor", Srow[:, 0:128],
                                PS[bnk][:, 0:128], 1.0, maskB, ALU.mult, ALU.add,
                                waits=[t, MB, sr_free[qb]]))
                    off = 128
                    pidx = 1
                    while qo + off < T:
                        ln_ = min(512, T - qo - off)
                        bank = (6, 5)[pidx % 2] if qb == 0 else (2, 1)[pidx % 2]
                        t = mmc(bank, [(qh[:, qo:qo + 128],
                                        kh[:, qo + off:qo + off + ln_])],
                                waits=[PROJ], n=ln_)
                        rd(bank, dve("tensor_copy", Srow[:, off:off + ln_],
                                     PS[bank][:, 0:ln_], waits=[t]))
                        off += ln_
                        pidx += 1
                    SR = now("dve")
                    rmx = dve("reduce_max", mc[:, 0:1], Srow[:, 0:L], AX.X,
                              waits=[SR])
                    BC = dve("scalar_tensor_tensor", mc[:, 3:4], mc[:, 0:1],
                             -1.0, CO["scalecol"], ALU.mult, ALU.mult,
                             waits=[rmx])
                    pw = [BC, pn_free[qt]]
                    if lead > 0:
                        pw = [dve("memset", pnq[:, 0:lead], 0.0, waits=pw)]
                    sr_free[qb] = act(pnq[:, lead:lead + L], Srow[:, 0:L], AF.Exp,
                                      scale=CO["scalecol"], bias=mc[:, 3:4],
                                      waits=pw)
                    dve("tensor_reduce", mc[:, 4:5], pnq[:, lead:lead + L],
                        AX.X, ALU.add, waits=[sr_free[qb]])
                    aL = act(mc[:, 2:3], mc[:, 4:5], AF.Ln,
                             waits=[now("dve")])
                    aE = act(mc[:, 6:7], mc[:, 2:3], AF.Exp, scale=-1.0)
                    dve("tensor_mul", pnq[:, lead:lead + L],
                        pnq[:, lead:lead + L],
                        mc[:, 6:7].broadcast_to((128, L)), waits=[aE])
                PN = now("dve")
                Lg = T - g * 512
                nkc = Lg // 128
                pv_tok = None
                for kc in range(nkc):
                    cs = kc % 2
                    bank = 3 + cs
                    t = trc(bank, [(PS[bank][:, qt * 128:(qt + 1) * 128],
                                    pnA[qt][:, kc * 128:(kc + 1) * 128], ident)
                                   for qt in range(4)],
                            waits=[PN, pt_free[cs]])
                    ptile = ptb[cs]
                    ptr = dve("tensor_copy", ptile, PS[bank], waits=[t, pt_free[cs]])
                    rd(bank, ptr)
                    pt_free[cs] = ptr
                    w = [ptr, VTM]
                    if kc == 0:
                        w.append(BF[7])
                    pv_tok = mm1(PS[7][0:64, :],
                                 vtm[:, (g * 4 + kc) * 64:(g * 4 + kc + 1) * 64],
                                 ptile, kc == 0, kc == nkc - 1, waits=w,
                                 inc=(kc == nkc - 1))
                    pt_free[cs] = pv_tok if kc >= nkc - 2 else pt_free[cs]
                for qt in range(4):
                    pn_free[qt] = pv_tok
                yb = ysb[g % 2]
                yev = dve("tensor_copy", yb, PS[7][0:64, :],
                          waits=[pv_tok, ytok[g % 2]])
                rd(7, yev)
                ytok[g % 2] = dma("gpsimd",
                                  y_hbm[h * 64:(h + 1) * 64, g * 512:(g + 1) * 512],
                                  yb, waits=[yev])
        return now("gpd")

    def out_proj(ar, wdram, bias_co, src_x, dst, y_done):
        S.barrier()
        wsb = ar.take(128, 8192)
        yc = ar.take(128, 4096)
        xp = [ar.take(128, 512), ar.take(128, 512)]
        ob = [ar.take(128, 512), ar.take(128, 512)]
        WL = dma("sync", wsb, wdram)
        xp_tok = [None, None]
        ob_tok = [None, None]
        yc_free = None
        for n in range(NS):
            yld = []
            for kc in range(KC):
                yld.append(dma("sync", yc[:, kc * 512:(kc + 1) * 512],
                               y_hbm[kc * 128:(kc + 1) * 128, n * 512:(n + 1) * 512],
                               waits=[y_done, yc_free]))
            for m in range(KC):
                bank = m % 2
                t = mmc(bank, [(wsb[:, kc * 1024 + m * 128: kc * 1024 + (m + 1) * 128],
                                yc[:, kc * 512:(kc + 1) * 512]) for kc in range(KC)],
                        waits=[WL] + yld)
                if m == KC - 1:
                    yc_free = t
                XL = dma("sync", xp[bank],
                         src_x[m * 128:(m + 1) * 128, n * 512:(n + 1) * 512],
                         waits=[xp_tok[bank]])
                d1 = dve("scalar_tensor_tensor", ob[bank], PS[bank], 1.0, xp[bank],
                         ALU.mult, ALU.add, waits=[t, XL, ob_tok[bank]])
                rd(bank, d1)
                d2 = dve("tensor_scalar_add", ob[bank], ob[bank], bias_co[:, m:m + 1])
                xp_tok[bank] = d2
                ob_tok[bank] = dma("gpsimd",
                                   dst[m * 128:(m + 1) * 128, n * 512:(n + 1) * 512],
                                   ob[bank], waits=[d2])
        return now("gpd")

    def ffn(ar, XNt, RT, dst_x, seed_tok):
        S.barrier()
        hwh = ar.take(128, IC * 512).bitcast(BF16)
        hwl = ar.take(128, IC * 512).bitcast(BF16)
        gsl = [ar.take(128, 2048), ar.take(128, 2048)]
        gsh = [ar.take(128, 1024).bitcast(BF16) for _ in range(2)]
        gsll = [ar.take(128, 1024).bitcast(BF16) for _ in range(2)]
        dslf = ar.take(128, 1408)
        dsh = [ar.take(128, 704).bitcast(BF16) for _ in range(2)]
        dsll = [ar.take(128, 704).bitcast(BF16) for _ in range(2)]
        wbc = ar.take(128, 1024)
        c1 = ar.take(128, 512)
        c2 = ar.take(128, 512)
        c2b = c1
        c3 = [ar.take(128, 512), ar.take(128, 512)]
        rt_dve, rt_act = RT
        gtok = [None, None]
        guse = [None, None]
        dtok = [None, None]
        duse = [None, None]
        ctok = [None, None]
        hw_rd = None
        wbc_rd = None
        prev_d = None
        for half in range(2):
            for e in range(NE):
                for n2 in range(2):
                    nb = half * 1024 + n2 * 512
                    if e < 8:
                        t = mmc(4, [(CO["sel8"][:, e * 128:(e + 1) * 128],
                                     CO["wfm"][:, nb:nb + 512])],
                                waits=[rt_dve])
                    else:
                        t = mmc(4, [(CO["onesr"], CO["swrow"][:, nb:nb + 512])],
                                waits=[rt_act])
                    rd(4, dve("tensor_copy", wbc[:, n2 * 512:(n2 + 1) * 512],
                              PS[4], waits=[t, wbc_rd]))
                WBC = now("dve")
                spl = {}
                def gu_load(i):
                    b = i % 2
                    GL = dma("sync", gsl[b],
                             wgu[(e * IC + i) * 128:(e * IC + i + 1) * 128, :],
                             waits=[gtok[b]])
                    dve("tensor_copy", gsh[b], gsl[b], waits=[GL, guse[b]])
                    gtok[b] = dve("tensor_sub", gsll[b], gsl[b], gsh[b])
                    spl[i] = gtok[b]
                gu_load(0)
                for i in range(IC):
                    b = i % 2
                    SPL = spl[i]
                    for n2 in range(2):
                        nb = half * 1024 + n2 * 512
                        TG = mmc3(0, [(gsh[b][:, kc * 128:(kc + 1) * 128],
                                       gsll[b][:, kc * 128:(kc + 1) * 128],
                                       xnh[:, kc * T + nb: kc * T + nb + 512],
                                       xnl[:, kc * T + nb: kc * T + nb + 512])
                                      for kc in range(KC)], waits=[SPL, XNt])
                        TU = mmc3(1, [(gsh[b][:, 1024 + kc * 128:1024 + (kc + 1) * 128],
                                       gsll[b][:, 1024 + kc * 128:1024 + (kc + 1) * 128],
                                       xnh[:, kc * T + nb: kc * T + nb + 512],
                                       xnl[:, kc * T + nb: kc * T + nb + 512])
                                      for kc in range(KC)], waits=[SPL])
                        a = act(c1, PS[0], AF.Silu, waits=[TG, prev_d])
                        rd(0, a)
                        d = dve("tensor_mul", c2, c1, PS[1], waits=[a, TU])
                        rd(1, d)
                        hwsl = slice(i * 1024 + n2 * 512, i * 1024 + (n2 + 1) * 512)
                        dve("tensor_mul", c2b, c2, wbc[:, n2 * 512:(n2 + 1) * 512],
                            waits=[WBC, hw_rd if (i == 0 and n2 == 0) else None])
                        dve("tensor_copy", hwh[:, hwsl], c2b)
                        prev_d = dve("tensor_sub", hwl[:, hwsl], c2b, hwh[:, hwsl])
                        if n2 == 0 and i + 1 < IC:
                            gu_load(i + 1)
                    guse[b] = now("pe")
                HW = now("dve")
                wbc_rd = HW
                dsp = {}
                def dn_load(m):
                    b = m % 2
                    DL = dma("sync", dslf,
                             wdn[(e * 8 + m) * 128:(e * 8 + m + 1) * 128, :],
                             waits=[dsp.get(m - 1)])
                    dve("tensor_copy", dsh[b], dslf, waits=[DL, duse[b]])
                    dsp[m] = dve("tensor_sub", dsll[b], dslf, dsh[b])
                dn_load(0)
                for m in range(KC):
                    b = m % 2
                    DSP = dsp[m]
                    for n2 in range(2):
                        nb = half * 1024 + n2 * 512
                        bank = 2 + n2
                        t = mmc3(bank, [(dsh[b][:, i * 128:(i + 1) * 128],
                                         dsll[b][:, i * 128:(i + 1) * 128],
                                         hwh[:, i * 1024 + n2 * 512: i * 1024 + (n2 + 1) * 512],
                                         hwl[:, i * 1024 + n2 * 512: i * 1024 + (n2 + 1) * 512])
                                        for i in range(IC)], waits=[DSP, HW])
                        duse[b] = t
                        if m == KC - 1 and n2 == 1:
                            hw_rd = t
                        cb = c3[n2]
                        cpy = dve("tensor_copy", cb, PS[bank], waits=[t, ctok[n2]])
                        rd(bank, cpy)
                        ctok[n2] = dma("gpsimd",
                                       dst_x[m * 128:(m + 1) * 128, nb:nb + 512],
                                       cb, waits=[cpy, seed_tok], accum=True)
                        if n2 == 0 and m + 1 < KC:
                            dn_load(m + 1)
        return now("gpd")

    def dump(name, ap, waits=()):
        d = nc.dram_tensor("dbg_" + name, list(ap.shape), ap.dtype,
                           kind="ExternalOutput").ap()
        dma("gpsimd", d, ap, waits=waits)
        S.barrier()

    PRO = prologue()

    ar = Arena()
    dbg1 = [] if stage <= 1 else None
    XN1, _ = layer_norm(x_in, ar, spiking=True, dbg=dbg1)
    if stage <= 1:
        for nm, ap, w in (dbg1 or []):
            dump(nm, ap, waits=w)
        dump("xnh", xnh, waits=[XN1, now("dve")])
        dump("xnl", xnl, waits=[XN1, now("dve")])
        dump("spike", CO["spike"], waits=[now("dve")])
    if stage > 1:
        ar = Arena()
        dbg2 = [] if stage <= 2 else None
        Y1 = attn_a(ar, XN1, dbg=dbg2)
        if stage <= 2:
            for nm, ap, w in (dbg2 or []):
                dump(nm, ap, waits=w)
            dump("y", y_hbm, waits=[Y1])
        if stage > 2:
            ar = Arena()
            X1 = out_proj(ar, wouta, CO["bout_a"], x_in, x_hbm, Y1)
            if stage <= 3:
                dump("x1", x_hbm, waits=[X1])
            if stage > 3:
                ar = Arena()
                dbg4 = [] if stage <= 4 else None
                XN2, RT2 = layer_norm(x_hbm, ar, routing=True, dbg=dbg4)
                if stage <= 4:
                    for nm, ap, w in dbg4:
                        dump(nm, ap, waits=w)
                    dump("xnh", xnh, waits=[XN2, now("dve")])
                    dump("xnl", xnl, waits=[XN2, now("dve")])
                    dump("wfm", CO["wfm"], waits=[now("dve")])
                    dump("swrow", CO["swrow"], waits=[now("act")])
                if stage > 4:
                    X2 = ffn(Arena(), XN2, RT2, x_hbm, None)
                    if stage <= 5:
                        dump("x2", x_hbm, waits=[X2])
                    if stage > 5:
                        ar = Arena()
                        XN3, _ = layer_norm(x_hbm, ar)
                        ar = Arena()
                        dbg6 = [] if stage <= 6 else None
                        Y3 = attn_b(ar, XN3, dbg=dbg6)
                        if stage <= 6:
                            for nm, ap, w in dbg6:
                                dump(nm, ap, waits=w)
                            dump("y", y_hbm, waits=[Y3])
                        if stage > 6:
                            ar = Arena()
                            X3 = out_proj(ar, woutb, CO["bout_b"], x_hbm,
                                          x_hbm, Y3)
                            if stage <= 7:
                                dump("x3", x_hbm, waits=[X3])
                            if stage > 7:
                                ar = Arena()
                                XN4, RT4 = layer_norm(x_hbm, ar, routing=True)
                                seed = dma("gpsimd", out, x_hbm, waits=[X3])
                                X4 = ffn(Arena(), XN4, RT4, out, seed)

    S.op("gpsimd", lambda e: None, waits=[now("gpd")])

    import contextlib
    sems = {}
    with contextlib.ExitStack() as st:
        for sname in S.cnt:
            sems[sname] = st.enter_context(nc.semaphore("s_" + sname))
        with nc.Block() as block:
            @block.sync
            def _s(sync):
                S.emit("sync", sync, sems)

            @block.tensor
            def _t(tensor):
                S.emit("tensor", tensor, sems)

            @block.vector
            def _v(vector):
                S.emit("vector", vector, sems)

            @block.scalar
            def _c(scalar):
                S.emit("scalar", scalar, sems)

            @block.gpsimd
            def _g(gpsimd):
                S.emit("gpsimd", gpsimd, sems)
    return nc


def _pack_weights(inp):
    g = {}
    Wqkv = np.asarray(inp["L0_attn_a_W_qkv"], np.float32)
    wa = np.empty((16, 128, 1536), np.float32)
    for h in range(16):
        cols = np.concatenate([np.arange(h * 64, h * 64 + 64),
                               1024 + np.arange(h * 64, h * 64 + 64),
                               2048 + np.arange(h * 64, h * 64 + 64)])
        wa[h] = Wqkv[:, cols].reshape(8, 128, 192).transpose(1, 0, 2).reshape(128, 1536)
    g["wa"] = wa.reshape(16 * 128, 1536)

    wbh = np.empty((16, 128, 1536), np.float32)
    WqT = np.ascontiguousarray(np.asarray(inp["L0_attn_b_Wq"], np.float32).T)
    WkT = np.ascontiguousarray(np.asarray(inp["L0_attn_b_Wk"], np.float32).T)
    WvT = np.ascontiguousarray(np.asarray(inp["L0_attn_b_Wv"], np.float32).T)
    for h in range(16):
        Wh = np.concatenate([WqT[:, h * 64:(h + 1) * 64], WkT[:, h * 64:(h + 1) * 64],
                             WvT[:, h * 64:(h + 1) * 64]], axis=1)
        wbh[h] = Wh.reshape(8, 128, 192).transpose(1, 0, 2).reshape(128, 1536)
    g["wb"] = wbh.reshape(16 * 128, 1536)

    g["wouta"] = np.asarray(inp["L0_attn_a_W_out"], np.float32).reshape(8, 128, 1024) \
        .transpose(1, 0, 2).reshape(128, 8192).copy()
    WoT = np.ascontiguousarray(np.asarray(inp["L0_attn_b_Wo"], np.float32).T)
    g["woutb"] = WoT.reshape(8, 128, 1024).transpose(1, 0, 2).reshape(128, 8192).copy()

    def bias48(bq, bk, bv):
        b = np.empty((64, 48), np.float32)
        for h in range(16):
            b[:, h] = bq[h * 64:(h + 1) * 64]
            b[:, 16 + h] = bk[h * 64:(h + 1) * 64]
            b[:, 32 + h] = bv[h * 64:(h + 1) * 64]
        return b
    bq = np.asarray(inp["L0_attn_a_b_qkv"], np.float32)
    g["bqkv_a"] = bias48(bq[0:1024], bq[1024:2048], bq[2048:3072])
    g["bqkv_b"] = bias48(np.asarray(inp["L0_attn_b_bq"], np.float32),
                         np.asarray(inp["L0_attn_b_bk"], np.float32),
                         np.asarray(inp["L0_attn_b_bv"], np.float32))
    g["bout_a"] = np.asarray(inp["L0_attn_a_b_out"], np.float32).reshape(8, 128).T.copy()
    g["bout_b"] = np.asarray(inp["L0_attn_b_bo"], np.float32).reshape(8, 128).T.copy()

    g["wgate"] = np.asarray(inp["L0_ffn_gate_w"], np.float32).T \
        .reshape(8, 128, 8).transpose(1, 0, 2).reshape(128, 64).copy()
    g["wseg"] = np.asarray(inp["L0_ffn_shared_expert_gate_w"], np.float32).T \
        .reshape(8, 128, 1).transpose(1, 0, 2).reshape(128, 8).copy()
    g["wsur"] = np.asarray(inp["L0_attn_a_w_sur"], np.float32) \
        .reshape(8, 128, 1).transpose(1, 0, 2).reshape(128, 8).copy()

    gu = np.asarray(inp["L0_ffn_experts_gate_up"], np.float32)
    dn = np.asarray(inp["L0_ffn_experts_down"], np.float32)
    sg = np.asarray(inp["L0_ffn_shared_gate_w"], np.float32)
    su = np.asarray(inp["L0_ffn_shared_up_w"], np.float32)
    sd = np.asarray(inp["L0_ffn_shared_down_w"], np.float32)
    wguh = np.empty((NE, IC, 128, 2048), np.float32)
    wdnh = np.empty((NE, 8, 128, 1408), np.float32)
    sgT, suT, sdT = sg.T, su.T, sd.T
    for e in range(NE):
        if e < 8:
            WT = gu[e].T
            gcols, ucols = WT[:, 0:1408], WT[:, 1408:2816]
            DT = dn[e].T
        else:
            off = (e - 8) * 1408
            gcols, ucols = sgT[:, off:off + 1408], suT[:, off:off + 1408]
            DT = sdT[off:off + 1408, :]
        for i in range(IC):
            gstrip = gcols[:, i * 128:(i + 1) * 128].reshape(8, 128, 128)
            ustrip = ucols[:, i * 128:(i + 1) * 128].reshape(8, 128, 128)
            wguh[e, i] = np.stack([gstrip, ustrip]).transpose(2, 0, 1, 3) \
                .reshape(128, 2048)
        for m in range(8):
            wdnh[e, m] = DT[:, m * 128:(m + 1) * 128].reshape(IC, 128, 128) \
                .transpose(1, 0, 2).reshape(128, 1408)
    g["wgu"] = wguh.reshape(NE * IC * 128, 2048)
    g["wdn"] = wdnh.reshape(NE * 8 * 128, 1408)

    g["scal"] = np.array([[np.float32(inp["L0_attn_a_b_sur"]),
                           np.float32(inp["L0_attn_a_threshold"]),
                           np.asarray(inp["L0_attn_b_attn_scale"],
                                      np.float32).reshape(-1)[0], 0.0]], np.float32)
    g["ident"] = np.eye(128, dtype=np.float32)
    g["onesr"] = np.ones((1, 128), np.float32)
    g["onesc"] = np.ones((128, 1), np.float32)
    wm = np.zeros((64, 4), np.float32)
    wm[:, 0] = 1.0; wm[0, 0] = -1.0
    wm[:, 3] = 1.0; wm[0, 3] = -1.0
    g["wmink"] = wm
    sT = np.zeros((2, 128), np.float32); sT[0, 0] = 1.0; sT[1, 64] = 1.0
    g["selT"] = sT
    sS = np.zeros((2, 128), np.float32); sS[0, 1:64] = -1.0; sS[1, 65:128] = 1.0
    g["selS"] = sS
    s8 = np.zeros((8, 1024), np.float32)
    for e in range(8):
        s8[e, e * 128:(e + 1) * 128] = 1.0
    g["sel8"] = s8
    i_ = np.arange(128)[:, None]; u_ = np.arange(896)[None, :]
    g["maskA"] = (u_ - i_ >= 384).astype(np.float32)
    mB = np.zeros((128, 128), np.float32)
    ii = np.arange(128)[:, None]; jj = np.arange(128)[None, :]
    vis = ((jj // 64 == ii // 64) & (jj <= ii)) | (jj // 64 > ii // 64)
    mB[~vis] = NEG
    g["maskB"] = mB
    g["epsc"] = np.full((1, 1), 1e-5, np.float32)
    g["negc"] = np.full((128, 1), -1.0, np.float32)
    return g


def kernel(**inputs):
    x = np.asarray(inputs["L0_attn_a_x"], np.float32)
    g = _pack_weights(inputs)
    nc = build_program()
    in_maps = []
    for c in range(8):
        m = dict(g)
        m["x"] = np.ascontiguousarray(x[c].T)
        in_maps.append(m)
    res = run_bass_kernel_spmd(nc, in_maps, core_ids=list(range(8)))
    outs = [np.asarray(res.results[c]["out"]).T for c in range(8)]
    return np.ascontiguousarray(np.stack(outs)).astype(np.float32)
